# version 10
# baseline (speedup 1.0000x reference)
import numpy as np
import concourse.bass as bass
import concourse.mybir as mybir
from concourse.bass_utils import run_bass_kernel_spmd

F32 = mybir.dt.float32
BF16 = mybir.dt.bfloat16
AF = mybir.ActivationFunctionType
ALU = mybir.AluOpType

C = 1024
HD = 256
NH = 4
TCORE = 4096
HALO = 512
RNG = 512
NRANGE = TCORE // RNG
NCH = C // 128
FF = 4096
NFCH = FF // 128
NEG = -1.0e5
EPS_LN = 1.0e-5
EPS_RMS_SCALED = 256.0 * 1.0e-6

ENGINES = ("tensor", "vector", "scalar", "gpsimd", "sync")


class Res:
    __slots__ = ("name", "w", "r")
    def __init__(self, name):
        self.name = name
        self.w = None
        self.r = []


class Builder:
    def __init__(self, nc):
        self.nc = nc
        self.prog = {e: [] for e in ENGINES}
        self.sem = {e: nc.alloc_semaphore(name="s_" + e) for e in ENGINES}
        self.cnt = {e: 0 for e in ENGINES}
        self.ic = {e: 0 for e in ENGINES}
        self.waited = {e: {} for e in ENGINES}
        self.drained = {e: 0 for e in ENGINES}
        self.R = {}

    def res(self, name):
        if name not in self.R:
            self.R[name] = Res(name)
        return self.R[name]

    def _sem(self, key):
        if key not in self.sem:
            self.sem[key] = self.nc.alloc_semaphore(name="s_" + key)
            self.cnt[key] = 0
        return self.sem[key]

    def step(self, eng, fn, reads=(), writes=(), dma=None, n_dma=1):
        deps = []
        for r in reads:
            if r.w is not None:
                deps.append(r.w)
            if r.name.startswith("pb") and eng in ("vector", "scalar"):
                other = "scalar" if eng == "vector" else "vector"
                deps.extend(t for t in r.r if t[0] == other)
        for w in writes:
            deps.extend(w.r)
            if w.w is not None:
                deps.append(w.w)
        self.ic[eng] += 1
        my_idx = self.ic[eng]
        if dma is not None:
            key = "q_" + dma
            self._sem(key)
            self.cnt[key] += 16 * n_dma
            my_cnt = self.cnt[key]
        else:
            key = eng
            self.cnt[eng] += 1
            my_cnt = self.cnt[eng]
        sem, waited = self.sem, self.waited[eng]
        xwaits, need_drain = {}, False
        for (de, dc, di, ddma) in deps:
            if de == eng and not ddma:
                if di > self.drained[eng]:
                    need_drain = True
            else:
                if dc > xwaits.get(de, 0):
                    xwaits[de] = dc
        if need_drain:
            self.drained[eng] = my_idx - 1
        do_drain = need_drain and eng in ("vector", "scalar", "gpsimd")

        def emit(E):
            for de, dc in sorted(xwaits.items()):
                if dc > waited.get(de, 0):
                    E.wait_ge(sem[de], dc)
                    waited[de] = dc
            if do_drain:
                E.drain()
            out = fn(E)
            if dma is not None:
                outs = out if isinstance(out, (list, tuple)) else [out]
                assert len(outs) == n_dma, (eng, len(outs), n_dma)
                for ins in outs:
                    ins.then_inc(sem[key], 16)
            else:
                out.then_inc(sem[key], 1)
        self.prog[eng].append(emit)
        tok = (key, my_cnt, my_idx, dma is not None)
        for w in writes:
            w.w = tok
            w.r = []
        for r in reads:
            r.r.append(tok)
        return tok

    def barrier(self):
        snap = dict(self.cnt)
        waited_all = self.waited
        sem = self.sem
        keys = sorted(snap.keys())
        for e in ENGINES:
            dr = e in ("vector", "scalar", "gpsimd")
            def f(E, e=e, dr=dr, keys=tuple(keys)):
                w = waited_all[e]
                for o in keys:
                    if o == e:
                        continue
                    v = snap[o]
                    if v > w.get(o, 0):
                        E.wait_ge(sem[o], v)
                        w[o] = v
                if dr:
                    E.drain()
            self.prog[e].append(f)
            self.drained[e] = self.ic[e]
        for r in self.R.values():
            r.w = None
            r.r = []

    def finalize(self):
        fincnt = dict(self.cnt)
        sem, waited = self.sem, self.waited["gpsimd"]
        def fin(E):
            for o in sorted(fincnt.keys()):
                if o == "gpsimd":
                    continue
                if fincnt[o] > waited.get(o, 0):
                    E.wait_ge(sem[o], fincnt[o])
        self.prog["gpsimd"].append(fin)
        with self.nc.Block() as block:
            @block.sync
            def _(E):
                for f in self.prog["sync"]:
                    f(E)
            @block.tensor
            def _(E):
                for f in self.prog["tensor"]:
                    f(E)
            @block.vector
            def _(E):
                for f in self.prog["vector"]:
                    f(E)
            @block.scalar
            def _(E):
                for f in self.prog["scalar"]:
                    f(E)
            @block.gpsimd
            def _(E):
                for f in self.prog["gpsimd"]:
                    f(E)


ACOLS = 53208
O_CONST = 0
O_CROW = 388
O_GAMS = 772
O_B1 = 776
O_B2 = 808
O_ROWS = 816
O_W = 884
O_TAB = 33648 + 12288
A0 = 21364
A_ = {}
for _n, _c in [("xtok", 2048), ("xfm", 4096), ("xn", 4096), ("ropes", 1024),
               ("kring", 3072), ("vring", 3072), ("qfm", 4096), ("q2", 1024),
               ("bmu", 512), ("binv", 512), ("binvk", 512), ("pp", 640), ("pt", 640),
               ("orng", 4096)]:
    A_[_n] = A0 + sum(v for k, v in A_.items()) if False else 0
def _layout(base, items):
    d, cur = {}, base
    for n, c in items:
        d[n] = (cur, c)
        cur += c
    return d, cur
A_, A_END = _layout(A0, [("xtok", 2048), ("xfm", 4096), ("xn", 4096), ("ropes", 1024),
                         ("kring", 3072), ("vring", 3072), ("qfm", 4096), ("q2", 1024),
                         ("bmu", 512), ("binv", 512), ("binvk", 512), ("pp", 640),
                         ("phl", 640), ("pt", 1280), ("orng", 4096)])
assert A_END <= ACOLS, A_END
B0_ = 33652
B_, B_END = _layout(B0_, [("x1slab", 4096), ("xn2", 4096), ("xn2p", 4096),
                          ("brows", 1024), ("bbc", 1024), ("hstg", 1024),
                          ("hhi", 512), ("hlo", 512)])
assert B_END <= ACOLS, B_END
C_, C_END = _layout(B0_, [("hbufs", 2048), ("cx1", 8192), ("x2", 4096), ("ostg", 2048)])
assert C_END <= ACOLS, C_END
O_TAB = A_END
assert O_TAB + 2 * RNG <= ACOLS, O_TAB


def build_program():
    import os
    dbg = os.environ.get("BASSDBG", "")
    dbg_nrange = NRANGE
    dbg_nob = "nob" in dbg
    dbg_noc = "noc" in dbg
    dbg_noq = "noq" in dbg
    dbg_noattn = "noattn" in dbg
    dbg_nowo = "nowo" in dbg
    dbg_noqms = "noqms" in dbg
    dbg_noqrope = "noqrope" in dbg
    dbg_noinvq = "noinvq" in dbg
    dbg_qms1 = "qms1" in dbg
    for tokn in dbg.split(","):
        if tokn.startswith("r="):
            dbg_nrange = int(tokn[2:])
    nc = bass.Bass("TRN2", target_bir_lowering=False)

    d_x = nc.dram_tensor("x_sh", [HALO + TCORE, C], F32, kind="ExternalInput").ap()
    d_wq = nc.dram_tensor("wqT", [C, C], F32, kind="ExternalInput").ap()
    d_wk = nc.dram_tensor("wkT", [C, HD], F32, kind="ExternalInput").ap()
    d_wv = nc.dram_tensor("wvT", [C, HD], F32, kind="ExternalInput").ap()
    d_wo = nc.dram_tensor("woT", [C, C], F32, kind="ExternalInput").ap()
    d_w1 = nc.dram_tensor("w1", [C, FF], F32, kind="ExternalInput").ap()
    d_w2 = nc.dram_tensor("w2", [FF, C], F32, kind="ExternalInput").ap()
    d_b1 = nc.dram_tensor("b1c", [128, NFCH], F32, kind="ExternalInput").ap()
    d_b2 = nc.dram_tensor("b2c", [128, NCH], F32, kind="ExternalInput").ap()
    d_g = nc.dram_tensor("gams", [128, 4], F32, kind="ExternalInput").ap()
    d_cos = nc.dram_tensor("cosT", [128, HALO + TCORE], F32, kind="ExternalInput").ap()
    d_sin = nc.dram_tensor("sinT", [128, HALO + TCORE], F32, kind="ExternalInput").ap()
    d_cst = nc.dram_tensor("consts", [128, 386], F32, kind="ExternalInput").ap()
    d_cr = nc.dram_tensor("crow", [1, 384], F32, kind="ExternalInput").ap()

    dbg_stat = "dbgstat" in dbg
    if dbg_stat:
        d_dbg = nc.dram_tensor("dbgrow", [1, 1024], F32, kind="ExternalOutput").ap()
        d_dbg2 = nc.dram_tensor("dbgxn", [128, 2048], F32, kind="ExternalOutput").ap()
        d_dbg3 = nc.dram_tensor("dbgkr", [128, 1024], F32, kind="ExternalOutput").ap()
        d_dbg4 = nc.dram_tensor("dbgo", [128, 4096], F32, kind="ExternalOutput").ap()
        d_dbg5 = nc.dram_tensor("dbgv", [128, 1024], F32, kind="ExternalOutput").ap()
    d_x1 = nc.dram_tensor("x1s", [C, TCORE], F32, kind="Internal").ap()
    d_hhi = nc.dram_tensor("hshi", [FF, TCORE], BF16, kind="Internal").ap()
    d_hlo = nc.dram_tensor("hslo", [FF, TCORE], BF16, kind="Internal").ap()
    d_y = nc.dram_tensor("y", [TCORE, C], F32, kind="ExternalOutput").ap()

    arena = nc.alloc_sbuf_tensor("arena", [128, ACOLS], F32)
    psum = nc.alloc_psum_tensor("psum", [128, 4096], F32)

    def bank(b, p0=0, p1=128, c0=0, c1=512):
        return psum[p0:p1, 512 * b + c0: 512 * b + c1]

    def aA(name, p0=0, p1=128, c0=0, c1=None):
        at, w = A_[name]
        return arena[p0:p1, at + c0: at + (w if c1 is None else c1)]

    def aB(name, p0=0, p1=128, c0=0, c1=None):
        at, w = B_[name]
        return arena[p0:p1, at + c0: at + (w if c1 is None else c1)]

    def aC(name, p0=0, p1=128, c0=0, c1=None):
        at, w = C_[name]
        return arena[p0:p1, at + c0: at + (w if c1 is None else c1)]

    ident = arena[:, O_CONST + 0: O_CONST + 128]
    ones_col = arena[:, O_CONST + 128: O_CONST + 129]
    epsln = arena[:, O_CONST + 129: O_CONST + 130]
    identb = arena[:, O_CONST + 130: O_CONST + 194].bitcast(BF16)
    strictLb = arena[:, O_CONST + 194: O_CONST + 258].bitcast(BF16)
    strictUb = arena[:, O_CONST + 258: O_CONST + 322].bitcast(BF16)
    negIb = arena[:, O_CONST + 322: O_CONST + 386].bitcast(BF16)
    ones_row = arena[0:1, O_CROW + 0: O_CROW + 128]
    kbias = arena[0:1, O_CROW + 128: O_CROW + 256]
    ones_row16 = arena[0:1, O_CROW + 256: O_CROW + 384]
    def gam(i): return arena[:, O_GAMS + i: O_GAMS + i + 1]
    def b1ap(fc): return arena[:, O_B1 + fc: O_B1 + fc + 1]
    def b2ap(dc): return arena[:, O_B2 + dc: O_B2 + dc + 1]
    def invq_ap(c0, c1): return arena[:, O_ROWS + 16 + c0: O_ROWS + 16 + c1]
    lcol = arena[:, O_ROWS + 32: O_ROWS + 33]
    lrec = arena[:, O_ROWS + 33: O_ROWS + 34]
    def wq_t(ch, d0, d1): return arena[:, O_W + ch * 1024 + d0: O_W + ch * 1024 + d1]
    def wk_t(ch, d0, d1): return arena[:, O_W + 8192 + ch * HD + d0: O_W + 8192 + ch * HD + d1]
    def wv_t(ch): return arena[:, O_W + 10240 + ch * HD: O_W + 10240 + (ch + 1) * HD]
    def wo_t(ch, d0, d1): return arena[:, O_W + 12288 + ch * 1024 + d0: O_W + 12288 + ch * 1024 + d1]
    def w1_t(ch, f0, f1): return arena[:, O_W + ch * FF + f0: O_W + ch * FF + f1]
    def w2_t(fch, d0, d1): return arena[:, O_W + fch * C + d0: O_W + fch * C + d1]
    def bfv(f32c0, nbf):
        return arena[:, f32c0: f32c0 + nbf // 2].bitcast(BF16)
    def w1b(hl, ch, f0, f1):
        base = O_W * 2 + hl * (NCH * FF) + ch * FF
        return bfv(0, 0) if False else arena[:, (base + f0) // 2: (base + f1) // 2].bitcast(BF16)
    def w2b(hl, fch, d0, d1):
        base = O_W * 2 + hl * (NFCH * C) + fch * C
        return arena[:, (base + d0) // 2: (base + d1) // 2].bitcast(BF16)
    def aBbf(name, c0bf, c1bf):
        at, w = B_[name]
        return arena[:, at + c0bf // 2: at + c1bf // 2].bitcast(BF16)
    def aCbf(name, c0bf, c1bf):
        at, w = C_[name]
        return arena[:, at + c0bf // 2: at + c1bf // 2].bitcast(BF16)
    def aAbf(name, c0bf, c1bf):
        at, w = A_[name]
        return arena[:, at + c0bf // 2: at + c1bf // 2].bitcast(BF16)
    def wab(base_bf, c0, c1):
        b = O_W * 2 + base_bf
        return arena[:, (b + c0) // 2: (b + c1) // 2].bitcast(BF16)
    def wqb(hl, ch, d0, d1): return wab(8192 * hl + ch * 1024, d0, d1)
    def wkb(hl, ch, d0, d1): return wab(16384 + 2048 * hl + ch * 256, d0, d1)
    def wvb(hl, ch): return wab(20480 + 2048 * hl + ch * 256, 0, 256)
    def wob(hl, ch, d0, d1): return wab(24576 + 8192 * hl + ch * 1024, d0, d1)
    def psbf(b, c0bf, c1bf):
        return psum[:, 512 * b + c0bf // 2: 512 * b + c1bf // 2].bitcast(BF16)
    def cosl(c0, c1): return arena[:, O_TAB + c0: O_TAB + c1]
    def sinl(c0, c1): return arena[:, O_TAB + RNG + c0: O_TAB + RNG + c1]

    bld = Builder(nc)
    step, res, barrier = bld.step, bld.res, bld.barrier

    step("sync", lambda E: [
        E.dma_start(out=arena[:, O_CONST:O_CONST + 386], in_=d_cst),
        E.dma_start(out=arena[0:1, O_CROW:O_CROW + 384], in_=d_cr),
        E.dma_start(out=arena[:, O_GAMS:O_GAMS + 4], in_=d_g),
        E.dma_start(out=arena[:, O_B1:O_B1 + NFCH], in_=d_b1),
        E.dma_start(out=arena[:, O_B2:O_B2 + NCH], in_=d_b2),
    ], writes=[res("consts")], dma="consts", n_dma=5)

    def wsplit_steps(tiles):
        batch, bcols = [], 0
        def flush(batch):
            n = len(batch)
            def ld(E, batch=tuple(batch)):
                out, c = [], 0
                for (dsrc, ncols, _hi, _lo) in batch:
                    out.append(E.dma_start(out=aA("xfm", c0=c, c1=c + ncols), in_=dsrc))
                    c += ncols
                return out
            step("sync", ld, reads=[res("xfm")], writes=[res("xfm")], dma="wload", n_dma=n)
            def sp(E, batch=tuple(batch)):
                c = 0
                for (dsrc, ncols, hi, lo) in batch:
                    E.tensor_copy(hi, aA("xfm", c0=c, c1=c + ncols))
                    c += ncols
                E.drain()
                c, last = 0, None
                for (dsrc, ncols, hi, lo) in batch:
                    last = E.tensor_tensor(lo, aA("xfm", c0=c, c1=c + ncols), hi, ALU.subtract)
                    c += ncols
                return last
            step("vector", sp, reads=[res("xfm"), res("w")], writes=[res("w")])
        for t in tiles:
            if bcols + t[1] > 4096:
                flush(batch)
                batch, bcols = [], 0
            batch.append(t)
            bcols += t[1]
        if batch:
            flush(batch)
    wtiles = []
    for ch in range(NCH):
        wtiles.append((d_wq[128 * ch:128 * (ch + 1), :], 1024,
                       wqb(0, ch, 0, 1024), wqb(1, ch, 0, 1024)))
    for ch in range(NCH):
        wtiles.append((d_wk[128 * ch:128 * (ch + 1), :], 256,
                       wkb(0, ch, 0, 256), wkb(1, ch, 0, 256)))
        wtiles.append((d_wv[128 * ch:128 * (ch + 1), :], 256,
                       wvb(0, ch), wvb(1, ch)))
    for ch in range(NCH):
        wtiles.append((d_wo[128 * ch:128 * (ch + 1), :], 1024,
                       wob(0, ch, 0, 1024), wob(1, ch, 0, 1024)))
    wsplit_steps(wtiles)

    for r in range(-1, dbg_nrange):
        slot = (r + 1) % 3
        tb0 = HALO + RNG * r

        step("sync", lambda E, tb0=tb0: [
            E.dma_start(out=cosl(0, RNG), in_=d_cos[:, tb0:tb0 + RNG]),
            E.dma_start(out=sinl(0, RNG), in_=d_sin[:, tb0:tb0 + RNG]),
        ], writes=[res("tabs")], dma="tabs", n_dma=2)

        for j in range(4):
            xb = j % 2
            step("sync", lambda E, tb0=tb0, j=j, xb=xb:
                 E.dma_start(out=aA("xtok", c0=1024 * xb, c1=1024 * (xb + 1)),
                             in_=d_x[tb0 + 128 * j: tb0 + 128 * (j + 1), :]),
                 writes=[res(f"xtok{xb}")], dma=f"xin{xb}")
            def tr(E, xb=xb):
                last = None
                for ch in range(NCH):
                    bb = (0 if ch < 4 else 1)
                    last = E.transpose(bank(bb, c0=128 * (ch % 4), c1=128 * (ch % 4) + 128),
                                       aA("xtok", c0=1024 * xb + 128 * ch,
                                          c1=1024 * xb + 128 * (ch + 1)),
                                       ident)
                return last
            step("tensor", tr, reads=[res(f"xtok{xb}"), res("consts"),
                                      res("pb0"), res("pb1")],
                 writes=[res("pb0"), res("pb1")])
            def tstat(E, j=j, xb=xb):
                sx = arena[:, O_ROWS + 0 + j: O_ROWS + 1 + j]
                sq = arena[:, O_ROWS + 4 + j: O_ROWS + 5 + j]
                E.reduce_sum(sx, aA("xtok", c0=1024 * xb, c1=1024 * (xb + 1)),
                             axis=mybir.AxisListType.X)
                return E.scalar_tensor_tensor(aA("ropes", c0=0, c1=1024),
                                              aA("xtok", c0=1024 * xb, c1=1024 * (xb + 1)),
                                              1.0,
                                              aA("xtok", c0=1024 * xb, c1=1024 * (xb + 1)),
                                              ALU.mult, ALU.mult, accum_out=sq)
            step("vector", tstat, reads=[res(f"xtok{xb}"), res("scr0"), res("scr1"),
                                         res("statc")],
                 writes=[res("scr0"), res("scr1"), res("statc")])
            def ev(E, j=j):
                last = None
                for ch in range(NCH):
                    bb = (0 if ch < 4 else 1)
                    last = E.copy(aA("xfm", c0=RNG * ch + 128 * j,
                                     c1=RNG * ch + 128 * (j + 1)),
                                  bank(bb, c0=128 * (ch % 4), c1=128 * (ch % 4) + 128))
                return last
            step("scalar", ev, reads=[res("pb0"), res("pb1"), res("xfm")], writes=[res("xfm")])

        def statchain(E):
            sx = arena[:, O_ROWS + 0: O_ROWS + 4]
            sq = arena[:, O_ROWS + 4: O_ROWS + 8]
            mu = arena[:, O_ROWS + 8: O_ROWS + 12]
            iv = arena[:, O_ROWS + 12: O_ROWS + 16]
            E.tensor_scalar_mul(mu, sx, 1.0 / C)
            E.tensor_scalar_mul(iv, sq, 1.0 / C)
            E.drain()
            E.tensor_tensor(iv, iv, arena[:, O_ROWS + 8: O_ROWS + 12], ALU.subtract) \
                if False else None
            tmp = arena[:, O_ROWS + 34: O_ROWS + 38]
            E.tensor_tensor(tmp, mu, mu, ALU.mult)
            E.drain()
            E.tensor_tensor(iv, iv, tmp, ALU.subtract)
            E.drain()
            E.tensor_scalar_add(iv, iv, EPS_LN)
            E.drain()
            return E.reciprocal(iv, iv)
        step("vector", statchain, reads=[res("statc")], writes=[res("statc")])
        step("scalar", lambda E: E.activation(arena[:, O_ROWS + 12: O_ROWS + 16],
                                              arena[:, O_ROWS + 12: O_ROWS + 16], AF.Sqrt),
             reads=[res("statc")], writes=[res("statc")])
        def stat_tr(E):
            last = None
            for j in range(4):
                E.transpose(bank(6, p1=1, c0=128 * j, c1=128 * (j + 1)),
                            arena[:, O_ROWS + 8 + j: O_ROWS + 9 + j], ident)
                last = E.transpose(bank(7, p1=1, c0=128 * j, c1=128 * (j + 1)),
                                   arena[:, O_ROWS + 12 + j: O_ROWS + 13 + j], ident)
            return last
        step("tensor", stat_tr, reads=[res("statc"), res("consts"), res("pb6"), res("pb7")],
             writes=[res("pb6"), res("pb7")])
        def statrow_ev(E):
            E.tensor_copy(aA("ropes", p0=0, p1=1, c0=0, c1=RNG), bank(6, p1=1, c0=0, c1=RNG))
            return E.tensor_copy(aA("ropes", p0=0, p1=1, c0=RNG, c1=2 * RNG),
                                 bank(7, p1=1, c0=0, c1=RNG))
        step("vector", statrow_ev, reads=[res("pb6"), res("pb7"), res("scr0"), res("scr1")],
             writes=[res("scr0"), res("scr1")])
        if dbg_stat and r == 0:
            step("gpsimd", lambda E: E.dma_start(out=d_dbg, in_=aA("ropes", p0=0, p1=1, c0=0, c1=1024)),
                 reads=[res("scr0"), res("scr1")], writes=[res("dbgx")], dma="dbgo")
        def bc_mm(E):
            E.matmul(bank(6, c0=0, c1=RNG), ones_row, aA("ropes", p0=0, p1=1, c0=0, c1=RNG),
                     start=True, stop=True)
            return E.matmul(bank(7, c0=0, c1=RNG), ones_row,
                            aA("ropes", p0=0, p1=1, c0=RNG, c1=2 * RNG),
                            start=True, stop=True)
        step("tensor", bc_mm, reads=[res("scr0"), res("scr1"), res("consts"),
                                     res("pb6"), res("pb7")],
             writes=[res("pb6"), res("pb7")])
        def bc_ev(E):
            E.copy(aA("bmu"), bank(6, c0=0, c1=RNG))
            return E.copy(aA("binv"), bank(7, c0=0, c1=RNG))
        step("scalar", bc_ev, reads=[res("pb6"), res("pb7"), res("bmu"), res("binv")],
             writes=[res("bmu"), res("binv")])
        for ch in range(NCH):
            sp = ch % 2
            def xns(E, ch=ch, sp=sp):
                s_ = aA("ropes", c0=RNG * sp, c1=RNG * (sp + 1))
                E.tensor_tensor(s_, aA("xfm", c0=RNG * ch, c1=RNG * (ch + 1)),
                                aA("bmu"), ALU.subtract)
                E.drain()
                E.tensor_tensor(s_, s_, aA("binv"), ALU.mult)
                E.drain()
                E.tensor_copy(aAbf("xn", RNG * ch, RNG * (ch + 1)), s_)
                E.drain()
                return E.tensor_tensor(aAbf("xn", NCH * RNG + RNG * ch,
                                            NCH * RNG + RNG * (ch + 1)),
                                       s_, aAbf("xn", RNG * ch, RNG * (ch + 1)), ALU.subtract)
            step("vector", xns, reads=[res("xfm"), res("bmu"), res("binv"), res(f"scr{sp}"),
                                       res(f"xnp{ch}")],
                 writes=[res(f"scr{sp}"), res(f"xnp{ch}")])
        if dbg_stat and r == 1:
            def dbgkr(E):
                at = A_["kring"][0]
                return [E.dma_start(out=d_dbg3[:, 0:256], in_=arena[:, at + 256: at + 512]),
                        E.dma_start(out=d_dbg3[:, 256:512], in_=arena[:, at + 1024: at + 1280]),
                        E.dma_start(out=d_dbg3[:, 512:768], in_=arena[:, at + 1536 + 256: at + 1536 + 512]),
                        E.dma_start(out=d_dbg3[:, 768:1024], in_=arena[:, at + 1536 + 1024: at + 1536 + 1280])]
            step("gpsimd", dbgkr, reads=[res("ks1")], writes=[res("dbgx3")], dma="dbgo3", n_dma=4)
            def dbgvr(E):
                at = A_["vring"][0]
                return [E.dma_start(out=d_dbg5[:, 0:512], in_=arena[:, at + 512: at + 1024]),
                        E.dma_start(out=d_dbg5[:, 512:1024], in_=arena[:, at + 1536 + 512: at + 1536 + 1024])]
            step("gpsimd", dbgvr, reads=[res("vs1")], writes=[res("dbgx5")], dma="dbgo5", n_dma=2)
        if dbg_stat and r == 0:
            def dbgxn(E):
                return E.dma_start(out=d_dbg2, in_=aA("xn", c0=0, c1=2048))
            step("gpsimd", dbgxn, reads=[res(f"xnp{c_}") for c_ in range(NCH)],
                 writes=[res("dbgx2")], dma="dbgo2")
        def xnhi(ch):
            return aAbf("xn", RNG * ch, RNG * (ch + 1))
        def xnlo(ch):
            return aAbf("xn", NCH * RNG + RNG * ch, NCH * RNG + RNG * (ch + 1))

        for ch in range(NCH):
            def kvch(E, ch=ch):
                last = None
                for kc in range(2):
                    whi = wkb(0, ch, 128 * kc, 128 * (kc + 1))
                    wlo = wkb(1, ch, 128 * kc, 128 * (kc + 1))
                    plan = [(whi, xnhi(ch)), (whi, xnlo(ch)), (wlo, xnhi(ch))]
                    for pi, (lhs, rhs) in enumerate(plan):
                        last = E.matmul(bank(2 + kc, c0=0, c1=RNG), lhs, rhs,
                                        start=(ch == 0 and pi == 0),
                                        stop=(ch == NCH - 1 and pi == 2))
                for j in range(4):
                    b = 4 + j
                    c0 = 0
                    xh = aAbf("xn", RNG * ch + 128 * j, RNG * ch + 128 * (j + 1))
                    xl = aAbf("xn", NCH * RNG + RNG * ch + 128 * j,
                              NCH * RNG + RNG * ch + 128 * (j + 1))
                    vh, vl = wvb(0, ch), wvb(1, ch)
                    plan = [(xh, vh), (xh, vl), (xl, vh)]
                    for pi, (lhs, rhs) in enumerate(plan):
                        last = E.matmul(bank(b, c0=c0, c1=c0 + HD), lhs, rhs,
                                        start=(ch == 0 and pi == 0),
                                        stop=(ch == NCH - 1 and pi == 2))
                return last
            step("tensor", kvch,
                 reads=[res(f"xnp{ch}"), res("w")] + [res(f"pb{b}") for b in range(2, 8)],
                 writes=([res(f"pb{b}") for b in range(2, 8)] if ch == NCH - 1 else []))
        def ksq(E):
            E.activation(aA("q2", c0=0, c1=RNG), bank(2, c0=0, c1=RNG), AF.Square)
            return E.activation(aA("q2", c0=RNG, c1=2 * RNG), bank(3, c0=0, c1=RNG), AF.Square)
        step("scalar", ksq, reads=[res("pb2"), res("pb3"), res("q2")], writes=[res("q2")])
        def k_ms(E):
            E.matmul(bank(0, p1=1, c0=0, c1=RNG), ones_col, aA("q2", c0=0, c1=RNG),
                     start=True, stop=False)
            return E.matmul(bank(0, p1=1, c0=0, c1=RNG), ones_col, aA("q2", c0=RNG, c1=2 * RNG),
                            start=False, stop=True)
        step("tensor", k_ms, reads=[res("q2"), res("consts"), res("pb0")], writes=[res("pb0")])
        def k_inv1(E):
            kr = aA("q2", p0=0, p1=1, c0=0, c1=RNG)
            E.tensor_scalar_add(kr, bank(0, p1=1, c0=0, c1=RNG), EPS_RMS_SCALED)
            E.drain()
            return E.reciprocal(kr, kr)
        step("vector", k_inv1, reads=[res("pb0"), res("q2")], writes=[res("q2")])
        def k_inv2(E):
            kr = aA("q2", p0=0, p1=1, c0=0, c1=RNG)
            E.activation(kr, kr, AF.Sqrt)
            E.drain()
            return E.activation(kr, kr, AF.Copy, scale=16.0)
        step("scalar", k_inv2, reads=[res("q2")], writes=[res("q2")])
        def k_bc(E):
            return E.matmul(bank(0, c0=0, c1=RNG), ones_row,
                            aA("q2", p0=0, p1=1, c0=0, c1=RNG), start=True, stop=True)
        step("tensor", k_bc, reads=[res("q2"), res("consts"), res("pb0")], writes=[res("pb0")])
        step("vector", lambda E: E.tensor_copy(aA("binvk"), bank(0, c0=0, c1=RNG)),
             reads=[res("pb0"), res("binvk")], writes=[res("binvk")])
        def k_rope(E, slot=slot):
            t1 = aA("ropes", c0=0, c1=RNG)
            t2 = aA("ropes", c0=RNG, c1=2 * RNG)
            kef = aA("q2", c0=0, c1=RNG)
            kof = aA("q2", c0=RNG, c1=2 * RNG)
            E.scalar_tensor_tensor(t1, bank(2, c0=0, c1=RNG), gam(2), cosl(0, RNG), ALU.mult, ALU.mult)
            E.scalar_tensor_tensor(t2, bank(3, c0=0, c1=RNG), gam(3), sinl(0, RNG), ALU.mult, ALU.mult)
            E.drain()
            E.tensor_tensor(kef, t1, t2, ALU.subtract)
            E.drain()
            E.scalar_tensor_tensor(t1, bank(2, c0=0, c1=RNG), gam(2), sinl(0, RNG), ALU.mult, ALU.mult)
            E.scalar_tensor_tensor(t2, bank(3, c0=0, c1=RNG), gam(3), cosl(0, RNG), ALU.mult, ALU.mult)
            E.drain()
            E.tensor_tensor(kof, t1, t2, ALU.add)
            E.drain()
            E.tensor_tensor(kef, kef, aA("binvk"), ALU.mult)
            E.tensor_tensor(kof, kof, aA("binvk"), ALU.mult)
            E.drain()
            E.tensor_copy(aAbf("kring", (0 * 3 + slot) * RNG, (0 * 3 + slot + 1) * RNG), kef)
            return E.tensor_copy(aAbf("kring", (1 * 3 + slot) * RNG, (1 * 3 + slot + 1) * RNG), kof)
        step("vector", k_rope,
             reads=[res("pb2"), res("pb3"), res("consts"), res("tabs"), res("binvk"),
                    res("scr0"), res("scr1"), res("q2"), res(f"ks{slot}")],
             writes=[res(f"ks{slot}"), res("scr0"), res("scr1"), res("q2")])
        def k_lo(E, slot=slot):
            E.tensor_tensor(aAbf("kring", 3072 + (0 * 3 + slot) * RNG,
                                 3072 + (0 * 3 + slot + 1) * RNG),
                            aA("q2", c0=0, c1=RNG),
                            aAbf("kring", (0 * 3 + slot) * RNG, (0 * 3 + slot + 1) * RNG),
                            ALU.subtract)
            return E.tensor_tensor(aAbf("kring", 3072 + (1 * 3 + slot) * RNG,
                                        3072 + (1 * 3 + slot + 1) * RNG),
                                   aA("q2", c0=RNG, c1=2 * RNG),
                                   aAbf("kring", (1 * 3 + slot) * RNG, (1 * 3 + slot + 1) * RNG),
                                   ALU.subtract)
        step("vector", k_lo, reads=[res("q2"), res(f"ks{slot}")], writes=[res(f"ks{slot}")])

        def v_hi(E, slot=slot):
            last = None
            for j in range(4):
                last = E.tensor_copy(aAbf("vring", (4 * slot + j) * HD, (4 * slot + j + 1) * HD),
                                     bank(4 + j, c0=0, c1=HD))
            return last
        step("vector", v_hi, reads=[res("pb4"), res("pb5"), res("pb6"), res("pb7"),
                                    res(f"vs{slot}")],
             writes=[res(f"vs{slot}")])
        def v_lo(E, slot=slot):
            last = None
            for j in range(4):
                last = E.tensor_tensor(
                    aAbf("vring", 3072 + (4 * slot + j) * HD, 3072 + (4 * slot + j + 1) * HD),
                    bank(4 + j, c0=0, c1=HD),
                    aAbf("vring", (4 * slot + j) * HD, (4 * slot + j + 1) * HD),
                    ALU.subtract)
            return last
        step("vector", v_lo, reads=[res("pb4"), res("pb5"), res("pb6"), res("pb7"),
                                    res(f"vs{slot}")],
             writes=[res(f"vs{slot}")])

        if r < 0 or dbg_noq:
            continue

        for hp in range(NH):
            qb = 4 if hp % 2 == 0 else 2
            def qmm(E, hp=hp, qb=qb):
                last, first = None, [True, True]
                for kc in range(2):
                    pc = 2 * hp + kc
                    for ch in range(NCH):
                        whi = wqb(0, ch, 128 * pc, 128 * (pc + 1))
                        wlo = wqb(1, ch, 128 * pc, 128 * (pc + 1))
                        for lhs, rhs in ((whi, xnhi(ch)), (whi, xnlo(ch)), (wlo, xnhi(ch))):
                            last = E.matmul(bank(qb + kc, c0=0, c1=RNG), lhs, rhs,
                                            start=first[kc],
                                            stop=(ch == NCH - 1 and lhs is wlo))
                            first[kc] = False
                return last
            step("tensor", qmm, reads=[res("w"), res(f"pb{qb}"), res(f"pb{qb+1}")] +
                                [res(f"xnp{c_}") for c_ in range(NCH)],
                 writes=[res(f"pb{qb}"), res(f"pb{qb+1}")])
            def qsq(E, qb=qb):
                E.activation(aA("q2", c0=0, c1=RNG), bank(qb, c0=0, c1=RNG), AF.Square)
                return E.activation(aA("q2", c0=RNG, c1=2 * RNG), bank(qb + 1, c0=0, c1=RNG), AF.Square)
            step("scalar", qsq, reads=[res(f"pb{qb}"), res(f"pb{qb+1}"), res("q2")], writes=[res("q2")])
            def q_ms(E, hp=hp):
                last = None
                for j in range(4):
                    col = 4 * hp + j
                    E.matmul(bank(6, c0=col, c1=col + 1),
                             aA("q2", c0=128 * j, c1=128 * (j + 1)),
                             ones_col, start=True, stop=False)
                    last = E.matmul(bank(6, c0=col, c1=col + 1),
                                    aA("q2", c0=RNG + 128 * j, c1=RNG + 128 * (j + 1)),
                                    ones_col, start=False, stop=True)
                return last
            step("tensor", q_ms, reads=[res("q2"), res("consts"), res("pb6")],
                 writes=[res("pb6")])
            def q_rope(E, hp=hp, qb=qb):
                t1 = aA("ropes", c0=0, c1=RNG)
                t2 = aA("ropes", c0=RNG, c1=2 * RNG)
                qef = aA("q2", c0=0, c1=RNG)
                qof = aA("q2", c0=RNG, c1=2 * RNG)
                E.scalar_tensor_tensor(t1, bank(qb, c0=0, c1=RNG), gam(0), cosl(0, RNG), ALU.mult, ALU.mult)
                E.scalar_tensor_tensor(t2, bank(qb + 1, c0=0, c1=RNG), gam(1), sinl(0, RNG), ALU.mult, ALU.mult)
                E.drain()
                E.tensor_tensor(qef, t1, t2, ALU.subtract)
                E.drain()
                E.scalar_tensor_tensor(t1, bank(qb, c0=0, c1=RNG), gam(0), sinl(0, RNG), ALU.mult, ALU.mult)
                E.scalar_tensor_tensor(t2, bank(qb + 1, c0=0, c1=RNG), gam(1), cosl(0, RNG), ALU.mult, ALU.mult)
                E.drain()
                E.tensor_tensor(qof, t1, t2, ALU.add)
                E.drain()
                E.tensor_copy(aAbf("qfm", RNG * (2 * hp), RNG * (2 * hp + 1)), qef)
                return E.tensor_copy(aAbf("qfm", RNG * (2 * hp + 1), RNG * (2 * hp + 2)), qof)
            step("vector", q_rope,
                 reads=[res(f"pb{qb}"), res(f"pb{qb+1}"), res("consts"), res("tabs"), res("scr0"),
                        res("scr1"), res("q2"), res("qfm")],
                 writes=[res("qfm"), res("scr0"), res("scr1"), res("q2")])
            def q_lo(E, hp=hp):
                E.tensor_tensor(aAbf("qfm", NCH * RNG + RNG * (2 * hp),
                                     NCH * RNG + RNG * (2 * hp + 1)),
                                aA("q2", c0=0, c1=RNG),
                                aAbf("qfm", RNG * (2 * hp), RNG * (2 * hp + 1)), ALU.subtract)
                return E.tensor_tensor(aAbf("qfm", NCH * RNG + RNG * (2 * hp + 1),
                                            NCH * RNG + RNG * (2 * hp + 2)),
                                       aA("q2", c0=RNG, c1=2 * RNG),
                                       aAbf("qfm", RNG * (2 * hp + 1), RNG * (2 * hp + 2)),
                                       ALU.subtract)
            step("vector", q_lo, reads=[res("q2"), res("qfm")], writes=[res("qfm")])
        def invq1(E):
            iq = invq_ap(0, 16)
            E.tensor_scalar_add(iq, bank(6, c0=0, c1=16), EPS_RMS_SCALED)
            E.drain()
            return E.reciprocal(iq, iq)
        step("vector", invq1, reads=[res("pb6"), res("invq")], writes=[res("invq")])
        step("scalar", lambda E: E.activation(invq_ap(0, 16), invq_ap(0, 16), AF.Sqrt),
             reads=[res("invq")], writes=[res("invq")])

        def qhib(h, kc, qt):
            return aAbf("qfm", RNG * (2 * h + kc) + 128 * qt, RNG * (2 * h + kc) + 128 * (qt + 1))
        def qlob(h, kc, qt):
            b0 = NCH * RNG
            return aAbf("qfm", b0 + RNG * (2 * h + kc) + 128 * qt,
                        b0 + RNG * (2 * h + kc) + 128 * (qt + 1))
        def khib(kc, sl, col):
            return aAbf("kring", (kc * 3 + sl) * RNG + col, (kc * 3 + sl) * RNG + col + 128)
        def klob(kc, sl, col):
            return aAbf("kring", 3072 + (kc * 3 + sl) * RNG + col,
                        3072 + (kc * 3 + sl) * RNG + col + 128)
        def vhib(sl, sub, dc):
            return aAbf("vring", (4 * sl + sub) * HD + 128 * dc,
                        (4 * sl + sub) * HD + 128 * (dc + 1))
        def vlob(sl, sub, dc):
            return aAbf("vring", 3072 + (4 * sl + sub) * HD + 128 * dc,
                        3072 + (4 * sl + sub) * HD + 128 * (dc + 1))

        def emit_smm(i, h, qt, t0g):
            p = i % 2
            bA, bB = 2 * p, 2 * p + 1
            def smm(E, h=h, qt=qt, t0g=t0g, r=r, bA=bA, bB=bB):
                prev_sl = ((t0g - 512) // RNG + 1) % 3
                cur_sl = ((t0g // RNG) + 1) % 3
                w1_ = 512 - 128 * qt
                runs = [(prev_sl, 128 * qt, 0, w1_)]
                if qt > 0:
                    runs.append((cur_sl, 0, w1_, 128 * qt))
                last = None
                first = True
                for kc in range(2):
                    for (sl, scol, dcol, w) in runs:
                        for lhs, rhsf in (((0, kc), (0, sl, scol, w)),
                                          ((0, kc), (1, sl, scol, w)),
                                          ((1, kc), (0, sl, scol, w))):
                            qh = qhib(h, kc, qt) if lhs[0] == 0 else qlob(h, kc, qt)
                            hl, _sl, _sc, _w = rhsf
                            if hl == 0:
                                kr_ = aAbf("kring", (kc * 3 + _sl) * RNG + _sc,
                                           (kc * 3 + _sl) * RNG + _sc + _w)
                            else:
                                kr_ = aAbf("kring", 3072 + (kc * 3 + _sl) * RNG + _sc,
                                           3072 + (kc * 3 + _sl) * RNG + _sc + _w)
                            if lhs[0] == 1 and hl == 1:
                                continue
                            last = E.matmul(bank(bA, c0=dcol, c1=dcol + w), qh, kr_,
                                            start=first, stop=False)
                            first = False
                ops = [("maskL",)]
                if r == 0:
                    for kb in range(4):
                        if qt + kb < 4:
                            ops.append(("kill", kb))
                for i_, op in enumerate(ops):
                    sp = (i_ == len(ops) - 1)
                    if op[0] == "maskL":
                        last = E.matmul(bank(bA, c0=0, c1=128), strictLb, negIb,
                                        start=False, stop=sp)
                    else:
                        last = E.matmul(bank(bA, c0=128 * op[1], c1=128 * (op[1] + 1)),
                                        ones_row, kbias, start=False, stop=sp)
                col = t0g - RNG * (t0g // RNG)
                firstB = True
                for kc in range(2):
                    for hl_q, hl_k in ((0, 0), (0, 1), (1, 0)):
                        qh = qhib(h, kc, qt) if hl_q == 0 else qlob(h, kc, qt)
                        base = 0 if hl_k == 0 else 3072
                        kr_ = aAbf("kring", base + (kc * 3 + cur_sl) * RNG + col,
                                   base + (kc * 3 + cur_sl) * RNG + col + 128)
                        E.matmul(bank(bB, c0=0, c1=128), qh, kr_, start=firstB, stop=False)
                        firstB = False
                return E.matmul(bank(bB, c0=0, c1=128), strictUb, negIb, start=False, stop=True)
            kslots = set()
            for kb in range(5):
                tk = t0g - 512 + 128 * kb
                kslots.add(((tk // RNG) + 1) % 3)
            step("tensor", smm,
                 reads=[res("qfm"), res("consts"), res(f"pb{bA}"), res(f"pb{bB}")] +
                       [res(f"ks{s}") for s in sorted(kslots)],
                 writes=[res(f"pb{bA}"), res(f"pb{bB}")])
            return kslots

        def emit_exp(i, h, qt):
            p = i % 2
            def expP(E, h=h, qt=qt, p=p):
                return E.activation(aA("pp"), psum[:, 1024 * p:1024 * p + 640], AF.Exp,
                                    scale=invq_ap(4 * h + qt, 4 * h + qt + 1),
                                    accum_out=lcol)
            step("scalar", expP, reads=[res(f"pb{2*p}"), res(f"pb{2*p+1}"), res("invq"),
                                        res("pp"), res("lc")],
                 writes=[res("pp"), res("lc")])

        def emit_psplit(i):
            def lr_(E):
                return E.reciprocal(lrec, lcol)
            step("vector", lr_, reads=[res("lc"), res("lr")], writes=[res("lr")])
            def phi(E):
                return E.activation(aAbf("phl", 0, 640), aA("pp"), AF.Copy, scale=lrec)
            step("scalar", phi, reads=[res("pp"), res("lr"), res("phl")], writes=[res("phl")])
            def plo(E):
                return E.scalar_tensor_tensor(aAbf("phl", 640, 1280), aA("pp"), lrec,
                                              aAbf("phl", 0, 640), ALU.mult, ALU.subtract)
            step("vector", plo, reads=[res("pp"), res("lr"), res("phl")], writes=[res("phl")])

        def ptb_at(i, kb, lo):
            at, _ = A_["pt"]
            base = at + 640 * (i % 2)
            off = (64 * kb) if not lo else (320 + 64 * kb)
            return arena[:, base + off: base + off + 64].bitcast(BF16)

        def emit_ptr(i):
            p = i % 2
            pb_pt = 4 + p
            def ptrA(E, pb_pt=pb_pt):
                last = None
                for kb in range(4):
                    last = E.transpose(psbf(pb_pt, 256 * kb, 256 * kb + 128),
                                       aAbf("phl", 128 * kb, 128 * (kb + 1)), identb)
                    last = E.transpose(psbf(pb_pt, 256 * kb + 128, 256 * kb + 256),
                                       aAbf("phl", 640 + 128 * kb, 640 + 128 * (kb + 1)), identb)
                return last
            step("tensor", ptrA, reads=[res("phl"), res("consts"), res(f"pb{pb_pt}")],
                 writes=[res(f"pb{pb_pt}")])
            def ptevA(E, pb_pt=pb_pt, i=i):
                at, _ = A_["pt"]
                base = at + 640 * (i % 2)
                hi = arena[:, base: base + 256].bitcast(BF16).rearrange("p (a c) -> p a c", c=128)
                lo = arena[:, base + 320: base + 576].bitcast(BF16).rearrange("p (a c) -> p a c", c=128)
                src3 = psbf(pb_pt, 0, 1024).rearrange("p (a c) -> p a c", c=128)
                E.tensor_copy(hi, src3[:, 0::2, :])
                return E.tensor_copy(lo, src3[:, 1::2, :])
            step("vector", ptevA, reads=[res(f"pb{pb_pt}"), res(f"pt{i%2}")],
                 writes=[res(f"pt{i%2}")])
            def ptrB(E, pb_pt=pb_pt):
                E.transpose(psbf(pb_pt, 0, 128), aAbf("phl", 512, 640), identb)
                return E.transpose(psbf(pb_pt, 128, 256), aAbf("phl", 1152, 1280), identb)
            step("tensor", ptrB, reads=[res("phl"), res("consts"), res(f"pb{pb_pt}"),
                                        res(f"pt{i%2}")],
                 writes=[res(f"pb{pb_pt}")])
            def ptevB(E, pb_pt=pb_pt, i=i):
                at, _ = A_["pt"]
                base = at + 640 * (i % 2)
                E.tensor_copy(arena[:, base + 256: base + 320].bitcast(BF16), psbf(pb_pt, 0, 128))
                return E.tensor_copy(arena[:, base + 576: base + 640].bitcast(BF16),
                                     psbf(pb_pt, 128, 256))
            step("vector", ptevB, reads=[res(f"pb{pb_pt}"), res(f"pt{i%2}")],
                 writes=[res(f"pt{i%2}")])

        def emit_av(i, h, qt, t0g, kslots):
            def avmm2(E, t0g=t0g, i=i):
                last = None
                for dc in range(2):
                    ob, oc0 = (7, 0) if dc == 0 else (6, 256)
                    plan = []
                    for kb in range(5):
                        tk = t0g - 512 + 128 * kb
                        rr = tk // RNG
                        sl = (rr + 1) % 3
                        col = tk - RNG * rr
                        sub = col // 128
                        plan.append((vhib(sl, sub, dc), ptb_at(i, kb, False)))
                        plan.append((vhib(sl, sub, dc), ptb_at(i, kb, True)))
                        plan.append((vlob(sl, sub, dc), ptb_at(i, kb, False)))
                    for i_, (lhs, rhs) in enumerate(plan):
                        last = E.matmul(bank(ob, c0=oc0, c1=oc0 + 128), lhs, rhs,
                                        start=(i_ == 0), stop=(i_ == len(plan) - 1))
                return last
            step("tensor", avmm2,
                 reads=[res(f"pt{i%2}"), res("pb6"), res("pb7")] +
                       [res(f"vs{s}") for s in sorted(kslots)],
                 writes=[res("pb6"), res("pb7")])
            def oevh(E, h=h, qt=qt):
                E.copy(aAbf("orng", RNG * (2 * h) + 128 * qt,
                            RNG * (2 * h) + 128 * (qt + 1)),
                       bank(7, c0=0, c1=128))
                return E.copy(aAbf("orng", RNG * (2 * h + 1) + 128 * qt,
                                   RNG * (2 * h + 1) + 128 * (qt + 1)),
                              bank(6, c0=256, c1=384))
            step("scalar", oevh, reads=[res("pb6"), res("pb7"), res("orng")],
                 writes=[res("orng")])
            def oevl(E, h=h, qt=qt):
                b0 = NCH * RNG
                E.tensor_tensor(aAbf("orng", b0 + RNG * (2 * h) + 128 * qt,
                                     b0 + RNG * (2 * h) + 128 * (qt + 1)),
                                bank(7, c0=0, c1=128),
                                aAbf("orng", RNG * (2 * h) + 128 * qt,
                                     RNG * (2 * h) + 128 * (qt + 1)), ALU.subtract)
                return E.tensor_tensor(aAbf("orng", b0 + RNG * (2 * h + 1) + 128 * qt,
                                            b0 + RNG * (2 * h + 1) + 128 * (qt + 1)),
                                       bank(6, c0=256, c1=384),
                                       aAbf("orng", RNG * (2 * h + 1) + 128 * qt,
                                            RNG * (2 * h + 1) + 128 * (qt + 1)), ALU.subtract)
            step("vector", oevl, reads=[res("pb6"), res("pb7"), res("orng")],
                 writes=[res("orng")])

        tiles = [(h, qt) for h in range(0 if dbg_noattn else NH) for qt in range(4)]
        pend_ptr = None
        pend_av = []
        for i, (h, qt) in enumerate(tiles):
            t0g = RNG * r + 128 * qt
            kslots = emit_smm(i, h, qt, t0g)
            emit_exp(i, h, qt)
            if pend_ptr is not None:
                emit_ptr(pend_ptr[0])
                pend_av.append(pend_ptr)
            emit_psplit(i)
            while len(pend_av) > 1:
                emit_av(*pend_av.pop(0))
            pend_ptr = (i, h, qt, t0g, kslots)
        if pend_ptr is not None:
            emit_ptr(pend_ptr[0])
            pend_av.append(pend_ptr)
        while pend_av:
            emit_av(*pend_av.pop(0))

        def ohib(ch):
            return aAbf("orng", RNG * ch, RNG * (ch + 1))
        def olob(ch):
            return aAbf("orng", NCH * RNG + RNG * ch, NCH * RNG + RNG * (ch + 1))
        for oc in range(0 if dbg_nowo else NCH):
            b = oc % 2
            def womm(E, oc=oc, b=b):
                plan = []
                for ch in range(NCH):
                    whi = wob(0, ch, 128 * oc, 128 * (oc + 1))
                    wlo = wob(1, ch, 128 * oc, 128 * (oc + 1))
                    plan += [(whi, ohib(ch)), (whi, olob(ch)), (wlo, ohib(ch))]
                last = None
                for i_, (lhs, rhs) in enumerate(plan):
                    last = E.matmul(bank(b, c0=0, c1=RNG), lhs, rhs,
                                    start=(i_ == 0), stop=(i_ == len(plan) - 1))
                return last
            step("tensor", womm, reads=[res("orng"), res("w"), res(f"pb{b}")],
                 writes=[res(f"pb{b}")])
            def woev(E, oc=oc, b=b):
                return E.tensor_tensor(aA("xfm", c0=RNG * oc, c1=RNG * (oc + 1)),
                                       bank(b, c0=0, c1=RNG),
                                       aA("xfm", c0=RNG * oc, c1=RNG * (oc + 1)), ALU.add)
            step("vector", woev, reads=[res(f"pb{b}"), res("xfm")], writes=[res("xfm")])
            step("gpsimd", lambda E, oc=oc, r=r:
                 E.dma_start(out=d_x1[128 * oc:128 * (oc + 1), RNG * r:RNG * (r + 1)],
                             in_=aA("xfm", c0=RNG * oc, c1=RNG * (oc + 1))),
                 reads=[res("xfm")], writes=[res("x1s")], dma="x1o")

    if dbg_nob:
        bld.finalize()
        return nc
    barrier()
    for ch in range(NCH):
        stg = "x1slab" if ch % 2 == 0 else "xn2"
        step("sync", lambda E, ch=ch, stg=stg: E.dma_start(out=aB(stg),
                                                           in_=d_w1[128 * ch:128 * (ch + 1), :]),
             reads=[res(stg)], writes=[res(stg)], dma=f"wld{ch % 2}")
        def w1split(E, ch=ch, stg=stg):
            E.tensor_copy(w1b(0, ch, 0, FF), aB(stg))
            E.drain()
            return E.tensor_tensor(w1b(1, ch, 0, FF), aB(stg), w1b(0, ch, 0, FF),
                                   ALU.subtract)
        step("vector", w1split, reads=[res(stg), res("w")], writes=[res("w")])

    for s in range(NRANGE):
        def ldx1(E, s=s):
            return [E.dma_start(out=aB("x1slab", c0=RNG * ch, c1=RNG * (ch + 1)),
                                in_=d_x1[128 * ch:128 * (ch + 1), RNG * s:RNG * (s + 1)])
                    for ch in range(NCH)]
        step("sync", ldx1, reads=[res("x1s"), res("x1slab")], writes=[res("x1slab")],
             dma="x1i", n_dma=NCH)
        def x1sq(E):
            last = None
            for ch in range(NCH):
                last = E.activation(aB("xn2", c0=RNG * ch, c1=RNG * (ch + 1)),
                                    aB("x1slab", c0=RNG * ch, c1=RNG * (ch + 1)), AF.Square)
            return last
        step("scalar", x1sq, reads=[res("x1slab"), res("xn2")], writes=[res("xn2")])
        def stat2(E):
            last = None
            for ch in range(NCH):
                E.matmul(bank(6, p1=1, c0=0, c1=RNG), ones_col,
                         aB("x1slab", c0=RNG * ch, c1=RNG * (ch + 1)),
                         start=(ch == 0), stop=(ch == NCH - 1))
            for ch in range(NCH):
                last = E.matmul(bank(7, p1=1, c0=0, c1=RNG), ones_col,
                                aB("xn2", c0=RNG * ch, c1=RNG * (ch + 1)),
                                start=(ch == 0), stop=(ch == NCH - 1))
            return last
        step("tensor", stat2, reads=[res("x1slab"), res("xn2"), res("consts"),
                                     res("pb6"), res("pb7")],
             writes=[res("pb6"), res("pb7")])
        def stat2r(E):
            mu = aB("brows", p0=0, p1=1, c0=0, c1=RNG)
            e2 = aB("brows", p0=0, p1=1, c0=RNG, c1=2 * RNG)
            tmp = aB("bbc", p0=0, p1=1, c0=0, c1=RNG)
            E.tensor_scalar_mul(mu, bank(6, p1=1, c0=0, c1=RNG), 1.0 / C)
            E.tensor_scalar_mul(e2, bank(7, p1=1, c0=0, c1=RNG), 1.0 / C)
            E.drain()
            E.tensor_tensor(tmp, mu, mu, ALU.mult)
            E.drain()
            return E.tensor_tensor(e2, e2, tmp, ALU.subtract)
        step("vector", stat2r, reads=[res("pb6"), res("pb7"), res("brows"), res("bbc")],
             writes=[res("brows"), res("bbc")])
        def stat2s(E):
            e2 = aB("brows", p0=0, p1=1, c0=RNG, c1=2 * RNG)
            E.activation(e2, e2, AF.Ln, bias=epsln[0:1, :])
            E.drain()
            return E.activation(e2, e2, AF.Exp, scale=-0.5)
        step("scalar", stat2s, reads=[res("brows"), res("consts")], writes=[res("brows")])
        def bc2(E):
            E.matmul(bank(6, c0=0, c1=RNG), ones_row, aB("brows", p0=0, p1=1, c0=0, c1=RNG),
                     start=True, stop=True)
            return E.matmul(bank(7, c0=0, c1=RNG), ones_row,
                            aB("brows", p0=0, p1=1, c0=RNG, c1=2 * RNG),
                            start=True, stop=True)
        step("tensor", bc2, reads=[res("brows"), res("consts"), res("pb6"), res("pb7")],
             writes=[res("pb6"), res("pb7")])
        def bc2ev(E):
            E.tensor_copy(aB("bbc", c0=0, c1=RNG), bank(6, c0=0, c1=RNG))
            return E.tensor_copy(aB("bbc", c0=RNG, c1=2 * RNG), bank(7, c0=0, c1=RNG))
        step("vector", bc2ev, reads=[res("pb6"), res("pb7"), res("bbc")], writes=[res("bbc")])
        def xn2mk(E):
            last = None
            for ch in range(NCH):
                E.tensor_tensor(aB("xn2", c0=RNG * ch, c1=RNG * (ch + 1)),
                                aB("x1slab", c0=RNG * ch, c1=RNG * (ch + 1)),
                                aB("bbc", c0=0, c1=RNG), ALU.subtract)
            E.drain()
            for ch in range(NCH):
                last = E.tensor_tensor(aB("xn2", c0=RNG * ch, c1=RNG * (ch + 1)),
                                       aB("xn2", c0=RNG * ch, c1=RNG * (ch + 1)),
                                       aB("bbc", c0=RNG, c1=2 * RNG), ALU.mult)
            return last
        step("vector", xn2mk, reads=[res("x1slab"), res("bbc"), res("xn2")], writes=[res("xn2")])
        def xn2split(E):
            E.tensor_copy(aBbf("xn2p", 0, NCH * RNG), aB("xn2"))
            E.drain()
            return E.tensor_tensor(aBbf("xn2p", NCH * RNG, 2 * NCH * RNG), aB("xn2"),
                                   aBbf("xn2p", 0, NCH * RNG), ALU.subtract)
        step("vector", xn2split, reads=[res("xn2"), res("xn2p")], writes=[res("xn2p")])
        for fc in range(NFCH):
            b = fc % 4
            def hmm(E, fc=fc, b=b):
                last = None
                first = True
                for ch in range(NCH):
                    xhi = aBbf("xn2p", RNG * ch, RNG * ch + RNG)
                    xlo = aBbf("xn2p", NCH * RNG + RNG * ch, NCH * RNG + RNG * (ch + 1))
                    whi = w1b(0, ch, 128 * fc, 128 * (fc + 1))
                    wlo = w1b(1, ch, 128 * fc, 128 * (fc + 1))
                    for lhs, rhs in ((whi, xhi), (whi, xlo), (wlo, xhi)):
                        last = E.matmul(bank(b, c0=0, c1=RNG), lhs, rhs,
                                        start=first,
                                        stop=(ch == NCH - 1 and rhs is xhi and lhs is wlo))
                        first = False
                return last
            step("tensor", hmm, reads=[res("xn2p"), res("w"), res(f"pb{b}")],
                 writes=[res(f"pb{b}")])
            hb = fc % 2
            def hgelu(E, fc=fc, b=b, hb=hb):
                return E.activation(aB("hstg", c0=512 * hb, c1=512 * hb + RNG),
                                    bank(b, c0=0, c1=RNG), AF.Gelu_apprx_tanh, bias=b1ap(fc))
            step("scalar", hgelu, reads=[res(f"pb{b}"), res("consts"), res(f"hst{hb}")],
                 writes=[res(f"hst{hb}")])
            def hhicast(E, hb=hb):
                return E.copy(aBbf("hhi", 512 * hb, 512 * hb + RNG),
                              aB("hstg", c0=512 * hb, c1=512 * hb + RNG))
            step("scalar", hhicast, reads=[res(f"hst{hb}"), res(f"hhi{hb}")],
                 writes=[res(f"hhi{hb}")])
            def hlocomp(E, hb=hb):
                return E.tensor_tensor(aBbf("hlo", 512 * hb, 512 * hb + RNG),
                                       aB("hstg", c0=512 * hb, c1=512 * hb + RNG),
                                       aBbf("hhi", 512 * hb, 512 * hb + RNG), ALU.subtract)
            step("vector", hlocomp, reads=[res(f"hst{hb}"), res(f"hhi{hb}"), res(f"hlo{hb}")],
                 writes=[res(f"hlo{hb}")])
            step("gpsimd", lambda E, fc=fc, s=s, hb=hb: [
                 E.dma_start(out=d_hhi[128 * fc:128 * (fc + 1), RNG * s:RNG * (s + 1)],
                             in_=aBbf("hhi", 512 * hb, 512 * hb + RNG)),
                 E.dma_start(out=d_hlo[128 * fc:128 * (fc + 1), RNG * s:RNG * (s + 1)],
                             in_=aBbf("hlo", 512 * hb, 512 * hb + RNG))],
                 reads=[res(f"hhi{hb}"), res(f"hlo{hb}")], writes=[res("hs")],
                 dma="hso", n_dma=2)

    if dbg_noc:
        bld.finalize()
        return nc
    barrier()
    for fc4 in range(NFCH // 4):
        stg = "cx1" if fc4 % 2 == 0 else "x2"
        step("sync", lambda E, fc4=fc4, stg=stg: [
            E.dma_start(out=aC(stg, c0=1024 * q, c1=1024 * (q + 1)),
                        in_=d_w2[128 * (4 * fc4 + q):128 * (4 * fc4 + q + 1), :])
            for q in range(4)],
             reads=[res("cx10" if stg == "cx1" else stg)],
             writes=[res("cx10" if stg == "cx1" else stg)], dma=f"wld{fc4 % 2}", n_dma=4)
        def w2split(E, fc4=fc4, stg=stg):
            for q in range(4):
                E.tensor_copy(w2b(0, 4 * fc4 + q, 0, C), aC(stg, c0=1024 * q, c1=1024 * (q + 1)))
            E.drain()
            last = None
            for q in range(4):
                last = E.tensor_tensor(w2b(1, 4 * fc4 + q, 0, C),
                                       aC(stg, c0=1024 * q, c1=1024 * (q + 1)),
                                       w2b(0, 4 * fc4 + q, 0, C), ALU.subtract)
            return last
        step("vector", w2split, reads=[res("cx10" if stg == "cx1" else stg), res("w")],
             writes=[res("w")])

    for s in range(NRANGE):
        sp_ = s % 2
        def ldx1c(E, s=s, sp_=sp_):
            return [E.dma_start(out=aC("cx1", c0=4096 * sp_ + RNG * ch,
                                       c1=4096 * sp_ + RNG * (ch + 1)),
                                in_=d_x1[128 * ch:128 * (ch + 1), RNG * s:RNG * (s + 1)])
                    for ch in range(NCH)]
        step("sync", ldx1c, reads=[res("x1s"), res(f"cx1{sp_}")], writes=[res(f"cx1{sp_}")],
             dma=f"x1i{sp_}", n_dma=NCH)
        for fch in range(NFCH):
            hbuf = fch % 4
            step("sync", lambda E, fch=fch, s=s, hbuf=hbuf: [
                 E.dma_start(out=aCbf("hbufs", 1024 * hbuf, 1024 * hbuf + RNG),
                             in_=d_hhi[128 * fch:128 * (fch + 1), RNG * s:RNG * (s + 1)]),
                 E.dma_start(out=aCbf("hbufs", 1024 * hbuf + RNG, 1024 * hbuf + 2 * RNG),
                             in_=d_hlo[128 * fch:128 * (fch + 1), RNG * s:RNG * (s + 1)])],
                 reads=[res("hs"), res(f"hb{hbuf}")], writes=[res(f"hb{hbuf}")],
                 dma=f"hsi{hbuf}", n_dma=2)
            def y2mm(E, fch=fch, hbuf=hbuf):
                hhi = aCbf("hbufs", 1024 * hbuf, 1024 * hbuf + RNG)
                hlo = aCbf("hbufs", 1024 * hbuf + RNG, 1024 * hbuf + 2 * RNG)
                last = None
                for dc in range(NCH):
                    whi = w2b(0, fch, 128 * dc, 128 * (dc + 1))
                    wlo = w2b(1, fch, 128 * dc, 128 * (dc + 1))
                    for lhs, rhs in ((whi, hhi), (whi, hlo), (wlo, hhi)):
                        last = E.matmul(bank(dc, c0=0, c1=RNG), lhs, rhs,
                                        start=(fch == 0 and lhs is whi and rhs is hhi),
                                        stop=(fch == NFCH - 1 and lhs is wlo))
                return last
            step("tensor", y2mm,
                 reads=[res(f"hb{hbuf}"), res("w")] + [res(f"pb{d}") for d in range(8)],
                 writes=[res(f"pb{d}") for d in range(8)])
        def y2ev(E, sp_=sp_):
            last = None
            for dc in range(NCH):
                last = E.scalar_tensor_tensor(aC("x2", c0=RNG * dc, c1=RNG * (dc + 1)),
                                              bank(dc, c0=0, c1=RNG), b2ap(dc),
                                              aC("cx1", c0=4096 * sp_ + RNG * dc,
                                                 c1=4096 * sp_ + RNG * (dc + 1)),
                                              ALU.add, ALU.add)
            return last
        step("vector", y2ev,
             reads=[res(f"pb{d}") for d in range(8)] + [res(f"cx1{sp_}"), res("consts"),
                                                        res("x2")],
             writes=[res("x2")])
        for j in range(4):
            b0 = 2 * (j % 2)
            def otr(E, j=j, b0=b0):
                last = None
                for dc in range(NCH):
                    bb = b0 + dc // 4
                    last = E.transpose(bank(bb, c0=128 * (dc % 4), c1=128 * (dc % 4) + 128),
                                       aC("x2", c0=RNG * dc + 128 * j,
                                          c1=RNG * dc + 128 * (j + 1)),
                                       ident)
                return last
            step("tensor", otr, reads=[res("x2"), res("consts"),
                                       res(f"pb{b0}"), res(f"pb{b0+1}")],
                 writes=[res(f"pb{b0}"), res(f"pb{b0+1}")])
            ob = j % 2
            def otev(E, b0=b0, ob=ob):
                E.tensor_copy(aC("ostg", c0=1024 * ob, c1=1024 * ob + 512),
                              bank(b0, c0=0, c1=512))
                return E.tensor_copy(aC("ostg", c0=1024 * ob + 512, c1=1024 * ob + 1024),
                                     bank(b0 + 1, c0=0, c1=512))
            step("vector", otev, reads=[res(f"pb{b0}"), res(f"pb{b0+1}"), res(f"os{ob}")],
                 writes=[res(f"os{ob}")])
            step("gpsimd", lambda E, j=j, s=s, ob=ob:
                 E.dma_start(out=d_y[RNG * s + 128 * j: RNG * s + 128 * (j + 1), :],
                             in_=aC("ostg", c0=1024 * ob, c1=1024 * ob + 1024)),
                 reads=[res(f"os{ob}")], writes=[res("y")], dma="yo")

    bld.finalize()
    return nc


def _host_stage(inputs):
    x = np.asarray(inputs["L0_attn_x"], dtype=np.float32)
    Wq = np.asarray(inputs["L0_attn_Wq"], dtype=np.float32)
    Wk = np.asarray(inputs["L0_attn_Wk"], dtype=np.float32)
    Wv = np.asarray(inputs["L0_attn_Wv"], dtype=np.float32)
    Wo = np.asarray(inputs["L0_attn_Wo"], dtype=np.float32)
    gq = np.asarray(inputs["L0_attn_q_gamma"], dtype=np.float32)
    gk = np.asarray(inputs["L0_attn_k_gamma"], dtype=np.float32)
    w1 = np.asarray(inputs["L0_ffn_w1"], dtype=np.float32)
    b1 = np.asarray(inputs["L0_ffn_b1"], dtype=np.float32)
    w2 = np.asarray(inputs["L0_ffn_w2"], dtype=np.float32)
    b2 = np.asarray(inputs["L0_ffn_b2"], dtype=np.float32)

    permq = np.concatenate([np.concatenate([np.arange(0, HD, 2), np.arange(1, HD, 2)]) + HD * h
                            for h in range(NH)])
    permk = np.concatenate([np.arange(0, HD, 2), np.arange(1, HD, 2)])
    wqT = np.ascontiguousarray(Wq[permq, :].T)
    wkT = np.ascontiguousarray(Wk[permk, :].T)
    wvT = np.ascontiguousarray(Wv.T)
    woT = np.ascontiguousarray(Wo.T)

    gams = np.ascontiguousarray(
        np.stack([gq[0::2], gq[1::2], gk[0::2], gk[1::2]], axis=1))

    import ml_dtypes
    ii = np.arange(128)
    consts = np.zeros((128, 386), np.float32)
    consts[:, 0:128] = np.eye(128, dtype=np.float32)
    consts[:, 128] = 1.0
    consts[:, 129] = EPS_LN
    def _b(a):
        return a.astype(ml_dtypes.bfloat16).view(np.uint16).view(np.float32)
    consts[:, 130:194] = _b(np.eye(128, dtype=np.float32))
    consts[:, 194:258] = _b((ii[:, None] < ii[None, :]).astype(np.float32))
    consts[:, 258:322] = _b((ii[:, None] > ii[None, :]).astype(np.float32))
    consts[:, 322:386] = _b(NEG * np.eye(128, dtype=np.float32))

    import jax
    import jax.numpy as jnp
    cpu = jax.devices("cpu")[0]
    with jax.default_device(cpu):
        inv_freq = 1.0 / (1000000.0 ** (jnp.arange(0, HD, 2, dtype=jnp.float32) / HD))
        t = jnp.arange(8192, dtype=jnp.float32)
        freqs = jnp.outer(t, inv_freq)
        cos_full = np.asarray(jnp.cos(freqs), dtype=np.float32)
        sin_full = np.asarray(jnp.sin(freqs), dtype=np.float32)

    in_maps = []
    for core in range(8):
        b = core // 2
        half = core % 2
        g0 = half * TCORE
        if half == 0:
            haloimg = np.zeros((HALO, C), np.float32)
        else:
            haloimg = x[b, g0 - HALO:g0, :]
        x_sh = np.ascontiguousarray(np.concatenate([haloimg, x[b, g0:g0 + TCORE, :]], axis=0))
        pos = np.clip(np.arange(g0 - HALO, g0 + TCORE), 0, 8191)
        cosT = np.ascontiguousarray(cos_full[pos, :].T)
        sinT = np.ascontiguousarray(sin_full[pos, :].T)
        crow = np.zeros((1, 384), np.float32)
        crow[0, 0:128] = 1.0
        crow[0, 128:256] = NEG if half == 0 else 0.0
        crow[0, 256:384] = 16.0
        in_maps.append(dict(
            x_sh=x_sh, wqT=wqT, wkT=wkT, wvT=wvT, woT=woT,
            w1=np.ascontiguousarray(w1[core]),
            b1c=np.ascontiguousarray(b1[core].reshape(NFCH, 128).T),
            w2=np.ascontiguousarray(w2[core]),
            b2c=np.ascontiguousarray(b2[core].reshape(NCH, 128).T),
            gams=gams, cosT=cosT, sinT=sinT, consts=consts, crow=crow,
        ))
    return in_maps


def kernel(**inputs):
    in_maps = _host_stage(inputs)
    nc = build_program()
    res = run_bass_kernel_spmd(nc, in_maps, core_ids=list(range(8)))
    out = np.empty((4, 8192, C), np.float32)
    for core in range(8):
        b = core // 2
        half = core % 2
        out[b, half * TCORE:(half + 1) * TCORE, :] = np.asarray(res.results[core]["y"])
    return out


# version 11
# speedup vs baseline: 1.0177x; 1.0177x over previous
import numpy as np
import concourse.bass as bass
import concourse.mybir as mybir
from concourse.bass_utils import run_bass_kernel_spmd

F32 = mybir.dt.float32
BF16 = mybir.dt.bfloat16
AF = mybir.ActivationFunctionType
ALU = mybir.AluOpType

C = 1024
HD = 256
NH = 4
TCORE = 4096
HALO = 512
RNG = 512
NRANGE = TCORE // RNG
NCH = C // 128
FF = 4096
NFCH = FF // 128
NEG = -1.0e5
EPS_LN = 1.0e-5
EPS_RMS_SCALED = 256.0 * 1.0e-6

ENGINES = ("tensor", "vector", "scalar", "gpsimd", "sync")


class Res:
    __slots__ = ("name", "w", "r")
    def __init__(self, name):
        self.name = name
        self.w = None
        self.r = []


class Builder:
    def __init__(self, nc):
        self.nc = nc
        self.prog = {e: [] for e in ENGINES}
        self.sem = {e: nc.alloc_semaphore(name="s_" + e) for e in ENGINES}
        self.cnt = {e: 0 for e in ENGINES}
        self.ic = {e: 0 for e in ENGINES}
        self.waited = {e: {} for e in ENGINES}
        self.drained = {e: 0 for e in ENGINES}
        self.R = {}

    def res(self, name):
        if name not in self.R:
            self.R[name] = Res(name)
        return self.R[name]

    def _sem(self, key):
        if key not in self.sem:
            self.sem[key] = self.nc.alloc_semaphore(name="s_" + key)
            self.cnt[key] = 0
        return self.sem[key]

    def step(self, eng, fn, reads=(), writes=(), dma=None, n_dma=1):
        deps = []
        for r in reads:
            if r.w is not None:
                deps.append(r.w)
            if r.name.startswith("pb") and eng in ("vector", "scalar"):
                other = "scalar" if eng == "vector" else "vector"
                deps.extend(t for t in r.r if t[0] == other)
        for w in writes:
            deps.extend(w.r)
            if w.w is not None:
                deps.append(w.w)
        self.ic[eng] += 1
        my_idx = self.ic[eng]
        if dma is not None:
            key = "q_" + dma
            self._sem(key)
            self.cnt[key] += 16 * n_dma
            my_cnt = self.cnt[key]
        else:
            key = eng
            self.cnt[eng] += 1
            my_cnt = self.cnt[eng]
        sem, waited = self.sem, self.waited[eng]
        xwaits, need_drain = {}, False
        for (de, dc, di, ddma) in deps:
            if de == eng and not ddma:
                if di > self.drained[eng]:
                    need_drain = True
            else:
                if dc > xwaits.get(de, 0):
                    xwaits[de] = dc
        if need_drain:
            self.drained[eng] = my_idx - 1
        do_drain = need_drain and eng in ("vector", "scalar", "gpsimd")

        def emit(E):
            for de, dc in sorted(xwaits.items()):
                if dc > waited.get(de, 0):
                    E.wait_ge(sem[de], dc)
                    waited[de] = dc
            if do_drain:
                E.drain()
            out = fn(E)
            if dma is not None:
                outs = out if isinstance(out, (list, tuple)) else [out]
                assert len(outs) == n_dma, (eng, len(outs), n_dma)
                for ins in outs:
                    ins.then_inc(sem[key], 16)
            else:
                out.then_inc(sem[key], 1)
        self.prog[eng].append(emit)
        tok = (key, my_cnt, my_idx, dma is not None)
        for w in writes:
            w.w = tok
            w.r = []
        for r in reads:
            r.r.append(tok)
        return tok

    def barrier(self):
        snap = dict(self.cnt)
        waited_all = self.waited
        sem = self.sem
        keys = sorted(snap.keys())
        for e in ENGINES:
            dr = e in ("vector", "scalar", "gpsimd")
            def f(E, e=e, dr=dr, keys=tuple(keys)):
                w = waited_all[e]
                for o in keys:
                    if o == e:
                        continue
                    v = snap[o]
                    if v > w.get(o, 0):
                        E.wait_ge(sem[o], v)
                        w[o] = v
                if dr:
                    E.drain()
            self.prog[e].append(f)
            self.drained[e] = self.ic[e]
        for r in self.R.values():
            r.w = None
            r.r = []

    def finalize(self):
        fincnt = dict(self.cnt)
        sem, waited = self.sem, self.waited["gpsimd"]
        def fin(E):
            for o in sorted(fincnt.keys()):
                if o == "gpsimd":
                    continue
                if fincnt[o] > waited.get(o, 0):
                    E.wait_ge(sem[o], fincnt[o])
        self.prog["gpsimd"].append(fin)
        with self.nc.Block() as block:
            @block.sync
            def _(E):
                for f in self.prog["sync"]:
                    f(E)
            @block.tensor
            def _(E):
                for f in self.prog["tensor"]:
                    f(E)
            @block.vector
            def _(E):
                for f in self.prog["vector"]:
                    f(E)
            @block.scalar
            def _(E):
                for f in self.prog["scalar"]:
                    f(E)
            @block.gpsimd
            def _(E):
                for f in self.prog["gpsimd"]:
                    f(E)


ACOLS = 53208
O_CONST = 0
O_CROW = 388
O_GAMS = 772
O_B1 = 776
O_B2 = 808
O_ROWS = 816
O_W = 884
O_TAB = 33648 + 12288
A0 = 21364
A_ = {}
for _n, _c in [("xtok", 2048), ("xfm", 4096), ("xn", 4096), ("ropes", 1024),
               ("kring", 3072), ("vring", 3072), ("qfm", 4096), ("q2", 1024),
               ("bmu", 512), ("binv", 512), ("binvk", 512), ("pp", 640), ("pt", 640),
               ("orng", 4096)]:
    A_[_n] = A0 + sum(v for k, v in A_.items()) if False else 0
def _layout(base, items):
    d, cur = {}, base
    for n, c in items:
        d[n] = (cur, c)
        cur += c
    return d, cur
A_, A_END = _layout(A0, [("xtok", 2048), ("xfm", 4096), ("xn", 4096), ("ropes", 1024),
                         ("kring", 3072), ("vring", 3072), ("qfm", 4096), ("q2", 1024),
                         ("bmu", 512), ("binv", 512), ("binvk", 512), ("pp", 640),
                         ("phl", 640), ("pt", 1280), ("orng", 4096)])
assert A_END <= ACOLS, A_END
B0_ = 33652
B_, B_END = _layout(B0_, [("x1slab", 4096), ("xn2", 4096), ("xn2p", 4096),
                          ("brows", 1024), ("bbc", 1024), ("hstg", 1024),
                          ("hhi", 512), ("hlo", 512)])
assert B_END <= ACOLS, B_END
C_, C_END = _layout(B0_, [("hbufs", 2048), ("cx1", 8192), ("x2", 4096), ("ostg", 2048)])
assert C_END <= ACOLS, C_END
O_TAB = A_END
assert O_TAB + 2 * RNG <= ACOLS, O_TAB


def build_program():
    import os
    dbg = os.environ.get("BASSDBG", "")
    dbg_nrange = NRANGE
    dbg_nob = "nob" in dbg
    dbg_noc = "noc" in dbg
    dbg_noq = "noq" in dbg
    dbg_noattn = "noattn" in dbg
    dbg_nowo = "nowo" in dbg
    dbg_noqms = "noqms" in dbg
    dbg_noqrope = "noqrope" in dbg
    dbg_noinvq = "noinvq" in dbg
    dbg_qms1 = "qms1" in dbg
    for tokn in dbg.split(","):
        if tokn.startswith("r="):
            dbg_nrange = int(tokn[2:])
    nc = bass.Bass("TRN2", target_bir_lowering=False)

    d_x = nc.dram_tensor("x_sh", [HALO + TCORE, C], F32, kind="ExternalInput").ap()
    d_wq = nc.dram_tensor("wqT", [C, C], F32, kind="ExternalInput").ap()
    d_wk = nc.dram_tensor("wkT", [C, HD], F32, kind="ExternalInput").ap()
    d_wv = nc.dram_tensor("wvT", [C, HD], F32, kind="ExternalInput").ap()
    d_wo = nc.dram_tensor("woT", [C, C], F32, kind="ExternalInput").ap()
    d_w1 = nc.dram_tensor("w1", [C, FF], F32, kind="ExternalInput").ap()
    d_w2 = nc.dram_tensor("w2", [FF, C], F32, kind="ExternalInput").ap()
    d_b1 = nc.dram_tensor("b1c", [128, NFCH], F32, kind="ExternalInput").ap()
    d_b2 = nc.dram_tensor("b2c", [128, NCH], F32, kind="ExternalInput").ap()
    d_g = nc.dram_tensor("gams", [128, 4], F32, kind="ExternalInput").ap()
    d_cos = nc.dram_tensor("cosT", [128, HALO + TCORE], F32, kind="ExternalInput").ap()
    d_sin = nc.dram_tensor("sinT", [128, HALO + TCORE], F32, kind="ExternalInput").ap()
    d_cst = nc.dram_tensor("consts", [128, 386], F32, kind="ExternalInput").ap()
    d_cr = nc.dram_tensor("crow", [1, 384], F32, kind="ExternalInput").ap()

    dbg_stat = "dbgstat" in dbg
    if dbg_stat:
        d_dbg = nc.dram_tensor("dbgrow", [1, 1024], F32, kind="ExternalOutput").ap()
        d_dbg2 = nc.dram_tensor("dbgxn", [128, 2048], F32, kind="ExternalOutput").ap()
        d_dbg3 = nc.dram_tensor("dbgkr", [128, 1024], F32, kind="ExternalOutput").ap()
        d_dbg4 = nc.dram_tensor("dbgo", [128, 4096], F32, kind="ExternalOutput").ap()
        d_dbg5 = nc.dram_tensor("dbgv", [128, 1024], F32, kind="ExternalOutput").ap()
    d_x1 = nc.dram_tensor("x1s", [C, TCORE], F32, kind="Internal").ap()
    d_mu = nc.dram_tensor("mus", [2, TCORE], F32, kind="Internal").ap()
    d_hhi = nc.dram_tensor("hshi", [FF, TCORE], BF16, kind="Internal").ap()
    d_hlo = nc.dram_tensor("hslo", [FF, TCORE], BF16, kind="Internal").ap()
    d_y = nc.dram_tensor("y", [TCORE, C], F32, kind="ExternalOutput").ap()

    arena = nc.alloc_sbuf_tensor("arena", [128, ACOLS], F32)
    psum = nc.alloc_psum_tensor("psum", [128, 4096], F32)

    def bank(b, p0=0, p1=128, c0=0, c1=512):
        return psum[p0:p1, 512 * b + c0: 512 * b + c1]

    def aA(name, p0=0, p1=128, c0=0, c1=None):
        at, w = A_[name]
        return arena[p0:p1, at + c0: at + (w if c1 is None else c1)]

    def aB(name, p0=0, p1=128, c0=0, c1=None):
        at, w = B_[name]
        return arena[p0:p1, at + c0: at + (w if c1 is None else c1)]

    def aC(name, p0=0, p1=128, c0=0, c1=None):
        at, w = C_[name]
        return arena[p0:p1, at + c0: at + (w if c1 is None else c1)]

    ident = arena[:, O_CONST + 0: O_CONST + 128]
    ones_col = arena[:, O_CONST + 128: O_CONST + 129]
    epsln = arena[:, O_CONST + 129: O_CONST + 130]
    identb = arena[:, O_CONST + 130: O_CONST + 194].bitcast(BF16)
    strictLb = arena[:, O_CONST + 194: O_CONST + 258].bitcast(BF16)
    strictUb = arena[:, O_CONST + 258: O_CONST + 322].bitcast(BF16)
    negIb = arena[:, O_CONST + 322: O_CONST + 386].bitcast(BF16)
    ones_row = arena[0:1, O_CROW + 0: O_CROW + 128]
    kbias = arena[0:1, O_CROW + 128: O_CROW + 256]
    ones_row16 = arena[0:1, O_CROW + 256: O_CROW + 384]
    def gam(i): return arena[:, O_GAMS + i: O_GAMS + i + 1]
    def b1ap(fc): return arena[:, O_B1 + fc: O_B1 + fc + 1]
    def b2ap(dc): return arena[:, O_B2 + dc: O_B2 + dc + 1]
    def invq_ap(c0, c1): return arena[:, O_ROWS + 16 + c0: O_ROWS + 16 + c1]
    lcol = arena[:, O_ROWS + 32: O_ROWS + 33]
    lrec = arena[:, O_ROWS + 33: O_ROWS + 34]
    def wq_t(ch, d0, d1): return arena[:, O_W + ch * 1024 + d0: O_W + ch * 1024 + d1]
    def wk_t(ch, d0, d1): return arena[:, O_W + 8192 + ch * HD + d0: O_W + 8192 + ch * HD + d1]
    def wv_t(ch): return arena[:, O_W + 10240 + ch * HD: O_W + 10240 + (ch + 1) * HD]
    def wo_t(ch, d0, d1): return arena[:, O_W + 12288 + ch * 1024 + d0: O_W + 12288 + ch * 1024 + d1]
    def w1_t(ch, f0, f1): return arena[:, O_W + ch * FF + f0: O_W + ch * FF + f1]
    def w2_t(fch, d0, d1): return arena[:, O_W + fch * C + d0: O_W + fch * C + d1]
    def bfv(f32c0, nbf):
        return arena[:, f32c0: f32c0 + nbf // 2].bitcast(BF16)
    def w1b(hl, ch, f0, f1):
        base = O_W * 2 + hl * (NCH * FF) + ch * FF
        return bfv(0, 0) if False else arena[:, (base + f0) // 2: (base + f1) // 2].bitcast(BF16)
    def w2b(hl, fch, d0, d1):
        base = O_W * 2 + hl * (NFCH * C) + fch * C
        return arena[:, (base + d0) // 2: (base + d1) // 2].bitcast(BF16)
    def aBbf(name, c0bf, c1bf):
        at, w = B_[name]
        return arena[:, at + c0bf // 2: at + c1bf // 2].bitcast(BF16)
    def aCbf(name, c0bf, c1bf):
        at, w = C_[name]
        return arena[:, at + c0bf // 2: at + c1bf // 2].bitcast(BF16)
    def aAbf(name, c0bf, c1bf):
        at, w = A_[name]
        return arena[:, at + c0bf // 2: at + c1bf // 2].bitcast(BF16)
    def wab(base_bf, c0, c1):
        b = O_W * 2 + base_bf
        return arena[:, (b + c0) // 2: (b + c1) // 2].bitcast(BF16)
    def wqb(hl, ch, d0, d1): return wab(8192 * hl + ch * 1024, d0, d1)
    def wkb(hl, ch, d0, d1): return wab(16384 + 2048 * hl + ch * 256, d0, d1)
    def wvb(hl, ch): return wab(20480 + 2048 * hl + ch * 256, 0, 256)
    def wob(hl, ch, d0, d1): return wab(24576 + 8192 * hl + ch * 1024, d0, d1)
    def psbf(b, c0bf, c1bf):
        return psum[:, 512 * b + c0bf // 2: 512 * b + c1bf // 2].bitcast(BF16)
    def cosl(c0, c1): return arena[:, O_TAB + c0: O_TAB + c1]
    def sinl(c0, c1): return arena[:, O_TAB + RNG + c0: O_TAB + RNG + c1]

    bld = Builder(nc)
    step, res, barrier = bld.step, bld.res, bld.barrier

    step("sync", lambda E: [
        E.dma_start(out=arena[:, O_CONST:O_CONST + 386], in_=d_cst),
        E.dma_start(out=arena[0:1, O_CROW:O_CROW + 384], in_=d_cr),
        E.dma_start(out=arena[:, O_GAMS:O_GAMS + 4], in_=d_g),
        E.dma_start(out=arena[:, O_B1:O_B1 + NFCH], in_=d_b1),
        E.dma_start(out=arena[:, O_B2:O_B2 + NCH], in_=d_b2),
    ], writes=[res("consts")], dma="consts", n_dma=5)

    def wsplit_steps(tiles):
        batch, bcols = [], 0
        def flush(batch):
            n = len(batch)
            def ld(E, batch=tuple(batch)):
                out, c = [], 0
                for (dsrc, ncols, _hi, _lo) in batch:
                    out.append(E.dma_start(out=aA("xfm", c0=c, c1=c + ncols), in_=dsrc))
                    c += ncols
                return out
            step("sync", ld, reads=[res("xfm")], writes=[res("xfm")], dma="wload", n_dma=n)
            def sp(E, batch=tuple(batch)):
                c = 0
                for (dsrc, ncols, hi, lo) in batch:
                    E.tensor_copy(hi, aA("xfm", c0=c, c1=c + ncols))
                    c += ncols
                E.drain()
                c, last = 0, None
                for (dsrc, ncols, hi, lo) in batch:
                    last = E.tensor_tensor(lo, aA("xfm", c0=c, c1=c + ncols), hi, ALU.subtract)
                    c += ncols
                return last
            step("vector", sp, reads=[res("xfm"), res("w")], writes=[res("w")])
        for t in tiles:
            if bcols + t[1] > 4096:
                flush(batch)
                batch, bcols = [], 0
            batch.append(t)
            bcols += t[1]
        if batch:
            flush(batch)
    wtiles = []
    for ch in range(NCH):
        wtiles.append((d_wq[128 * ch:128 * (ch + 1), :], 1024,
                       wqb(0, ch, 0, 1024), wqb(1, ch, 0, 1024)))
    for ch in range(NCH):
        wtiles.append((d_wk[128 * ch:128 * (ch + 1), :], 256,
                       wkb(0, ch, 0, 256), wkb(1, ch, 0, 256)))
        wtiles.append((d_wv[128 * ch:128 * (ch + 1), :], 256,
                       wvb(0, ch), wvb(1, ch)))
    for ch in range(NCH):
        wtiles.append((d_wo[128 * ch:128 * (ch + 1), :], 1024,
                       wob(0, ch, 0, 1024), wob(1, ch, 0, 1024)))
    wsplit_steps(wtiles)

    for r in range(-1, dbg_nrange):
        slot = (r + 1) % 3
        tb0 = HALO + RNG * r

        step("sync", lambda E, tb0=tb0: [
            E.dma_start(out=cosl(0, RNG), in_=d_cos[:, tb0:tb0 + RNG]),
            E.dma_start(out=sinl(0, RNG), in_=d_sin[:, tb0:tb0 + RNG]),
        ], writes=[res("tabs")], dma="tabs", n_dma=2)

        for j in range(4):
            xb = j % 2
            step("sync", lambda E, tb0=tb0, j=j, xb=xb:
                 E.dma_start(out=aA("xtok", c0=1024 * xb, c1=1024 * (xb + 1)),
                             in_=d_x[tb0 + 128 * j: tb0 + 128 * (j + 1), :]),
                 writes=[res(f"xtok{xb}")], dma=f"xin{xb}")
            def tr(E, xb=xb, j=j):
                last = None
                b0 = 2 * (j % 2)
                for ch in range(NCH):
                    bb = b0 + (0 if ch < 4 else 1)
                    last = E.transpose(bank(bb, c0=128 * (ch % 4), c1=128 * (ch % 4) + 128),
                                       aA("xtok", c0=1024 * xb + 128 * ch,
                                          c1=1024 * xb + 128 * (ch + 1)),
                                       ident)
                return last
            step("tensor", tr, reads=[res(f"xtok{xb}"), res("consts"),
                                      res(f"pb{2*(j%2)}"), res(f"pb{2*(j%2)+1}")],
                 writes=[res(f"pb{2*(j%2)}"), res(f"pb{2*(j%2)+1}")])
            def tstat(E, j=j, xb=xb):
                sx = arena[:, O_ROWS + 0 + j: O_ROWS + 1 + j]
                sq = arena[:, O_ROWS + 4 + j: O_ROWS + 5 + j]
                E.reduce_sum(sx, aA("xtok", c0=1024 * xb, c1=1024 * (xb + 1)),
                             axis=mybir.AxisListType.X)
                return E.scalar_tensor_tensor(aA("ropes", c0=0, c1=1024),
                                              aA("xtok", c0=1024 * xb, c1=1024 * (xb + 1)),
                                              1.0,
                                              aA("xtok", c0=1024 * xb, c1=1024 * (xb + 1)),
                                              ALU.mult, ALU.mult, accum_out=sq)
            step("vector", tstat, reads=[res(f"xtok{xb}"), res("scr0"), res("scr1"),
                                         res("statc")],
                 writes=[res("scr0"), res("scr1"), res("statc")])
            def ev(E, j=j):
                last = None
                b0 = 2 * (j % 2)
                for ch in range(NCH):
                    bb = b0 + (0 if ch < 4 else 1)
                    last = E.copy(aA("xfm", c0=RNG * ch + 128 * j,
                                     c1=RNG * ch + 128 * (j + 1)),
                                  bank(bb, c0=128 * (ch % 4), c1=128 * (ch % 4) + 128))
                return last
            step("scalar", ev, reads=[res(f"pb{2*(j%2)}"), res(f"pb{2*(j%2)+1}"), res("xfm")],
                 writes=[res("xfm")])

        def statchain(E):
            sx = arena[:, O_ROWS + 0: O_ROWS + 4]
            sq = arena[:, O_ROWS + 4: O_ROWS + 8]
            mu = arena[:, O_ROWS + 8: O_ROWS + 12]
            iv = arena[:, O_ROWS + 12: O_ROWS + 16]
            E.tensor_scalar_mul(mu, sx, 1.0 / C)
            E.tensor_scalar_mul(iv, sq, 1.0 / C)
            E.drain()
            E.tensor_tensor(iv, iv, arena[:, O_ROWS + 8: O_ROWS + 12], ALU.subtract) \
                if False else None
            tmp = arena[:, O_ROWS + 34: O_ROWS + 38]
            E.tensor_tensor(tmp, mu, mu, ALU.mult)
            E.drain()
            E.tensor_tensor(iv, iv, tmp, ALU.subtract)
            E.drain()
            E.tensor_scalar_add(iv, iv, EPS_LN)
            E.drain()
            return E.reciprocal(iv, iv)
        step("vector", statchain, reads=[res("statc")], writes=[res("statc")])
        step("scalar", lambda E: E.activation(arena[:, O_ROWS + 12: O_ROWS + 16],
                                              arena[:, O_ROWS + 12: O_ROWS + 16], AF.Sqrt),
             reads=[res("statc")], writes=[res("statc")])
        def stat_tr(E):
            last = None
            for j in range(4):
                E.transpose(bank(6, p1=1, c0=128 * j, c1=128 * (j + 1)),
                            arena[:, O_ROWS + 8 + j: O_ROWS + 9 + j], ident)
                last = E.transpose(bank(7, p1=1, c0=128 * j, c1=128 * (j + 1)),
                                   arena[:, O_ROWS + 12 + j: O_ROWS + 13 + j], ident)
            return last
        step("tensor", stat_tr, reads=[res("statc"), res("consts"), res("pb6"), res("pb7")],
             writes=[res("pb6"), res("pb7")])
        def statrow_ev(E):
            E.tensor_copy(aA("ropes", p0=0, p1=1, c0=0, c1=RNG), bank(6, p1=1, c0=0, c1=RNG))
            return E.tensor_copy(aA("ropes", p0=0, p1=1, c0=RNG, c1=2 * RNG),
                                 bank(7, p1=1, c0=0, c1=RNG))
        step("vector", statrow_ev, reads=[res("pb6"), res("pb7"), res("scr0"), res("scr1")],
             writes=[res("scr0"), res("scr1")])
        if dbg_stat and r == 0:
            step("gpsimd", lambda E: E.dma_start(out=d_dbg, in_=aA("ropes", p0=0, p1=1, c0=0, c1=1024)),
                 reads=[res("scr0"), res("scr1")], writes=[res("dbgx")], dma="dbgo")
        def bc_mm(E):
            E.matmul(bank(6, c0=0, c1=RNG), ones_row, aA("ropes", p0=0, p1=1, c0=0, c1=RNG),
                     start=True, stop=True)
            return E.matmul(bank(7, c0=0, c1=RNG), ones_row,
                            aA("ropes", p0=0, p1=1, c0=RNG, c1=2 * RNG),
                            start=True, stop=True)
        step("tensor", bc_mm, reads=[res("scr0"), res("scr1"), res("consts"),
                                     res("pb6"), res("pb7")],
             writes=[res("pb6"), res("pb7")])
        def bc_ev(E):
            E.copy(aA("bmu"), bank(6, c0=0, c1=RNG))
            return E.copy(aA("binv"), bank(7, c0=0, c1=RNG))
        step("scalar", bc_ev, reads=[res("pb6"), res("pb7"), res("bmu"), res("binv")],
             writes=[res("bmu"), res("binv")])
        for ch in range(NCH):
            sp = ch % 2
            def xns(E, ch=ch, sp=sp):
                s_ = aA("ropes", c0=RNG * sp, c1=RNG * (sp + 1))
                E.tensor_tensor(s_, aA("xfm", c0=RNG * ch, c1=RNG * (ch + 1)),
                                aA("bmu"), ALU.subtract)
                E.drain()
                E.tensor_tensor(s_, s_, aA("binv"), ALU.mult)
                E.drain()
                E.tensor_copy(aAbf("xn", RNG * ch, RNG * (ch + 1)), s_)
                E.drain()
                return E.tensor_tensor(aAbf("xn", NCH * RNG + RNG * ch,
                                            NCH * RNG + RNG * (ch + 1)),
                                       s_, aAbf("xn", RNG * ch, RNG * (ch + 1)), ALU.subtract)
            step("vector", xns, reads=[res("xfm"), res("bmu"), res("binv"), res(f"scr{sp}"),
                                       res(f"xnp{ch}")],
                 writes=[res(f"scr{sp}"), res(f"xnp{ch}")])
        if dbg_stat and r == 1:
            def dbgkr(E):
                at = A_["kring"][0]
                return [E.dma_start(out=d_dbg3[:, 0:256], in_=arena[:, at + 256: at + 512]),
                        E.dma_start(out=d_dbg3[:, 256:512], in_=arena[:, at + 1024: at + 1280]),
                        E.dma_start(out=d_dbg3[:, 512:768], in_=arena[:, at + 1536 + 256: at + 1536 + 512]),
                        E.dma_start(out=d_dbg3[:, 768:1024], in_=arena[:, at + 1536 + 1024: at + 1536 + 1280])]
            step("gpsimd", dbgkr, reads=[res("ks1")], writes=[res("dbgx3")], dma="dbgo3", n_dma=4)
            def dbgvr(E):
                at = A_["vring"][0]
                return [E.dma_start(out=d_dbg5[:, 0:512], in_=arena[:, at + 512: at + 1024]),
                        E.dma_start(out=d_dbg5[:, 512:1024], in_=arena[:, at + 1536 + 512: at + 1536 + 1024])]
            step("gpsimd", dbgvr, reads=[res("vs1")], writes=[res("dbgx5")], dma="dbgo5", n_dma=2)
        if dbg_stat and r == 0:
            def dbgxn(E):
                return E.dma_start(out=d_dbg2, in_=aA("xn", c0=0, c1=2048))
            step("gpsimd", dbgxn, reads=[res(f"xnp{c_}") for c_ in range(NCH)],
                 writes=[res("dbgx2")], dma="dbgo2")
        def xnhi(ch):
            return aAbf("xn", RNG * ch, RNG * (ch + 1))
        def xnlo(ch):
            return aAbf("xn", NCH * RNG + RNG * ch, NCH * RNG + RNG * (ch + 1))

        for ch in range(NCH):
            def kvch(E, ch=ch):
                last = None
                for kc in range(2):
                    whi = wkb(0, ch, 128 * kc, 128 * (kc + 1))
                    wlo = wkb(1, ch, 128 * kc, 128 * (kc + 1))
                    plan = [(whi, xnhi(ch)), (whi, xnlo(ch)), (wlo, xnhi(ch))]
                    for pi, (lhs, rhs) in enumerate(plan):
                        last = E.matmul(bank(2 + kc, c0=0, c1=RNG), lhs, rhs,
                                        start=(ch == 0 and pi == 0),
                                        stop=(ch == NCH - 1 and pi == 2))
                for j in range(4):
                    b = 4 + j
                    c0 = 0
                    xh = aAbf("xn", RNG * ch + 128 * j, RNG * ch + 128 * (j + 1))
                    xl = aAbf("xn", NCH * RNG + RNG * ch + 128 * j,
                              NCH * RNG + RNG * ch + 128 * (j + 1))
                    vh, vl = wvb(0, ch), wvb(1, ch)
                    plan = [(xh, vh), (xh, vl), (xl, vh)]
                    for pi, (lhs, rhs) in enumerate(plan):
                        last = E.matmul(bank(b, c0=c0, c1=c0 + HD), lhs, rhs,
                                        start=(ch == 0 and pi == 0),
                                        stop=(ch == NCH - 1 and pi == 2))
                return last
            step("tensor", kvch,
                 reads=[res(f"xnp{ch}"), res("w")] + [res(f"pb{b}") for b in range(2, 8)],
                 writes=([res(f"pb{b}") for b in range(2, 8)] if ch == NCH - 1 else []))
        def ksq(E):
            E.activation(aA("q2", c0=0, c1=RNG), bank(2, c0=0, c1=RNG), AF.Square)
            return E.activation(aA("q2", c0=RNG, c1=2 * RNG), bank(3, c0=0, c1=RNG), AF.Square)
        step("scalar", ksq, reads=[res("pb2"), res("pb3"), res("q2")], writes=[res("q2")])
        def k_ms(E):
            E.matmul(bank(0, p1=1, c0=0, c1=RNG), ones_col, aA("q2", c0=0, c1=RNG),
                     start=True, stop=False)
            return E.matmul(bank(0, p1=1, c0=0, c1=RNG), ones_col, aA("q2", c0=RNG, c1=2 * RNG),
                            start=False, stop=True)
        step("tensor", k_ms, reads=[res("q2"), res("consts"), res("pb0")], writes=[res("pb0")])
        def k_inv1(E):
            kr = aA("q2", p0=0, p1=1, c0=0, c1=RNG)
            E.tensor_scalar_add(kr, bank(0, p1=1, c0=0, c1=RNG), EPS_RMS_SCALED)
            E.drain()
            return E.reciprocal(kr, kr)
        step("vector", k_inv1, reads=[res("pb0"), res("q2")], writes=[res("q2")])
        def k_inv2(E):
            kr = aA("q2", p0=0, p1=1, c0=0, c1=RNG)
            E.activation(kr, kr, AF.Sqrt)
            E.drain()
            return E.activation(kr, kr, AF.Copy, scale=16.0)
        step("scalar", k_inv2, reads=[res("q2")], writes=[res("q2")])
        def k_bc(E):
            return E.matmul(bank(0, c0=0, c1=RNG), ones_row,
                            aA("q2", p0=0, p1=1, c0=0, c1=RNG), start=True, stop=True)
        step("tensor", k_bc, reads=[res("q2"), res("consts"), res("pb0")], writes=[res("pb0")])
        step("vector", lambda E: E.tensor_copy(aA("binvk"), bank(0, c0=0, c1=RNG)),
             reads=[res("pb0"), res("binvk")], writes=[res("binvk")])
        def k_rope(E, slot=slot):
            t1 = aA("ropes", c0=0, c1=RNG)
            t2 = aA("ropes", c0=RNG, c1=2 * RNG)
            kef = aA("q2", c0=0, c1=RNG)
            kof = aA("q2", c0=RNG, c1=2 * RNG)
            E.scalar_tensor_tensor(t1, bank(2, c0=0, c1=RNG), gam(2), cosl(0, RNG), ALU.mult, ALU.mult)
            E.scalar_tensor_tensor(t2, bank(3, c0=0, c1=RNG), gam(3), sinl(0, RNG), ALU.mult, ALU.mult)
            E.drain()
            E.tensor_tensor(kef, t1, t2, ALU.subtract)
            E.drain()
            E.scalar_tensor_tensor(t1, bank(2, c0=0, c1=RNG), gam(2), sinl(0, RNG), ALU.mult, ALU.mult)
            E.scalar_tensor_tensor(t2, bank(3, c0=0, c1=RNG), gam(3), cosl(0, RNG), ALU.mult, ALU.mult)
            E.drain()
            E.tensor_tensor(kof, t1, t2, ALU.add)
            E.drain()
            E.tensor_tensor(kef, kef, aA("binvk"), ALU.mult)
            E.tensor_tensor(kof, kof, aA("binvk"), ALU.mult)
            E.drain()
            E.tensor_copy(aAbf("kring", (0 * 3 + slot) * RNG, (0 * 3 + slot + 1) * RNG), kef)
            return E.tensor_copy(aAbf("kring", (1 * 3 + slot) * RNG, (1 * 3 + slot + 1) * RNG), kof)
        step("vector", k_rope,
             reads=[res("pb2"), res("pb3"), res("consts"), res("tabs"), res("binvk"),
                    res("scr0"), res("scr1"), res("q2"), res(f"ks{slot}")],
             writes=[res(f"ks{slot}"), res("scr0"), res("scr1"), res("q2")])
        def k_lo(E, slot=slot):
            E.tensor_tensor(aAbf("kring", 3072 + (0 * 3 + slot) * RNG,
                                 3072 + (0 * 3 + slot + 1) * RNG),
                            aA("q2", c0=0, c1=RNG),
                            aAbf("kring", (0 * 3 + slot) * RNG, (0 * 3 + slot + 1) * RNG),
                            ALU.subtract)
            return E.tensor_tensor(aAbf("kring", 3072 + (1 * 3 + slot) * RNG,
                                        3072 + (1 * 3 + slot + 1) * RNG),
                                   aA("q2", c0=RNG, c1=2 * RNG),
                                   aAbf("kring", (1 * 3 + slot) * RNG, (1 * 3 + slot + 1) * RNG),
                                   ALU.subtract)
        step("vector", k_lo, reads=[res("q2"), res(f"ks{slot}")], writes=[res(f"ks{slot}")])

        def v_hi(E, slot=slot):
            last = None
            for j in range(4):
                last = E.tensor_copy(aAbf("vring", (4 * slot + j) * HD, (4 * slot + j + 1) * HD),
                                     bank(4 + j, c0=0, c1=HD))
            return last
        step("vector", v_hi, reads=[res("pb4"), res("pb5"), res("pb6"), res("pb7"),
                                    res(f"vs{slot}")],
             writes=[res(f"vs{slot}")])
        def v_lo(E, slot=slot):
            last = None
            for j in range(4):
                last = E.tensor_tensor(
                    aAbf("vring", 3072 + (4 * slot + j) * HD, 3072 + (4 * slot + j + 1) * HD),
                    bank(4 + j, c0=0, c1=HD),
                    aAbf("vring", (4 * slot + j) * HD, (4 * slot + j + 1) * HD),
                    ALU.subtract)
            return last
        step("vector", v_lo, reads=[res("pb4"), res("pb5"), res("pb6"), res("pb7"),
                                    res(f"vs{slot}")],
             writes=[res(f"vs{slot}")])

        if r < 0 or dbg_noq:
            continue

        for hp in range(NH):
            qb = 4 if hp % 2 == 0 else 2
            def qmm(E, hp=hp, qb=qb):
                last, first = None, [True, True]
                for kc in range(2):
                    pc = 2 * hp + kc
                    for ch in range(NCH):
                        whi = wqb(0, ch, 128 * pc, 128 * (pc + 1))
                        wlo = wqb(1, ch, 128 * pc, 128 * (pc + 1))
                        for lhs, rhs in ((whi, xnhi(ch)), (whi, xnlo(ch)), (wlo, xnhi(ch))):
                            last = E.matmul(bank(qb + kc, c0=0, c1=RNG), lhs, rhs,
                                            start=first[kc],
                                            stop=(ch == NCH - 1 and lhs is wlo))
                            first[kc] = False
                return last
            step("tensor", qmm, reads=[res("w"), res(f"pb{qb}"), res(f"pb{qb+1}")] +
                                [res(f"xnp{c_}") for c_ in range(NCH)],
                 writes=[res(f"pb{qb}"), res(f"pb{qb+1}")])
            def qsq(E, qb=qb):
                E.activation(aA("q2", c0=0, c1=RNG), bank(qb, c0=0, c1=RNG), AF.Square)
                return E.activation(aA("q2", c0=RNG, c1=2 * RNG), bank(qb + 1, c0=0, c1=RNG), AF.Square)
            step("scalar", qsq, reads=[res(f"pb{qb}"), res(f"pb{qb+1}"), res("q2")], writes=[res("q2")])
            def q_ms(E, hp=hp):
                last = None
                for j in range(4):
                    col = 4 * hp + j
                    E.matmul(bank(6, c0=col, c1=col + 1),
                             aA("q2", c0=128 * j, c1=128 * (j + 1)),
                             ones_col, start=True, stop=False)
                    last = E.matmul(bank(6, c0=col, c1=col + 1),
                                    aA("q2", c0=RNG + 128 * j, c1=RNG + 128 * (j + 1)),
                                    ones_col, start=False, stop=True)
                return last
            step("tensor", q_ms, reads=[res("q2"), res("consts"), res("pb6")],
                 writes=[res("pb6")])
            def q_rope(E, hp=hp, qb=qb):
                t1 = aA("ropes", c0=0, c1=RNG)
                t2 = aA("ropes", c0=RNG, c1=2 * RNG)
                qef = aA("q2", c0=0, c1=RNG)
                qof = aA("q2", c0=RNG, c1=2 * RNG)
                E.scalar_tensor_tensor(t1, bank(qb, c0=0, c1=RNG), gam(0), cosl(0, RNG), ALU.mult, ALU.mult)
                E.scalar_tensor_tensor(t2, bank(qb + 1, c0=0, c1=RNG), gam(1), sinl(0, RNG), ALU.mult, ALU.mult)
                E.drain()
                E.tensor_tensor(qef, t1, t2, ALU.subtract)
                E.drain()
                E.scalar_tensor_tensor(t1, bank(qb, c0=0, c1=RNG), gam(0), sinl(0, RNG), ALU.mult, ALU.mult)
                E.scalar_tensor_tensor(t2, bank(qb + 1, c0=0, c1=RNG), gam(1), cosl(0, RNG), ALU.mult, ALU.mult)
                E.drain()
                E.tensor_tensor(qof, t1, t2, ALU.add)
                E.drain()
                E.tensor_copy(aAbf("qfm", RNG * (2 * hp), RNG * (2 * hp + 1)), qef)
                return E.tensor_copy(aAbf("qfm", RNG * (2 * hp + 1), RNG * (2 * hp + 2)), qof)
            step("vector", q_rope,
                 reads=[res(f"pb{qb}"), res(f"pb{qb+1}"), res("consts"), res("tabs"), res("scr0"),
                        res("scr1"), res("q2"), res("qfm")],
                 writes=[res("qfm"), res("scr0"), res("scr1"), res("q2")])
            def q_lo(E, hp=hp):
                E.tensor_tensor(aAbf("qfm", NCH * RNG + RNG * (2 * hp),
                                     NCH * RNG + RNG * (2 * hp + 1)),
                                aA("q2", c0=0, c1=RNG),
                                aAbf("qfm", RNG * (2 * hp), RNG * (2 * hp + 1)), ALU.subtract)
                return E.tensor_tensor(aAbf("qfm", NCH * RNG + RNG * (2 * hp + 1),
                                            NCH * RNG + RNG * (2 * hp + 2)),
                                       aA("q2", c0=RNG, c1=2 * RNG),
                                       aAbf("qfm", RNG * (2 * hp + 1), RNG * (2 * hp + 2)),
                                       ALU.subtract)
            step("vector", q_lo, reads=[res("q2"), res("qfm")], writes=[res("qfm")])
        def invq1(E):
            iq = invq_ap(0, 16)
            E.tensor_scalar_add(iq, bank(6, c0=0, c1=16), EPS_RMS_SCALED)
            E.drain()
            return E.reciprocal(iq, iq)
        step("vector", invq1, reads=[res("pb6"), res("invq")], writes=[res("invq")])
        step("scalar", lambda E: E.activation(invq_ap(0, 16), invq_ap(0, 16), AF.Sqrt),
             reads=[res("invq")], writes=[res("invq")])

        def qhib(h, kc, qt):
            return aAbf("qfm", RNG * (2 * h + kc) + 128 * qt, RNG * (2 * h + kc) + 128 * (qt + 1))
        def qlob(h, kc, qt):
            b0 = NCH * RNG
            return aAbf("qfm", b0 + RNG * (2 * h + kc) + 128 * qt,
                        b0 + RNG * (2 * h + kc) + 128 * (qt + 1))
        def khib(kc, sl, col):
            return aAbf("kring", (kc * 3 + sl) * RNG + col, (kc * 3 + sl) * RNG + col + 128)
        def klob(kc, sl, col):
            return aAbf("kring", 3072 + (kc * 3 + sl) * RNG + col,
                        3072 + (kc * 3 + sl) * RNG + col + 128)
        def vhib(sl, sub, dc):
            return aAbf("vring", (4 * sl + sub) * HD + 128 * dc,
                        (4 * sl + sub) * HD + 128 * (dc + 1))
        def vlob(sl, sub, dc):
            return aAbf("vring", 3072 + (4 * sl + sub) * HD + 128 * dc,
                        3072 + (4 * sl + sub) * HD + 128 * (dc + 1))

        def emit_smm(i, h, qt, t0g):
            p = i % 2
            bA, bB = 2 * p, 2 * p + 1
            def smm(E, h=h, qt=qt, t0g=t0g, r=r, bA=bA, bB=bB):
                prev_sl = ((t0g - 512) // RNG + 1) % 3
                cur_sl = ((t0g // RNG) + 1) % 3
                w1_ = 512 - 128 * qt
                runs = [(prev_sl, 128 * qt, 0, w1_)]
                if qt > 0:
                    runs.append((cur_sl, 0, w1_, 128 * qt))
                last = None
                first = True
                for kc in range(2):
                    for (sl, scol, dcol, w) in runs:
                        for lhs, rhsf in (((0, kc), (0, sl, scol, w)),
                                          ((0, kc), (1, sl, scol, w)),
                                          ((1, kc), (0, sl, scol, w))):
                            qh = qhib(h, kc, qt) if lhs[0] == 0 else qlob(h, kc, qt)
                            hl, _sl, _sc, _w = rhsf
                            if hl == 0:
                                kr_ = aAbf("kring", (kc * 3 + _sl) * RNG + _sc,
                                           (kc * 3 + _sl) * RNG + _sc + _w)
                            else:
                                kr_ = aAbf("kring", 3072 + (kc * 3 + _sl) * RNG + _sc,
                                           3072 + (kc * 3 + _sl) * RNG + _sc + _w)
                            if lhs[0] == 1 and hl == 1:
                                continue
                            last = E.matmul(bank(bA, c0=dcol, c1=dcol + w), qh, kr_,
                                            start=first, stop=False)
                            first = False
                ops = [("maskL",)]
                if r == 0:
                    for kb in range(4):
                        if qt + kb < 4:
                            ops.append(("kill", kb))
                for i_, op in enumerate(ops):
                    sp = (i_ == len(ops) - 1)
                    if op[0] == "maskL":
                        last = E.matmul(bank(bA, c0=0, c1=128), strictLb, negIb,
                                        start=False, stop=sp)
                    else:
                        last = E.matmul(bank(bA, c0=128 * op[1], c1=128 * (op[1] + 1)),
                                        ones_row, kbias, start=False, stop=sp)
                col = t0g - RNG * (t0g // RNG)
                firstB = True
                for kc in range(2):
                    for hl_q, hl_k in ((0, 0), (0, 1), (1, 0)):
                        qh = qhib(h, kc, qt) if hl_q == 0 else qlob(h, kc, qt)
                        base = 0 if hl_k == 0 else 3072
                        kr_ = aAbf("kring", base + (kc * 3 + cur_sl) * RNG + col,
                                   base + (kc * 3 + cur_sl) * RNG + col + 128)
                        E.matmul(bank(bB, c0=0, c1=128), qh, kr_, start=firstB, stop=False)
                        firstB = False
                return E.matmul(bank(bB, c0=0, c1=128), strictUb, negIb, start=False, stop=True)
            kslots = set()
            for kb in range(5):
                tk = t0g - 512 + 128 * kb
                kslots.add(((tk // RNG) + 1) % 3)
            step("tensor", smm,
                 reads=[res("qfm"), res("consts"), res(f"pb{bA}"), res(f"pb{bB}")] +
                       [res(f"ks{s}") for s in sorted(kslots)],
                 writes=[res(f"pb{bA}"), res(f"pb{bB}")])
            return kslots

        def emit_exp(i, h, qt):
            p = i % 2
            def expP(E, h=h, qt=qt, p=p):
                return E.activation(aA("pp"), psum[:, 1024 * p:1024 * p + 640], AF.Exp,
                                    scale=invq_ap(4 * h + qt, 4 * h + qt + 1),
                                    accum_out=lcol)
            step("scalar", expP, reads=[res(f"pb{2*p}"), res(f"pb{2*p+1}"), res("invq"),
                                        res("pp"), res("lc")],
                 writes=[res("pp"), res("lc")])

        def emit_psplit(i):
            def lr_(E):
                return E.reciprocal(lrec, lcol)
            step("vector", lr_, reads=[res("lc"), res("lr")], writes=[res("lr")])
            def phi(E):
                return E.activation(aAbf("phl", 0, 640), aA("pp"), AF.Copy, scale=lrec)
            step("scalar", phi, reads=[res("pp"), res("lr"), res("phl")], writes=[res("phl")])
            def plo(E):
                return E.scalar_tensor_tensor(aAbf("phl", 640, 1280), aA("pp"), lrec,
                                              aAbf("phl", 0, 640), ALU.mult, ALU.subtract)
            step("vector", plo, reads=[res("pp"), res("lr"), res("phl")], writes=[res("phl")])

        def ptb_at(i, kb, lo):
            at, _ = A_["pt"]
            base = at + 640 * (i % 2)
            off = (64 * kb) if not lo else (320 + 64 * kb)
            return arena[:, base + off: base + off + 64].bitcast(BF16)

        def emit_ptr(i):
            p = i % 2
            pb_pt = 4 + p
            def ptrA(E, pb_pt=pb_pt):
                last = None
                for kb in range(4):
                    last = E.transpose(psbf(pb_pt, 256 * kb, 256 * kb + 128),
                                       aAbf("phl", 128 * kb, 128 * (kb + 1)), identb)
                    last = E.transpose(psbf(pb_pt, 256 * kb + 128, 256 * kb + 256),
                                       aAbf("phl", 640 + 128 * kb, 640 + 128 * (kb + 1)), identb)
                return last
            step("tensor", ptrA, reads=[res("phl"), res("consts"), res(f"pb{pb_pt}")],
                 writes=[res(f"pb{pb_pt}")])
            def ptevA(E, pb_pt=pb_pt, i=i):
                at, _ = A_["pt"]
                base = at + 640 * (i % 2)
                hi = arena[:, base: base + 256].bitcast(BF16).rearrange("p (a c) -> p a c", c=128)
                lo = arena[:, base + 320: base + 576].bitcast(BF16).rearrange("p (a c) -> p a c", c=128)
                src3 = psbf(pb_pt, 0, 1024).rearrange("p (a c) -> p a c", c=128)
                E.tensor_copy(hi, src3[:, 0::2, :])
                return E.tensor_copy(lo, src3[:, 1::2, :])
            step("vector", ptevA, reads=[res(f"pb{pb_pt}"), res(f"pt{i%2}")],
                 writes=[res(f"pt{i%2}")])
            def ptrB(E, pb_pt=pb_pt):
                E.transpose(psbf(pb_pt, 0, 128), aAbf("phl", 512, 640), identb)
                return E.transpose(psbf(pb_pt, 128, 256), aAbf("phl", 1152, 1280), identb)
            step("tensor", ptrB, reads=[res("phl"), res("consts"), res(f"pb{pb_pt}"),
                                        res(f"pt{i%2}")],
                 writes=[res(f"pb{pb_pt}")])
            def ptevB(E, pb_pt=pb_pt, i=i):
                at, _ = A_["pt"]
                base = at + 640 * (i % 2)
                E.tensor_copy(arena[:, base + 256: base + 320].bitcast(BF16), psbf(pb_pt, 0, 128))
                return E.tensor_copy(arena[:, base + 576: base + 640].bitcast(BF16),
                                     psbf(pb_pt, 128, 256))
            step("vector", ptevB, reads=[res(f"pb{pb_pt}"), res(f"pt{i%2}")],
                 writes=[res(f"pt{i%2}")])

        def emit_av(i, h, qt, t0g, kslots):
            def avmm2(E, t0g=t0g, i=i):
                last = None
                for dc in range(2):
                    ob, oc0 = (7, 0) if dc == 0 else (6, 256)
                    plan = []
                    for kb in range(5):
                        tk = t0g - 512 + 128 * kb
                        rr = tk // RNG
                        sl = (rr + 1) % 3
                        col = tk - RNG * rr
                        sub = col // 128
                        plan.append((vhib(sl, sub, dc), ptb_at(i, kb, False)))
                        plan.append((vhib(sl, sub, dc), ptb_at(i, kb, True)))
                        plan.append((vlob(sl, sub, dc), ptb_at(i, kb, False)))
                    for i_, (lhs, rhs) in enumerate(plan):
                        last = E.matmul(bank(ob, c0=oc0, c1=oc0 + 128), lhs, rhs,
                                        start=(i_ == 0), stop=(i_ == len(plan) - 1))
                return last
            step("tensor", avmm2,
                 reads=[res(f"pt{i%2}"), res("pb6"), res("pb7")] +
                       [res(f"vs{s}") for s in sorted(kslots)],
                 writes=[res("pb6"), res("pb7")])
            def oevh(E, h=h, qt=qt):
                E.copy(aAbf("orng", RNG * (2 * h) + 128 * qt,
                            RNG * (2 * h) + 128 * (qt + 1)),
                       bank(7, c0=0, c1=128))
                return E.copy(aAbf("orng", RNG * (2 * h + 1) + 128 * qt,
                                   RNG * (2 * h + 1) + 128 * (qt + 1)),
                              bank(6, c0=256, c1=384))
            step("scalar", oevh, reads=[res("pb6"), res("pb7"), res("orng")],
                 writes=[res("orng")])
            def oevl(E, h=h, qt=qt):
                b0 = NCH * RNG
                E.tensor_tensor(aAbf("orng", b0 + RNG * (2 * h) + 128 * qt,
                                     b0 + RNG * (2 * h) + 128 * (qt + 1)),
                                bank(7, c0=0, c1=128),
                                aAbf("orng", RNG * (2 * h) + 128 * qt,
                                     RNG * (2 * h) + 128 * (qt + 1)), ALU.subtract)
                return E.tensor_tensor(aAbf("orng", b0 + RNG * (2 * h + 1) + 128 * qt,
                                            b0 + RNG * (2 * h + 1) + 128 * (qt + 1)),
                                       bank(6, c0=256, c1=384),
                                       aAbf("orng", RNG * (2 * h + 1) + 128 * qt,
                                            RNG * (2 * h + 1) + 128 * (qt + 1)), ALU.subtract)
            step("vector", oevl, reads=[res("pb6"), res("pb7"), res("orng")],
                 writes=[res("orng")])

        tiles = [(h, qt) for h in range(0 if dbg_noattn else NH) for qt in range(4)]
        pend_ptr = None
        pend_av = []
        for i, (h, qt) in enumerate(tiles):
            t0g = RNG * r + 128 * qt
            kslots = emit_smm(i, h, qt, t0g)
            emit_exp(i, h, qt)
            if pend_ptr is not None:
                emit_ptr(pend_ptr[0])
                pend_av.append(pend_ptr)
            emit_psplit(i)
            while len(pend_av) > 1:
                emit_av(*pend_av.pop(0))
            pend_ptr = (i, h, qt, t0g, kslots)
        if pend_ptr is not None:
            emit_ptr(pend_ptr[0])
            pend_av.append(pend_ptr)
        while pend_av:
            emit_av(*pend_av.pop(0))

        def ohib(ch):
            return aAbf("orng", RNG * ch, RNG * (ch + 1))
        def olob(ch):
            return aAbf("orng", NCH * RNG + RNG * ch, NCH * RNG + RNG * (ch + 1))
        for oc in range(0 if dbg_nowo else NCH):
            b = oc % 2
            def womm(E, oc=oc, b=b):
                plan = []
                for ch in range(NCH):
                    whi = wob(0, ch, 128 * oc, 128 * (oc + 1))
                    wlo = wob(1, ch, 128 * oc, 128 * (oc + 1))
                    plan += [(whi, ohib(ch)), (whi, olob(ch)), (wlo, ohib(ch))]
                last = None
                for i_, (lhs, rhs) in enumerate(plan):
                    last = E.matmul(bank(b, c0=0, c1=RNG), lhs, rhs,
                                    start=(i_ == 0), stop=(i_ == len(plan) - 1))
                return last
            step("tensor", womm, reads=[res("orng"), res("w"), res(f"pb{b}")],
                 writes=[res(f"pb{b}")])
            def woev(E, oc=oc, b=b):
                return E.tensor_tensor(aA("xfm", c0=RNG * oc, c1=RNG * (oc + 1)),
                                       bank(b, c0=0, c1=RNG),
                                       aA("xfm", c0=RNG * oc, c1=RNG * (oc + 1)), ALU.add)
            step("vector", woev, reads=[res(f"pb{b}"), res("xfm")], writes=[res("xfm")])
            sp = oc % 2
            def x1sq(E, oc=oc, sp=sp):
                return E.activation(aA("ropes", c0=RNG * sp, c1=RNG * (sp + 1)),
                                    aA("xfm", c0=RNG * oc, c1=RNG * (oc + 1)), AF.Square)
            step("scalar", x1sq, reads=[res("xfm"), res(f"scr{sp}")], writes=[res(f"scr{sp}")])
            def lnmm(E, oc=oc, sp=sp):
                E.matmul(bank(2, p1=1, c0=0, c1=RNG), ones_col,
                         aA("xfm", c0=RNG * oc, c1=RNG * (oc + 1)),
                         start=(oc == 0), stop=(oc == NCH - 1))
                return E.matmul(bank(3, p1=1, c0=0, c1=RNG), ones_col,
                                aA("ropes", c0=RNG * sp, c1=RNG * (sp + 1)),
                                start=(oc == 0), stop=(oc == NCH - 1))
            step("tensor", lnmm, reads=[res("xfm"), res(f"scr{sp}"), res("consts"),
                                        res("pb2"), res("pb3")],
                 writes=[res("pb2"), res("pb3")] if oc == NCH - 1 else [])
            step("gpsimd", lambda E, oc=oc, r=r:
                 E.dma_start(out=d_x1[128 * oc:128 * (oc + 1), RNG * r:RNG * (r + 1)],
                             in_=aA("xfm", c0=RNG * oc, c1=RNG * (oc + 1))),
                 reads=[res("xfm")], writes=[res("x1s")], dma="x1o")
        def lnrows(E):
            mu = aA("ropes", p0=0, p1=1, c0=0, c1=RNG)
            e2 = aA("ropes", p0=0, p1=1, c0=RNG, c1=2 * RNG)
            tmp = aA("q2", p0=0, p1=1, c0=0, c1=RNG)
            E.tensor_scalar_mul(mu, bank(2, p1=1, c0=0, c1=RNG), 1.0 / C)
            E.tensor_scalar_mul(e2, bank(3, p1=1, c0=0, c1=RNG), 1.0 / C)
            E.drain()
            E.tensor_tensor(tmp, mu, mu, ALU.mult)
            E.drain()
            return E.tensor_tensor(e2, e2, tmp, ALU.subtract)
        step("vector", lnrows, reads=[res("pb2"), res("pb3"), res("scr0"), res("scr1"),
                                      res("q2")],
             writes=[res("scr0"), res("scr1"), res("q2")])
        def lnrows2(E):
            e2 = aA("ropes", p0=0, p1=1, c0=RNG, c1=2 * RNG)
            E.activation(e2, e2, AF.Ln, bias=epsln[0:1, :])
            E.drain()
            return E.activation(e2, e2, AF.Exp, scale=-0.5)
        step("scalar", lnrows2, reads=[res("scr1"), res("consts")], writes=[res("scr1")])
        step("gpsimd", lambda E, r=r: [
             E.dma_start(out=d_mu[0:1, RNG * r:RNG * (r + 1)],
                         in_=aA("ropes", p0=0, p1=1, c0=0, c1=RNG)),
             E.dma_start(out=d_mu[1:2, RNG * r:RNG * (r + 1)],
                         in_=aA("ropes", p0=0, p1=1, c0=RNG, c1=2 * RNG))],
             reads=[res("scr0"), res("scr1")], writes=[res("mus")], dma="muo", n_dma=2)

    if dbg_nob:
        bld.finalize()
        return nc
    barrier()
    for ch in range(NCH):
        stg = "x1slab" if ch % 2 == 0 else "xn2"
        step("sync", lambda E, ch=ch, stg=stg: E.dma_start(out=aB(stg),
                                                           in_=d_w1[128 * ch:128 * (ch + 1), :]),
             reads=[res(stg)], writes=[res(stg)], dma=f"wld{ch % 2}")
        def w1split(E, ch=ch, stg=stg):
            E.tensor_copy(w1b(0, ch, 0, FF), aB(stg))
            E.drain()
            return E.tensor_tensor(w1b(1, ch, 0, FF), aB(stg), w1b(0, ch, 0, FF),
                                   ALU.subtract)
        step("vector", w1split, reads=[res(stg), res("w")], writes=[res("w")])

    for s in range(NRANGE):
        def ldx1(E, s=s):
            return [E.dma_start(out=aB("x1slab", c0=RNG * ch, c1=RNG * (ch + 1)),
                                in_=d_x1[128 * ch:128 * (ch + 1), RNG * s:RNG * (s + 1)])
                    for ch in range(NCH)]
        step("sync", ldx1, reads=[res("x1s"), res("x1slab")], writes=[res("x1slab")],
             dma="x1i", n_dma=NCH)
        def ldmu(E, s=s):
            return [E.dma_start(out=aB("brows", p0=0, p1=1, c0=0, c1=RNG),
                                in_=d_mu[0:1, RNG * s:RNG * (s + 1)]),
                    E.dma_start(out=aB("brows", p0=0, p1=1, c0=RNG, c1=2 * RNG),
                                in_=d_mu[1:2, RNG * s:RNG * (s + 1)])]
        step("sync", ldmu, reads=[res("mus"), res("brows")], writes=[res("brows")],
             dma="mui", n_dma=2)
        def bc2(E):
            E.matmul(bank(6, c0=0, c1=RNG), ones_row, aB("brows", p0=0, p1=1, c0=0, c1=RNG),
                     start=True, stop=True)
            return E.matmul(bank(7, c0=0, c1=RNG), ones_row,
                            aB("brows", p0=0, p1=1, c0=RNG, c1=2 * RNG),
                            start=True, stop=True)
        step("tensor", bc2, reads=[res("brows"), res("consts"), res("pb6"), res("pb7")],
             writes=[res("pb6"), res("pb7")])
        def bc2ev(E):
            E.tensor_copy(aB("bbc", c0=0, c1=RNG), bank(6, c0=0, c1=RNG))
            return E.tensor_copy(aB("bbc", c0=RNG, c1=2 * RNG), bank(7, c0=0, c1=RNG))
        step("vector", bc2ev, reads=[res("pb6"), res("pb7"), res("bbc")], writes=[res("bbc")])
        def xn2mk(E):
            last = None
            for ch in range(NCH):
                E.tensor_tensor(aB("xn2", c0=RNG * ch, c1=RNG * (ch + 1)),
                                aB("x1slab", c0=RNG * ch, c1=RNG * (ch + 1)),
                                aB("bbc", c0=0, c1=RNG), ALU.subtract)
            E.drain()
            for ch in range(NCH):
                last = E.tensor_tensor(aB("xn2", c0=RNG * ch, c1=RNG * (ch + 1)),
                                       aB("xn2", c0=RNG * ch, c1=RNG * (ch + 1)),
                                       aB("bbc", c0=RNG, c1=2 * RNG), ALU.mult)
            return last
        step("vector", xn2mk, reads=[res("x1slab"), res("bbc"), res("xn2")], writes=[res("xn2")])
        def xn2split(E):
            E.tensor_copy(aBbf("xn2p", 0, NCH * RNG), aB("xn2"))
            E.drain()
            return E.tensor_tensor(aBbf("xn2p", NCH * RNG, 2 * NCH * RNG), aB("xn2"),
                                   aBbf("xn2p", 0, NCH * RNG), ALU.subtract)
        step("vector", xn2split, reads=[res("xn2"), res("xn2p")], writes=[res("xn2p")])
        for fc in range(NFCH):
            b = fc % 4
            def hmm(E, fc=fc, b=b):
                last = None
                first = True
                for ch in range(NCH):
                    xhi = aBbf("xn2p", RNG * ch, RNG * ch + RNG)
                    xlo = aBbf("xn2p", NCH * RNG + RNG * ch, NCH * RNG + RNG * (ch + 1))
                    whi = w1b(0, ch, 128 * fc, 128 * (fc + 1))
                    wlo = w1b(1, ch, 128 * fc, 128 * (fc + 1))
                    for lhs, rhs in ((whi, xhi), (whi, xlo), (wlo, xhi)):
                        last = E.matmul(bank(b, c0=0, c1=RNG), lhs, rhs,
                                        start=first,
                                        stop=(ch == NCH - 1 and rhs is xhi and lhs is wlo))
                        first = False
                return last
            step("tensor", hmm, reads=[res("xn2p"), res("w"), res(f"pb{b}")],
                 writes=[res(f"pb{b}")])
            hb = fc % 2
            def hgelu(E, fc=fc, b=b, hb=hb):
                return E.activation(aB("hstg", c0=512 * hb, c1=512 * hb + RNG),
                                    bank(b, c0=0, c1=RNG), AF.Gelu_apprx_tanh, bias=b1ap(fc))
            step("scalar", hgelu, reads=[res(f"pb{b}"), res("consts"), res(f"hst{hb}")],
                 writes=[res(f"hst{hb}")])
            def hhicast(E, hb=hb):
                return E.copy(aBbf("hhi", 512 * hb, 512 * hb + RNG),
                              aB("hstg", c0=512 * hb, c1=512 * hb + RNG))
            step("scalar", hhicast, reads=[res(f"hst{hb}"), res(f"hhi{hb}")],
                 writes=[res(f"hhi{hb}")])
            def hlocomp(E, hb=hb):
                return E.tensor_tensor(aBbf("hlo", 512 * hb, 512 * hb + RNG),
                                       aB("hstg", c0=512 * hb, c1=512 * hb + RNG),
                                       aBbf("hhi", 512 * hb, 512 * hb + RNG), ALU.subtract)
            step("vector", hlocomp, reads=[res(f"hst{hb}"), res(f"hhi{hb}"), res(f"hlo{hb}")],
                 writes=[res(f"hlo{hb}")])
            step("gpsimd", lambda E, fc=fc, s=s, hb=hb: [
                 E.dma_start(out=d_hhi[128 * fc:128 * (fc + 1), RNG * s:RNG * (s + 1)],
                             in_=aBbf("hhi", 512 * hb, 512 * hb + RNG)),
                 E.dma_start(out=d_hlo[128 * fc:128 * (fc + 1), RNG * s:RNG * (s + 1)],
                             in_=aBbf("hlo", 512 * hb, 512 * hb + RNG))],
                 reads=[res(f"hhi{hb}"), res(f"hlo{hb}")], writes=[res("hs")],
                 dma="hso", n_dma=2)

    if dbg_noc:
        bld.finalize()
        return nc
    barrier()
    for fc4 in range(NFCH // 4):
        stg = "cx1" if fc4 % 2 == 0 else "x2"
        step("sync", lambda E, fc4=fc4, stg=stg: [
            E.dma_start(out=aC(stg, c0=1024 * q, c1=1024 * (q + 1)),
                        in_=d_w2[128 * (4 * fc4 + q):128 * (4 * fc4 + q + 1), :])
            for q in range(4)],
             reads=[res("cx10" if stg == "cx1" else stg)],
             writes=[res("cx10" if stg == "cx1" else stg)], dma=f"wld{fc4 % 2}", n_dma=4)
        def w2split(E, fc4=fc4, stg=stg):
            for q in range(4):
                E.tensor_copy(w2b(0, 4 * fc4 + q, 0, C), aC(stg, c0=1024 * q, c1=1024 * (q + 1)))
            E.drain()
            last = None
            for q in range(4):
                last = E.tensor_tensor(w2b(1, 4 * fc4 + q, 0, C),
                                       aC(stg, c0=1024 * q, c1=1024 * (q + 1)),
                                       w2b(0, 4 * fc4 + q, 0, C), ALU.subtract)
            return last
        step("vector", w2split, reads=[res("cx10" if stg == "cx1" else stg), res("w")],
             writes=[res("w")])

    for s in range(NRANGE):
        sp_ = s % 2
        def ldx1c(E, s=s, sp_=sp_):
            return [E.dma_start(out=aC("cx1", c0=4096 * sp_ + RNG * ch,
                                       c1=4096 * sp_ + RNG * (ch + 1)),
                                in_=d_x1[128 * ch:128 * (ch + 1), RNG * s:RNG * (s + 1)])
                    for ch in range(NCH)]
        step("sync", ldx1c, reads=[res("x1s"), res(f"cx1{sp_}")], writes=[res(f"cx1{sp_}")],
             dma=f"x1i{sp_}", n_dma=NCH)
        for fch in range(NFCH):
            hbuf = fch % 4
            step("sync", lambda E, fch=fch, s=s, hbuf=hbuf: [
                 E.dma_start(out=aCbf("hbufs", 1024 * hbuf, 1024 * hbuf + RNG),
                             in_=d_hhi[128 * fch:128 * (fch + 1), RNG * s:RNG * (s + 1)]),
                 E.dma_start(out=aCbf("hbufs", 1024 * hbuf + RNG, 1024 * hbuf + 2 * RNG),
                             in_=d_hlo[128 * fch:128 * (fch + 1), RNG * s:RNG * (s + 1)])],
                 reads=[res("hs"), res(f"hb{hbuf}")], writes=[res(f"hb{hbuf}")],
                 dma=f"hsi{hbuf}", n_dma=2)
            def y2mm(E, fch=fch, hbuf=hbuf):
                hhi = aCbf("hbufs", 1024 * hbuf, 1024 * hbuf + RNG)
                hlo = aCbf("hbufs", 1024 * hbuf + RNG, 1024 * hbuf + 2 * RNG)
                last = None
                for dc in range(NCH):
                    whi = w2b(0, fch, 128 * dc, 128 * (dc + 1))
                    wlo = w2b(1, fch, 128 * dc, 128 * (dc + 1))
                    for lhs, rhs in ((whi, hhi), (whi, hlo), (wlo, hhi)):
                        last = E.matmul(bank(dc, c0=0, c1=RNG), lhs, rhs,
                                        start=(fch == 0 and lhs is whi and rhs is hhi),
                                        stop=(fch == NFCH - 1 and lhs is wlo))
                return last
            step("tensor", y2mm,
                 reads=[res(f"hb{hbuf}"), res("w")] + [res(f"pb{d}") for d in range(8)],
                 writes=[res(f"pb{d}") for d in range(8)])
        def y2ev(E, sp_=sp_):
            last = None
            for dc in range(NCH):
                last = E.scalar_tensor_tensor(aC("x2", c0=RNG * dc, c1=RNG * (dc + 1)),
                                              bank(dc, c0=0, c1=RNG), b2ap(dc),
                                              aC("cx1", c0=4096 * sp_ + RNG * dc,
                                                 c1=4096 * sp_ + RNG * (dc + 1)),
                                              ALU.add, ALU.add)
            return last
        step("vector", y2ev,
             reads=[res(f"pb{d}") for d in range(8)] + [res(f"cx1{sp_}"), res("consts"),
                                                        res("x2")],
             writes=[res("x2")])
        for j in range(4):
            b0 = 2 * (j % 2)
            def otr(E, j=j, b0=b0):
                last = None
                for dc in range(NCH):
                    bb = b0 + dc // 4
                    last = E.transpose(bank(bb, c0=128 * (dc % 4), c1=128 * (dc % 4) + 128),
                                       aC("x2", c0=RNG * dc + 128 * j,
                                          c1=RNG * dc + 128 * (j + 1)),
                                       ident)
                return last
            step("tensor", otr, reads=[res("x2"), res("consts"),
                                       res(f"pb{b0}"), res(f"pb{b0+1}")],
                 writes=[res(f"pb{b0}"), res(f"pb{b0+1}")])
            ob = j % 2
            def otev(E, b0=b0, ob=ob):
                E.tensor_copy(aC("ostg", c0=1024 * ob, c1=1024 * ob + 512),
                              bank(b0, c0=0, c1=512))
                return E.tensor_copy(aC("ostg", c0=1024 * ob + 512, c1=1024 * ob + 1024),
                                     bank(b0 + 1, c0=0, c1=512))
            step("vector", otev, reads=[res(f"pb{b0}"), res(f"pb{b0+1}"), res(f"os{ob}")],
                 writes=[res(f"os{ob}")])
            step("gpsimd", lambda E, j=j, s=s, ob=ob:
                 E.dma_start(out=d_y[RNG * s + 128 * j: RNG * s + 128 * (j + 1), :],
                             in_=aC("ostg", c0=1024 * ob, c1=1024 * ob + 1024)),
                 reads=[res(f"os{ob}")], writes=[res("y")], dma="yo")

    bld.finalize()
    return nc


def _host_stage(inputs):
    x = np.asarray(inputs["L0_attn_x"], dtype=np.float32)
    Wq = np.asarray(inputs["L0_attn_Wq"], dtype=np.float32)
    Wk = np.asarray(inputs["L0_attn_Wk"], dtype=np.float32)
    Wv = np.asarray(inputs["L0_attn_Wv"], dtype=np.float32)
    Wo = np.asarray(inputs["L0_attn_Wo"], dtype=np.float32)
    gq = np.asarray(inputs["L0_attn_q_gamma"], dtype=np.float32)
    gk = np.asarray(inputs["L0_attn_k_gamma"], dtype=np.float32)
    w1 = np.asarray(inputs["L0_ffn_w1"], dtype=np.float32)
    b1 = np.asarray(inputs["L0_ffn_b1"], dtype=np.float32)
    w2 = np.asarray(inputs["L0_ffn_w2"], dtype=np.float32)
    b2 = np.asarray(inputs["L0_ffn_b2"], dtype=np.float32)

    permq = np.concatenate([np.concatenate([np.arange(0, HD, 2), np.arange(1, HD, 2)]) + HD * h
                            for h in range(NH)])
    permk = np.concatenate([np.arange(0, HD, 2), np.arange(1, HD, 2)])
    wqT = np.ascontiguousarray(Wq[permq, :].T)
    wkT = np.ascontiguousarray(Wk[permk, :].T)
    wvT = np.ascontiguousarray(Wv.T)
    woT = np.ascontiguousarray(Wo.T)

    gams = np.ascontiguousarray(
        np.stack([gq[0::2], gq[1::2], gk[0::2], gk[1::2]], axis=1))

    import ml_dtypes
    ii = np.arange(128)
    consts = np.zeros((128, 386), np.float32)
    consts[:, 0:128] = np.eye(128, dtype=np.float32)
    consts[:, 128] = 1.0
    consts[:, 129] = EPS_LN
    def _b(a):
        return a.astype(ml_dtypes.bfloat16).view(np.uint16).view(np.float32)
    consts[:, 130:194] = _b(np.eye(128, dtype=np.float32))
    consts[:, 194:258] = _b((ii[:, None] < ii[None, :]).astype(np.float32))
    consts[:, 258:322] = _b((ii[:, None] > ii[None, :]).astype(np.float32))
    consts[:, 322:386] = _b(NEG * np.eye(128, dtype=np.float32))

    import jax
    import jax.numpy as jnp
    cpu = jax.devices("cpu")[0]
    with jax.default_device(cpu):
        inv_freq = 1.0 / (1000000.0 ** (jnp.arange(0, HD, 2, dtype=jnp.float32) / HD))
        t = jnp.arange(8192, dtype=jnp.float32)
        freqs = jnp.outer(t, inv_freq)
        cos_full = np.asarray(jnp.cos(freqs), dtype=np.float32)
        sin_full = np.asarray(jnp.sin(freqs), dtype=np.float32)

    in_maps = []
    for core in range(8):
        b = core // 2
        half = core % 2
        g0 = half * TCORE
        if half == 0:
            haloimg = np.zeros((HALO, C), np.float32)
        else:
            haloimg = x[b, g0 - HALO:g0, :]
        x_sh = np.ascontiguousarray(np.concatenate([haloimg, x[b, g0:g0 + TCORE, :]], axis=0))
        pos = np.clip(np.arange(g0 - HALO, g0 + TCORE), 0, 8191)
        cosT = np.ascontiguousarray(cos_full[pos, :].T)
        sinT = np.ascontiguousarray(sin_full[pos, :].T)
        crow = np.zeros((1, 384), np.float32)
        crow[0, 0:128] = 1.0
        crow[0, 128:256] = NEG if half == 0 else 0.0
        crow[0, 256:384] = 16.0
        in_maps.append(dict(
            x_sh=x_sh, wqT=wqT, wkT=wkT, wvT=wvT, woT=woT,
            w1=np.ascontiguousarray(w1[core]),
            b1c=np.ascontiguousarray(b1[core].reshape(NFCH, 128).T),
            w2=np.ascontiguousarray(w2[core]),
            b2c=np.ascontiguousarray(b2[core].reshape(NCH, 128).T),
            gams=gams, cosT=cosT, sinT=sinT, consts=consts, crow=crow,
        ))
    return in_maps


def kernel(**inputs):
    in_maps = _host_stage(inputs)
    nc = build_program()
    res = run_bass_kernel_spmd(nc, in_maps, core_ids=list(range(8)))
    out = np.empty((4, 8192, C), np.float32)
    for core in range(8):
        b = core // 2
        half = core % 2
        out[b, half * TCORE:(half + 1) * TCORE, :] = np.asarray(res.results[core]["y"])
    return out


# version 12
# speedup vs baseline: 1.0281x; 1.0101x over previous
import numpy as np
import concourse.bass as bass
import concourse.mybir as mybir
from concourse.bass_utils import run_bass_kernel_spmd

F32 = mybir.dt.float32
BF16 = mybir.dt.bfloat16
AF = mybir.ActivationFunctionType
ALU = mybir.AluOpType

C = 1024
HD = 256
NH = 4
TCORE = 4096
HALO = 512
RNG = 512
NRANGE = TCORE // RNG
NCH = C // 128
FF = 4096
NFCH = FF // 128
NEG = -1.0e5
EPS_LN = 1.0e-5
EPS_RMS_SCALED = 256.0 * 1.0e-6

ENGINES = ("tensor", "vector", "scalar", "gpsimd", "sync")


class Res:
    __slots__ = ("name", "w", "r")
    def __init__(self, name):
        self.name = name
        self.w = None
        self.r = []


class Builder:
    def __init__(self, nc):
        self.nc = nc
        self.prog = {e: [] for e in ENGINES}
        self.sem = {e: nc.alloc_semaphore(name="s_" + e) for e in ENGINES}
        self.cnt = {e: 0 for e in ENGINES}
        self.ic = {e: 0 for e in ENGINES}
        self.waited = {e: {} for e in ENGINES}
        self.drained = {e: 0 for e in ENGINES}
        self.R = {}

    def res(self, name):
        if name not in self.R:
            self.R[name] = Res(name)
        return self.R[name]

    def _sem(self, key):
        if key not in self.sem:
            self.sem[key] = self.nc.alloc_semaphore(name="s_" + key)
            self.cnt[key] = 0
        return self.sem[key]

    def step(self, eng, fn, reads=(), writes=(), dma=None, n_dma=1):
        deps = []
        for r in reads:
            if r.w is not None:
                deps.append(r.w)
            if r.name.startswith("pb") and eng in ("vector", "scalar"):
                other = "scalar" if eng == "vector" else "vector"
                deps.extend(t for t in r.r if t[0] == other)
        for w in writes:
            deps.extend(w.r)
            if w.w is not None:
                deps.append(w.w)
        self.ic[eng] += 1
        my_idx = self.ic[eng]
        if dma is not None:
            key = "q_" + dma
            self._sem(key)
            self.cnt[key] += 16 * n_dma
            my_cnt = self.cnt[key]
        else:
            key = eng
            self.cnt[eng] += 1
            my_cnt = self.cnt[eng]
        sem, waited = self.sem, self.waited[eng]
        xwaits, need_drain = {}, False
        for (de, dc, di, ddma) in deps:
            if de == eng and not ddma:
                if di > self.drained[eng]:
                    need_drain = True
            else:
                if dc > xwaits.get(de, 0):
                    xwaits[de] = dc
        if need_drain:
            self.drained[eng] = my_idx - 1
        do_drain = need_drain and eng in ("vector", "scalar", "gpsimd")

        def emit(E):
            for de, dc in sorted(xwaits.items()):
                if dc > waited.get(de, 0):
                    E.wait_ge(sem[de], dc)
                    waited[de] = dc
            if do_drain:
                E.drain()
            out = fn(E)
            if dma is not None:
                outs = out if isinstance(out, (list, tuple)) else [out]
                assert len(outs) == n_dma, (eng, len(outs), n_dma)
                for ins in outs:
                    ins.then_inc(sem[key], 16)
            else:
                out.then_inc(sem[key], 1)
        self.prog[eng].append(emit)
        tok = (key, my_cnt, my_idx, dma is not None)
        for w in writes:
            w.w = tok
            w.r = []
        for r in reads:
            r.r.append(tok)
        return tok

    def barrier(self):
        snap = dict(self.cnt)
        waited_all = self.waited
        sem = self.sem
        keys = sorted(snap.keys())
        for e in ENGINES:
            dr = e in ("vector", "scalar", "gpsimd")
            def f(E, e=e, dr=dr, keys=tuple(keys)):
                w = waited_all[e]
                for o in keys:
                    if o == e:
                        continue
                    v = snap[o]
                    if v > w.get(o, 0):
                        E.wait_ge(sem[o], v)
                        w[o] = v
                if dr:
                    E.drain()
            self.prog[e].append(f)
            self.drained[e] = self.ic[e]
        for r in self.R.values():
            r.w = None
            r.r = []

    def finalize(self):
        fincnt = dict(self.cnt)
        sem, waited = self.sem, self.waited["gpsimd"]
        def fin(E):
            for o in sorted(fincnt.keys()):
                if o == "gpsimd":
                    continue
                if fincnt[o] > waited.get(o, 0):
                    E.wait_ge(sem[o], fincnt[o])
        self.prog["gpsimd"].append(fin)
        with self.nc.Block() as block:
            @block.sync
            def _(E):
                for f in self.prog["sync"]:
                    f(E)
            @block.tensor
            def _(E):
                for f in self.prog["tensor"]:
                    f(E)
            @block.vector
            def _(E):
                for f in self.prog["vector"]:
                    f(E)
            @block.scalar
            def _(E):
                for f in self.prog["scalar"]:
                    f(E)
            @block.gpsimd
            def _(E):
                for f in self.prog["gpsimd"]:
                    f(E)


ACOLS = 53208
O_CONST = 0
O_CROW = 388
O_GAMS = 772
O_B1 = 776
O_B2 = 808
O_ROWS = 816
O_W = 884
O_TAB = 33648 + 12288
A0 = 21364
A_ = {}
for _n, _c in [("xtok", 2048), ("xfm", 4096), ("xn", 4096), ("ropes", 1024),
               ("kring", 3072), ("vring", 3072), ("qfm", 4096), ("q2", 1024),
               ("bmu", 512), ("binv", 512), ("binvk", 512), ("pp", 640), ("pt", 640),
               ("orng", 4096)]:
    A_[_n] = A0 + sum(v for k, v in A_.items()) if False else 0
def _layout(base, items):
    d, cur = {}, base
    for n, c in items:
        d[n] = (cur, c)
        cur += c
    return d, cur
A_, A_END = _layout(A0, [("xtok", 2048), ("xfm", 4096), ("xn", 4096), ("ropes", 1024),
                         ("kring", 3072), ("vring", 3072), ("qfm", 4096), ("q2", 1024),
                         ("bmu", 512), ("binv", 512), ("binvk", 512), ("pp", 640),
                         ("phl", 640), ("pt", 1280), ("orng", 4096)])
assert A_END <= ACOLS, A_END
B0_ = 33652
B_, B_END = _layout(B0_, [("x1slab", 4096), ("xn2", 4096), ("xn2p", 4096),
                          ("brows", 1024), ("bbc", 1024), ("hstg", 1024),
                          ("hhi", 512), ("hlo", 512)])
assert B_END <= ACOLS, B_END
C_, C_END = _layout(B0_, [("hbufs", 2048), ("cx1", 8192), ("x2", 4096), ("ostg", 2048)])
assert C_END <= ACOLS, C_END
O_TAB = A_END
assert O_TAB + 2 * RNG <= ACOLS, O_TAB


def build_program():
    import os
    dbg = os.environ.get("BASSDBG", "")
    dbg_nrange = NRANGE
    dbg_nob = "nob" in dbg
    dbg_noc = "noc" in dbg
    dbg_noq = "noq" in dbg
    dbg_noattn = "noattn" in dbg
    dbg_nowo = "nowo" in dbg
    dbg_noqms = "noqms" in dbg
    dbg_noqrope = "noqrope" in dbg
    dbg_noinvq = "noinvq" in dbg
    dbg_qms1 = "qms1" in dbg
    for tokn in dbg.split(","):
        if tokn.startswith("r="):
            dbg_nrange = int(tokn[2:])
    nc = bass.Bass("TRN2", target_bir_lowering=False)

    d_x = nc.dram_tensor("x_sh", [HALO + TCORE, C], F32, kind="ExternalInput").ap()
    d_wq = nc.dram_tensor("wqT", [C, C], F32, kind="ExternalInput").ap()
    d_wk = nc.dram_tensor("wkT", [C, HD], F32, kind="ExternalInput").ap()
    d_wv = nc.dram_tensor("wvT", [C, HD], F32, kind="ExternalInput").ap()
    d_wo = nc.dram_tensor("woT", [C, C], F32, kind="ExternalInput").ap()
    d_w1 = nc.dram_tensor("w1", [C, FF], F32, kind="ExternalInput").ap()
    d_w2 = nc.dram_tensor("w2", [FF, C], F32, kind="ExternalInput").ap()
    d_b1 = nc.dram_tensor("b1c", [128, NFCH], F32, kind="ExternalInput").ap()
    d_b2 = nc.dram_tensor("b2c", [128, NCH], F32, kind="ExternalInput").ap()
    d_g = nc.dram_tensor("gams", [128, 4], F32, kind="ExternalInput").ap()
    d_cos = nc.dram_tensor("cosT", [128, HALO + TCORE], F32, kind="ExternalInput").ap()
    d_sin = nc.dram_tensor("sinT", [128, HALO + TCORE], F32, kind="ExternalInput").ap()
    d_cst = nc.dram_tensor("consts", [128, 386], F32, kind="ExternalInput").ap()
    d_cr = nc.dram_tensor("crow", [1, 384], F32, kind="ExternalInput").ap()

    dbg_stat = "dbgstat" in dbg
    if dbg_stat:
        d_dbg = nc.dram_tensor("dbgrow", [1, 1024], F32, kind="ExternalOutput").ap()
        d_dbg2 = nc.dram_tensor("dbgxn", [128, 2048], F32, kind="ExternalOutput").ap()
        d_dbg3 = nc.dram_tensor("dbgkr", [128, 1024], F32, kind="ExternalOutput").ap()
        d_dbg4 = nc.dram_tensor("dbgo", [128, 4096], F32, kind="ExternalOutput").ap()
        d_dbg5 = nc.dram_tensor("dbgv", [128, 1024], F32, kind="ExternalOutput").ap()
    d_x1 = nc.dram_tensor("x1s", [C, TCORE], F32, kind="Internal").ap()
    d_hhi = nc.dram_tensor("hshi", [FF, TCORE], BF16, kind="Internal").ap()
    d_hlo = nc.dram_tensor("hslo", [FF, TCORE], BF16, kind="Internal").ap()
    d_y = nc.dram_tensor("y", [TCORE, C], F32, kind="ExternalOutput").ap()

    arena = nc.alloc_sbuf_tensor("arena", [128, ACOLS], F32)
    psum = nc.alloc_psum_tensor("psum", [128, 4096], F32)

    def bank(b, p0=0, p1=128, c0=0, c1=512):
        return psum[p0:p1, 512 * b + c0: 512 * b + c1]

    def aA(name, p0=0, p1=128, c0=0, c1=None):
        at, w = A_[name]
        return arena[p0:p1, at + c0: at + (w if c1 is None else c1)]

    def aB(name, p0=0, p1=128, c0=0, c1=None):
        at, w = B_[name]
        return arena[p0:p1, at + c0: at + (w if c1 is None else c1)]

    def aC(name, p0=0, p1=128, c0=0, c1=None):
        at, w = C_[name]
        return arena[p0:p1, at + c0: at + (w if c1 is None else c1)]

    ident = arena[:, O_CONST + 0: O_CONST + 128]
    ones_col = arena[:, O_CONST + 128: O_CONST + 129]
    epsln = arena[:, O_CONST + 129: O_CONST + 130]
    identb = arena[:, O_CONST + 130: O_CONST + 194].bitcast(BF16)
    strictLb = arena[:, O_CONST + 194: O_CONST + 258].bitcast(BF16)
    strictUb = arena[:, O_CONST + 258: O_CONST + 322].bitcast(BF16)
    negIb = arena[:, O_CONST + 322: O_CONST + 386].bitcast(BF16)
    ones_row = arena[0:1, O_CROW + 0: O_CROW + 128]
    kbias = arena[0:1, O_CROW + 128: O_CROW + 256]
    ones_row16 = arena[0:1, O_CROW + 256: O_CROW + 384]
    def gam(i): return arena[:, O_GAMS + i: O_GAMS + i + 1]
    def b1ap(fc): return arena[:, O_B1 + fc: O_B1 + fc + 1]
    def b2ap(dc): return arena[:, O_B2 + dc: O_B2 + dc + 1]
    def invq_ap(c0, c1): return arena[:, O_ROWS + 16 + c0: O_ROWS + 16 + c1]
    lcol = arena[:, O_ROWS + 32: O_ROWS + 33]
    lrec = arena[:, O_ROWS + 33: O_ROWS + 34]
    def wq_t(ch, d0, d1): return arena[:, O_W + ch * 1024 + d0: O_W + ch * 1024 + d1]
    def wk_t(ch, d0, d1): return arena[:, O_W + 8192 + ch * HD + d0: O_W + 8192 + ch * HD + d1]
    def wv_t(ch): return arena[:, O_W + 10240 + ch * HD: O_W + 10240 + (ch + 1) * HD]
    def wo_t(ch, d0, d1): return arena[:, O_W + 12288 + ch * 1024 + d0: O_W + 12288 + ch * 1024 + d1]
    def w1_t(ch, f0, f1): return arena[:, O_W + ch * FF + f0: O_W + ch * FF + f1]
    def w2_t(fch, d0, d1): return arena[:, O_W + fch * C + d0: O_W + fch * C + d1]
    def bfv(f32c0, nbf):
        return arena[:, f32c0: f32c0 + nbf // 2].bitcast(BF16)
    def w1b(hl, ch, f0, f1):
        base = O_W * 2 + hl * (NCH * FF) + ch * FF
        return bfv(0, 0) if False else arena[:, (base + f0) // 2: (base + f1) // 2].bitcast(BF16)
    def w2b(hl, fch, d0, d1):
        base = O_W * 2 + hl * (NFCH * C) + fch * C
        return arena[:, (base + d0) // 2: (base + d1) // 2].bitcast(BF16)
    def aBbf(name, c0bf, c1bf):
        at, w = B_[name]
        return arena[:, at + c0bf // 2: at + c1bf // 2].bitcast(BF16)
    def aCbf(name, c0bf, c1bf):
        at, w = C_[name]
        return arena[:, at + c0bf // 2: at + c1bf // 2].bitcast(BF16)
    def aAbf(name, c0bf, c1bf):
        at, w = A_[name]
        return arena[:, at + c0bf // 2: at + c1bf // 2].bitcast(BF16)
    def wab(base_bf, c0, c1):
        b = O_W * 2 + base_bf
        return arena[:, (b + c0) // 2: (b + c1) // 2].bitcast(BF16)
    def wqb(hl, ch, d0, d1): return wab(8192 * hl + ch * 1024, d0, d1)
    def wkb(hl, ch, d0, d1): return wab(16384 + 2048 * hl + ch * 256, d0, d1)
    def wvb(hl, ch): return wab(20480 + 2048 * hl + ch * 256, 0, 256)
    def wob(hl, ch, d0, d1): return wab(24576 + 8192 * hl + ch * 1024, d0, d1)
    def psbf(b, c0bf, c1bf):
        return psum[:, 512 * b + c0bf // 2: 512 * b + c1bf // 2].bitcast(BF16)
    def cosl(c0, c1): return arena[:, O_TAB + c0: O_TAB + c1]
    def sinl(c0, c1): return arena[:, O_TAB + RNG + c0: O_TAB + RNG + c1]

    bld = Builder(nc)
    step, res, barrier = bld.step, bld.res, bld.barrier

    step("sync", lambda E: [
        E.dma_start(out=arena[:, O_CONST:O_CONST + 386], in_=d_cst),
        E.dma_start(out=arena[0:1, O_CROW:O_CROW + 384], in_=d_cr),
        E.dma_start(out=arena[:, O_GAMS:O_GAMS + 4], in_=d_g),
        E.dma_start(out=arena[:, O_B1:O_B1 + NFCH], in_=d_b1),
        E.dma_start(out=arena[:, O_B2:O_B2 + NCH], in_=d_b2),
    ], writes=[res("consts")], dma="consts", n_dma=5)

    def wsplit_steps(tiles):
        batch, bcols = [], 0
        def flush(batch):
            n = len(batch)
            def ld(E, batch=tuple(batch)):
                out, c = [], 0
                for (dsrc, ncols, _hi, _lo) in batch:
                    out.append(E.dma_start(out=aA("xfm", c0=c, c1=c + ncols), in_=dsrc))
                    c += ncols
                return out
            step("sync", ld, reads=[res("xfm")], writes=[res("xfm")], dma="wload", n_dma=n)
            def sp(E, batch=tuple(batch)):
                c = 0
                for (dsrc, ncols, hi, lo) in batch:
                    E.tensor_copy(hi, aA("xfm", c0=c, c1=c + ncols))
                    c += ncols
                E.drain()
                c, last = 0, None
                for (dsrc, ncols, hi, lo) in batch:
                    last = E.tensor_tensor(lo, aA("xfm", c0=c, c1=c + ncols), hi, ALU.subtract)
                    c += ncols
                return last
            step("vector", sp, reads=[res("xfm"), res("w")], writes=[res("w")])
        for t in tiles:
            if bcols + t[1] > 4096:
                flush(batch)
                batch, bcols = [], 0
            batch.append(t)
            bcols += t[1]
        if batch:
            flush(batch)
    wtiles = []
    for ch in range(NCH):
        wtiles.append((d_wq[128 * ch:128 * (ch + 1), :], 1024,
                       wqb(0, ch, 0, 1024), wqb(1, ch, 0, 1024)))
    for ch in range(NCH):
        wtiles.append((d_wk[128 * ch:128 * (ch + 1), :], 256,
                       wkb(0, ch, 0, 256), wkb(1, ch, 0, 256)))
        wtiles.append((d_wv[128 * ch:128 * (ch + 1), :], 256,
                       wvb(0, ch), wvb(1, ch)))
    for ch in range(NCH):
        wtiles.append((d_wo[128 * ch:128 * (ch + 1), :], 1024,
                       wob(0, ch, 0, 1024), wob(1, ch, 0, 1024)))
    wsplit_steps(wtiles)

    for r in range(-1, dbg_nrange):
        slot = (r + 1) % 3
        tb0 = HALO + RNG * r

        step("sync", lambda E, tb0=tb0: [
            E.dma_start(out=cosl(0, RNG), in_=d_cos[:, tb0:tb0 + RNG]),
            E.dma_start(out=sinl(0, RNG), in_=d_sin[:, tb0:tb0 + RNG]),
        ], writes=[res("tabs")], dma="tabs", n_dma=2)

        for j in range(4):
            xb = j % 2
            step("sync", lambda E, tb0=tb0, j=j, xb=xb:
                 E.dma_start(out=aA("xtok", c0=1024 * xb, c1=1024 * (xb + 1)),
                             in_=d_x[tb0 + 128 * j: tb0 + 128 * (j + 1), :]),
                 writes=[res(f"xtok{xb}")], dma=f"xin{xb}")
            def tr(E, xb=xb, j=j):
                last = None
                b0 = 2 * (j % 2)
                for ch in range(NCH):
                    bb = b0 + (0 if ch < 4 else 1)
                    last = E.transpose(bank(bb, c0=128 * (ch % 4), c1=128 * (ch % 4) + 128),
                                       aA("xtok", c0=1024 * xb + 128 * ch,
                                          c1=1024 * xb + 128 * (ch + 1)),
                                       ident)
                return last
            step("tensor", tr, reads=[res(f"xtok{xb}"), res("consts"),
                                      res(f"pb{2*(j%2)}"), res(f"pb{2*(j%2)+1}")],
                 writes=[res(f"pb{2*(j%2)}"), res(f"pb{2*(j%2)+1}")])
            def tstat(E, j=j, xb=xb):
                sx = arena[:, O_ROWS + 0 + j: O_ROWS + 1 + j]
                sq = arena[:, O_ROWS + 4 + j: O_ROWS + 5 + j]
                E.reduce_sum(sx, aA("xtok", c0=1024 * xb, c1=1024 * (xb + 1)),
                             axis=mybir.AxisListType.X)
                return E.scalar_tensor_tensor(aA("ropes", c0=0, c1=1024),
                                              aA("xtok", c0=1024 * xb, c1=1024 * (xb + 1)),
                                              1.0,
                                              aA("xtok", c0=1024 * xb, c1=1024 * (xb + 1)),
                                              ALU.mult, ALU.mult, accum_out=sq)
            step("vector", tstat, reads=[res(f"xtok{xb}"), res("scr0"), res("scr1"),
                                         res("statc")],
                 writes=[res("scr0"), res("scr1"), res("statc")])
            def ev(E, j=j):
                last = None
                b0 = 2 * (j % 2)
                for ch in range(NCH):
                    bb = b0 + (0 if ch < 4 else 1)
                    last = E.copy(aA("xfm", c0=RNG * ch + 128 * j,
                                     c1=RNG * ch + 128 * (j + 1)),
                                  bank(bb, c0=128 * (ch % 4), c1=128 * (ch % 4) + 128))
                return last
            step("scalar", ev, reads=[res(f"pb{2*(j%2)}"), res(f"pb{2*(j%2)+1}"), res("xfm")],
                 writes=[res("xfm")])

        def statchain(E):
            sx = arena[:, O_ROWS + 0: O_ROWS + 4]
            sq = arena[:, O_ROWS + 4: O_ROWS + 8]
            mu = arena[:, O_ROWS + 8: O_ROWS + 12]
            iv = arena[:, O_ROWS + 12: O_ROWS + 16]
            E.tensor_scalar_mul(mu, sx, 1.0 / C)
            E.tensor_scalar_mul(iv, sq, 1.0 / C)
            E.drain()
            E.tensor_tensor(iv, iv, arena[:, O_ROWS + 8: O_ROWS + 12], ALU.subtract) \
                if False else None
            tmp = arena[:, O_ROWS + 34: O_ROWS + 38]
            E.tensor_tensor(tmp, mu, mu, ALU.mult)
            E.drain()
            E.tensor_tensor(iv, iv, tmp, ALU.subtract)
            E.drain()
            E.tensor_scalar_add(iv, iv, EPS_LN)
            E.drain()
            return E.reciprocal(iv, iv)
        step("vector", statchain, reads=[res("statc")], writes=[res("statc")])
        step("scalar", lambda E: E.activation(arena[:, O_ROWS + 12: O_ROWS + 16],
                                              arena[:, O_ROWS + 12: O_ROWS + 16], AF.Sqrt),
             reads=[res("statc")], writes=[res("statc")])
        def stat_tr(E):
            last = None
            for j in range(4):
                E.transpose(bank(6, p1=1, c0=128 * j, c1=128 * (j + 1)),
                            arena[:, O_ROWS + 8 + j: O_ROWS + 9 + j], ident)
                last = E.transpose(bank(7, p1=1, c0=128 * j, c1=128 * (j + 1)),
                                   arena[:, O_ROWS + 12 + j: O_ROWS + 13 + j], ident)
            return last
        step("tensor", stat_tr, reads=[res("statc"), res("consts"), res("pb6"), res("pb7")],
             writes=[res("pb6"), res("pb7")])
        def statrow_ev(E):
            E.tensor_copy(aA("ropes", p0=0, p1=1, c0=0, c1=RNG), bank(6, p1=1, c0=0, c1=RNG))
            return E.tensor_copy(aA("ropes", p0=0, p1=1, c0=RNG, c1=2 * RNG),
                                 bank(7, p1=1, c0=0, c1=RNG))
        step("vector", statrow_ev, reads=[res("pb6"), res("pb7"), res("scr0"), res("scr1")],
             writes=[res("scr0"), res("scr1")])
        if dbg_stat and r == 0:
            step("gpsimd", lambda E: E.dma_start(out=d_dbg, in_=aA("ropes", p0=0, p1=1, c0=0, c1=1024)),
                 reads=[res("scr0"), res("scr1")], writes=[res("dbgx")], dma="dbgo")
        def bc_mm(E):
            E.matmul(bank(6, c0=0, c1=RNG), ones_row, aA("ropes", p0=0, p1=1, c0=0, c1=RNG),
                     start=True, stop=True)
            return E.matmul(bank(7, c0=0, c1=RNG), ones_row,
                            aA("ropes", p0=0, p1=1, c0=RNG, c1=2 * RNG),
                            start=True, stop=True)
        step("tensor", bc_mm, reads=[res("scr0"), res("scr1"), res("consts"),
                                     res("pb6"), res("pb7")],
             writes=[res("pb6"), res("pb7")])
        def bc_ev(E):
            E.copy(aA("bmu"), bank(6, c0=0, c1=RNG))
            return E.copy(aA("binv"), bank(7, c0=0, c1=RNG))
        step("scalar", bc_ev, reads=[res("pb6"), res("pb7"), res("bmu"), res("binv")],
             writes=[res("bmu"), res("binv")])
        for ch in range(NCH):
            sp = ch % 2
            def xns(E, ch=ch, sp=sp):
                s_ = aA("ropes", c0=RNG * sp, c1=RNG * (sp + 1))
                E.tensor_tensor(s_, aA("xfm", c0=RNG * ch, c1=RNG * (ch + 1)),
                                aA("bmu"), ALU.subtract)
                E.drain()
                E.tensor_tensor(s_, s_, aA("binv"), ALU.mult)
                E.drain()
                E.tensor_copy(aAbf("xn", RNG * ch, RNG * (ch + 1)), s_)
                E.drain()
                return E.tensor_tensor(aAbf("xn", NCH * RNG + RNG * ch,
                                            NCH * RNG + RNG * (ch + 1)),
                                       s_, aAbf("xn", RNG * ch, RNG * (ch + 1)), ALU.subtract)
            step("vector", xns, reads=[res("xfm"), res("bmu"), res("binv"), res(f"scr{sp}"),
                                       res(f"xnp{ch}")],
                 writes=[res(f"scr{sp}"), res(f"xnp{ch}")])
        if dbg_stat and r == 1:
            def dbgkr(E):
                at = A_["kring"][0]
                return [E.dma_start(out=d_dbg3[:, 0:256], in_=arena[:, at + 256: at + 512]),
                        E.dma_start(out=d_dbg3[:, 256:512], in_=arena[:, at + 1024: at + 1280]),
                        E.dma_start(out=d_dbg3[:, 512:768], in_=arena[:, at + 1536 + 256: at + 1536 + 512]),
                        E.dma_start(out=d_dbg3[:, 768:1024], in_=arena[:, at + 1536 + 1024: at + 1536 + 1280])]
            step("gpsimd", dbgkr, reads=[res("ks1")], writes=[res("dbgx3")], dma="dbgo3", n_dma=4)
            def dbgvr(E):
                at = A_["vring"][0]
                return [E.dma_start(out=d_dbg5[:, 0:512], in_=arena[:, at + 512: at + 1024]),
                        E.dma_start(out=d_dbg5[:, 512:1024], in_=arena[:, at + 1536 + 512: at + 1536 + 1024])]
            step("gpsimd", dbgvr, reads=[res("vs1")], writes=[res("dbgx5")], dma="dbgo5", n_dma=2)
        if dbg_stat and r == 0:
            def dbgxn(E):
                return E.dma_start(out=d_dbg2, in_=aA("xn", c0=0, c1=2048))
            step("gpsimd", dbgxn, reads=[res(f"xnp{c_}") for c_ in range(NCH)],
                 writes=[res("dbgx2")], dma="dbgo2")
        def xnhi(ch):
            return aAbf("xn", RNG * ch, RNG * (ch + 1))
        def xnlo(ch):
            return aAbf("xn", NCH * RNG + RNG * ch, NCH * RNG + RNG * (ch + 1))

        for ch in range(NCH):
            def kvch(E, ch=ch):
                last = None
                for kc in range(2):
                    whi = wkb(0, ch, 128 * kc, 128 * (kc + 1))
                    wlo = wkb(1, ch, 128 * kc, 128 * (kc + 1))
                    plan = [(whi, xnhi(ch)), (whi, xnlo(ch)), (wlo, xnhi(ch))]
                    for pi, (lhs, rhs) in enumerate(plan):
                        last = E.matmul(bank(2 + kc, c0=0, c1=RNG), lhs, rhs,
                                        start=(ch == 0 and pi == 0),
                                        stop=(ch == NCH - 1 and pi == 2))
                for j in range(4):
                    b = 4 + j
                    c0 = 0
                    xh = aAbf("xn", RNG * ch + 128 * j, RNG * ch + 128 * (j + 1))
                    xl = aAbf("xn", NCH * RNG + RNG * ch + 128 * j,
                              NCH * RNG + RNG * ch + 128 * (j + 1))
                    vh, vl = wvb(0, ch), wvb(1, ch)
                    plan = [(xh, vh), (xh, vl), (xl, vh)]
                    for pi, (lhs, rhs) in enumerate(plan):
                        last = E.matmul(bank(b, c0=c0, c1=c0 + HD), lhs, rhs,
                                        start=(ch == 0 and pi == 0),
                                        stop=(ch == NCH - 1 and pi == 2))
                return last
            step("tensor", kvch,
                 reads=[res(f"xnp{ch}"), res("w")] + [res(f"pb{b}") for b in range(2, 8)],
                 writes=([res(f"pb{b}") for b in range(2, 8)] if ch == NCH - 1 else []))
        def ksq(E):
            E.activation(aA("q2", c0=0, c1=RNG), bank(2, c0=0, c1=RNG), AF.Square)
            return E.activation(aA("q2", c0=RNG, c1=2 * RNG), bank(3, c0=0, c1=RNG), AF.Square)
        step("scalar", ksq, reads=[res("pb2"), res("pb3"), res("q2")], writes=[res("q2")])
        def k_ms(E):
            E.matmul(bank(0, p1=1, c0=0, c1=RNG), ones_col, aA("q2", c0=0, c1=RNG),
                     start=True, stop=False)
            return E.matmul(bank(0, p1=1, c0=0, c1=RNG), ones_col, aA("q2", c0=RNG, c1=2 * RNG),
                            start=False, stop=True)
        step("tensor", k_ms, reads=[res("q2"), res("consts"), res("pb0")], writes=[res("pb0")])
        def k_inv1(E):
            kr = aA("q2", p0=0, p1=1, c0=0, c1=RNG)
            E.tensor_scalar_add(kr, bank(0, p1=1, c0=0, c1=RNG), EPS_RMS_SCALED)
            E.drain()
            return E.reciprocal(kr, kr)
        step("vector", k_inv1, reads=[res("pb0"), res("q2")], writes=[res("q2")])
        def k_inv2(E):
            kr = aA("q2", p0=0, p1=1, c0=0, c1=RNG)
            E.activation(kr, kr, AF.Sqrt)
            E.drain()
            return E.activation(kr, kr, AF.Copy, scale=16.0)
        step("scalar", k_inv2, reads=[res("q2")], writes=[res("q2")])
        def k_bc(E):
            return E.matmul(bank(0, c0=0, c1=RNG), ones_row,
                            aA("q2", p0=0, p1=1, c0=0, c1=RNG), start=True, stop=True)
        step("tensor", k_bc, reads=[res("q2"), res("consts"), res("pb0")], writes=[res("pb0")])
        step("vector", lambda E: E.tensor_copy(aA("binvk"), bank(0, c0=0, c1=RNG)),
             reads=[res("pb0"), res("binvk")], writes=[res("binvk")])
        def k_rope(E, slot=slot):
            t1 = aA("ropes", c0=0, c1=RNG)
            t2 = aA("ropes", c0=RNG, c1=2 * RNG)
            kef = aA("q2", c0=0, c1=RNG)
            kof = aA("q2", c0=RNG, c1=2 * RNG)
            E.scalar_tensor_tensor(t1, bank(2, c0=0, c1=RNG), gam(2), cosl(0, RNG), ALU.mult, ALU.mult)
            E.scalar_tensor_tensor(t2, bank(3, c0=0, c1=RNG), gam(3), sinl(0, RNG), ALU.mult, ALU.mult)
            E.drain()
            E.tensor_tensor(kef, t1, t2, ALU.subtract)
            E.drain()
            E.scalar_tensor_tensor(t1, bank(2, c0=0, c1=RNG), gam(2), sinl(0, RNG), ALU.mult, ALU.mult)
            E.scalar_tensor_tensor(t2, bank(3, c0=0, c1=RNG), gam(3), cosl(0, RNG), ALU.mult, ALU.mult)
            E.drain()
            E.tensor_tensor(kof, t1, t2, ALU.add)
            E.drain()
            E.tensor_tensor(kef, kef, aA("binvk"), ALU.mult)
            E.tensor_tensor(kof, kof, aA("binvk"), ALU.mult)
            E.drain()
            E.tensor_copy(aAbf("kring", (0 * 3 + slot) * RNG, (0 * 3 + slot + 1) * RNG), kef)
            return E.tensor_copy(aAbf("kring", (1 * 3 + slot) * RNG, (1 * 3 + slot + 1) * RNG), kof)
        step("vector", k_rope,
             reads=[res("pb2"), res("pb3"), res("consts"), res("tabs"), res("binvk"),
                    res("scr0"), res("scr1"), res("q2"), res(f"ks{slot}")],
             writes=[res(f"ks{slot}"), res("scr0"), res("scr1"), res("q2")])
        def k_lo(E, slot=slot):
            E.tensor_tensor(aAbf("kring", 3072 + (0 * 3 + slot) * RNG,
                                 3072 + (0 * 3 + slot + 1) * RNG),
                            aA("q2", c0=0, c1=RNG),
                            aAbf("kring", (0 * 3 + slot) * RNG, (0 * 3 + slot + 1) * RNG),
                            ALU.subtract)
            return E.tensor_tensor(aAbf("kring", 3072 + (1 * 3 + slot) * RNG,
                                        3072 + (1 * 3 + slot + 1) * RNG),
                                   aA("q2", c0=RNG, c1=2 * RNG),
                                   aAbf("kring", (1 * 3 + slot) * RNG, (1 * 3 + slot + 1) * RNG),
                                   ALU.subtract)
        step("vector", k_lo, reads=[res("q2"), res(f"ks{slot}")], writes=[res(f"ks{slot}")])

        def v_hi(E, slot=slot):
            last = None
            for j in range(4):
                last = E.tensor_copy(aAbf("vring", (4 * slot + j) * HD, (4 * slot + j + 1) * HD),
                                     bank(4 + j, c0=0, c1=HD))
            return last
        step("vector", v_hi, reads=[res("pb4"), res("pb5"), res("pb6"), res("pb7"),
                                    res(f"vs{slot}")],
             writes=[res(f"vs{slot}")])
        def v_lo(E, slot=slot):
            last = None
            for j in range(4):
                last = E.tensor_tensor(
                    aAbf("vring", 3072 + (4 * slot + j) * HD, 3072 + (4 * slot + j + 1) * HD),
                    bank(4 + j, c0=0, c1=HD),
                    aAbf("vring", (4 * slot + j) * HD, (4 * slot + j + 1) * HD),
                    ALU.subtract)
            return last
        step("vector", v_lo, reads=[res("pb4"), res("pb5"), res("pb6"), res("pb7"),
                                    res(f"vs{slot}")],
             writes=[res(f"vs{slot}")])

        if r < 0 or dbg_noq:
            continue

        for hp in range(NH):
            qb = 4 if hp % 2 == 0 else 2
            def qmm(E, hp=hp, qb=qb):
                last, first = None, [True, True]
                for kc in range(2):
                    pc = 2 * hp + kc
                    for ch in range(NCH):
                        whi = wqb(0, ch, 128 * pc, 128 * (pc + 1))
                        wlo = wqb(1, ch, 128 * pc, 128 * (pc + 1))
                        for lhs, rhs in ((whi, xnhi(ch)), (whi, xnlo(ch)), (wlo, xnhi(ch))):
                            last = E.matmul(bank(qb + kc, c0=0, c1=RNG), lhs, rhs,
                                            start=first[kc],
                                            stop=(ch == NCH - 1 and lhs is wlo))
                            first[kc] = False
                return last
            step("tensor", qmm, reads=[res("w"), res(f"pb{qb}"), res(f"pb{qb+1}")] +
                                [res(f"xnp{c_}") for c_ in range(NCH)],
                 writes=[res(f"pb{qb}"), res(f"pb{qb+1}")])
            def qsq(E, qb=qb):
                E.activation(aA("q2", c0=0, c1=RNG), bank(qb, c0=0, c1=RNG), AF.Square)
                return E.activation(aA("q2", c0=RNG, c1=2 * RNG), bank(qb + 1, c0=0, c1=RNG), AF.Square)
            step("scalar", qsq, reads=[res(f"pb{qb}"), res(f"pb{qb+1}"), res("q2")], writes=[res("q2")])
            def q_ms(E, hp=hp):
                last = None
                for j in range(4):
                    col = 4 * hp + j
                    E.matmul(bank(6, c0=col, c1=col + 1),
                             aA("q2", c0=128 * j, c1=128 * (j + 1)),
                             ones_col, start=True, stop=False)
                    last = E.matmul(bank(6, c0=col, c1=col + 1),
                                    aA("q2", c0=RNG + 128 * j, c1=RNG + 128 * (j + 1)),
                                    ones_col, start=False, stop=True)
                return last
            step("tensor", q_ms, reads=[res("q2"), res("consts"), res("pb6")],
                 writes=[res("pb6")])
            def q_rope(E, hp=hp, qb=qb):
                t1 = aA("ropes", c0=0, c1=RNG)
                t2 = aA("ropes", c0=RNG, c1=2 * RNG)
                qef = aA("q2", c0=0, c1=RNG)
                qof = aA("q2", c0=RNG, c1=2 * RNG)
                E.scalar_tensor_tensor(t1, bank(qb, c0=0, c1=RNG), gam(0), cosl(0, RNG), ALU.mult, ALU.mult)
                E.scalar_tensor_tensor(t2, bank(qb + 1, c0=0, c1=RNG), gam(1), sinl(0, RNG), ALU.mult, ALU.mult)
                E.drain()
                E.tensor_tensor(qef, t1, t2, ALU.subtract)
                E.drain()
                E.scalar_tensor_tensor(t1, bank(qb, c0=0, c1=RNG), gam(0), sinl(0, RNG), ALU.mult, ALU.mult)
                E.scalar_tensor_tensor(t2, bank(qb + 1, c0=0, c1=RNG), gam(1), cosl(0, RNG), ALU.mult, ALU.mult)
                E.drain()
                E.tensor_tensor(qof, t1, t2, ALU.add)
                E.drain()
                E.tensor_copy(aAbf("qfm", RNG * (2 * hp), RNG * (2 * hp + 1)), qef)
                return E.tensor_copy(aAbf("qfm", RNG * (2 * hp + 1), RNG * (2 * hp + 2)), qof)
            step("vector", q_rope,
                 reads=[res(f"pb{qb}"), res(f"pb{qb+1}"), res("consts"), res("tabs"), res("scr0"),
                        res("scr1"), res("q2"), res("qfm")],
                 writes=[res("qfm"), res("scr0"), res("scr1"), res("q2")])
            def q_lo(E, hp=hp):
                E.tensor_tensor(aAbf("qfm", NCH * RNG + RNG * (2 * hp),
                                     NCH * RNG + RNG * (2 * hp + 1)),
                                aA("q2", c0=0, c1=RNG),
                                aAbf("qfm", RNG * (2 * hp), RNG * (2 * hp + 1)), ALU.subtract)
                return E.tensor_tensor(aAbf("qfm", NCH * RNG + RNG * (2 * hp + 1),
                                            NCH * RNG + RNG * (2 * hp + 2)),
                                       aA("q2", c0=RNG, c1=2 * RNG),
                                       aAbf("qfm", RNG * (2 * hp + 1), RNG * (2 * hp + 2)),
                                       ALU.subtract)
            step("vector", q_lo, reads=[res("q2"), res("qfm")], writes=[res("qfm")])
        def invq1(E):
            iq = invq_ap(0, 16)
            E.tensor_scalar_add(iq, bank(6, c0=0, c1=16), EPS_RMS_SCALED)
            E.drain()
            return E.reciprocal(iq, iq)
        step("vector", invq1, reads=[res("pb6"), res("invq")], writes=[res("invq")])
        step("scalar", lambda E: E.activation(invq_ap(0, 16), invq_ap(0, 16), AF.Sqrt),
             reads=[res("invq")], writes=[res("invq")])

        def qhib(h, kc, qt):
            return aAbf("qfm", RNG * (2 * h + kc) + 128 * qt, RNG * (2 * h + kc) + 128 * (qt + 1))
        def qlob(h, kc, qt):
            b0 = NCH * RNG
            return aAbf("qfm", b0 + RNG * (2 * h + kc) + 128 * qt,
                        b0 + RNG * (2 * h + kc) + 128 * (qt + 1))
        def khib(kc, sl, col):
            return aAbf("kring", (kc * 3 + sl) * RNG + col, (kc * 3 + sl) * RNG + col + 128)
        def klob(kc, sl, col):
            return aAbf("kring", 3072 + (kc * 3 + sl) * RNG + col,
                        3072 + (kc * 3 + sl) * RNG + col + 128)
        def vhib(sl, sub, dc):
            return aAbf("vring", (4 * sl + sub) * HD + 128 * dc,
                        (4 * sl + sub) * HD + 128 * (dc + 1))
        def vlob(sl, sub, dc):
            return aAbf("vring", 3072 + (4 * sl + sub) * HD + 128 * dc,
                        3072 + (4 * sl + sub) * HD + 128 * (dc + 1))

        def emit_smm(i, h, qt, t0g):
            p = i % 2
            bA, bB = 2 * p, 2 * p + 1
            def smm(E, h=h, qt=qt, t0g=t0g, r=r, bA=bA, bB=bB):
                prev_sl = ((t0g - 512) // RNG + 1) % 3
                cur_sl = ((t0g // RNG) + 1) % 3
                w1_ = 512 - 128 * qt
                runs = [(prev_sl, 128 * qt, 0, w1_)]
                if qt > 0:
                    runs.append((cur_sl, 0, w1_, 128 * qt))
                last = None
                first = True
                for kc in range(2):
                    for (sl, scol, dcol, w) in runs:
                        for lhs, rhsf in (((0, kc), (0, sl, scol, w)),
                                          ((0, kc), (1, sl, scol, w)),
                                          ((1, kc), (0, sl, scol, w))):
                            qh = qhib(h, kc, qt) if lhs[0] == 0 else qlob(h, kc, qt)
                            hl, _sl, _sc, _w = rhsf
                            if hl == 0:
                                kr_ = aAbf("kring", (kc * 3 + _sl) * RNG + _sc,
                                           (kc * 3 + _sl) * RNG + _sc + _w)
                            else:
                                kr_ = aAbf("kring", 3072 + (kc * 3 + _sl) * RNG + _sc,
                                           3072 + (kc * 3 + _sl) * RNG + _sc + _w)
                            if lhs[0] == 1 and hl == 1:
                                continue
                            last = E.matmul(bank(bA, c0=dcol, c1=dcol + w), qh, kr_,
                                            start=first, stop=False)
                            first = False
                ops = [("maskL",)]
                if r == 0:
                    for kb in range(4):
                        if qt + kb < 4:
                            ops.append(("kill", kb))
                for i_, op in enumerate(ops):
                    sp = (i_ == len(ops) - 1)
                    if op[0] == "maskL":
                        last = E.matmul(bank(bA, c0=0, c1=128), strictLb, negIb,
                                        start=False, stop=sp)
                    else:
                        last = E.matmul(bank(bA, c0=128 * op[1], c1=128 * (op[1] + 1)),
                                        ones_row, kbias, start=False, stop=sp)
                col = t0g - RNG * (t0g // RNG)
                firstB = True
                for kc in range(2):
                    for hl_q, hl_k in ((0, 0), (0, 1), (1, 0)):
                        qh = qhib(h, kc, qt) if hl_q == 0 else qlob(h, kc, qt)
                        base = 0 if hl_k == 0 else 3072
                        kr_ = aAbf("kring", base + (kc * 3 + cur_sl) * RNG + col,
                                   base + (kc * 3 + cur_sl) * RNG + col + 128)
                        E.matmul(bank(bB, c0=0, c1=128), qh, kr_, start=firstB, stop=False)
                        firstB = False
                return E.matmul(bank(bB, c0=0, c1=128), strictUb, negIb, start=False, stop=True)
            kslots = set()
            for kb in range(5):
                tk = t0g - 512 + 128 * kb
                kslots.add(((tk // RNG) + 1) % 3)
            step("tensor", smm,
                 reads=[res("qfm"), res("consts"), res(f"pb{bA}"), res(f"pb{bB}")] +
                       [res(f"ks{s}") for s in sorted(kslots)],
                 writes=[res(f"pb{bA}"), res(f"pb{bB}")])
            return kslots

        def emit_exp(i, h, qt):
            p = i % 2
            def expP(E, h=h, qt=qt, p=p):
                return E.activation(aA("pp"), psum[:, 1024 * p:1024 * p + 640], AF.Exp,
                                    scale=invq_ap(4 * h + qt, 4 * h + qt + 1),
                                    accum_out=lcol)
            step("scalar", expP, reads=[res(f"pb{2*p}"), res(f"pb{2*p+1}"), res("invq"),
                                        res("pp"), res("lc")],
                 writes=[res("pp"), res("lc")])

        def emit_psplit(i):
            def lr_(E):
                return E.reciprocal(lrec, lcol)
            step("vector", lr_, reads=[res("lc"), res("lr")], writes=[res("lr")])
            def phi(E):
                return E.activation(aAbf("phl", 0, 640), aA("pp"), AF.Copy, scale=lrec)
            step("scalar", phi, reads=[res("pp"), res("lr"), res("phl")], writes=[res("phl")])
            def plo(E):
                return E.scalar_tensor_tensor(aAbf("phl", 640, 1280), aA("pp"), lrec,
                                              aAbf("phl", 0, 640), ALU.mult, ALU.subtract)
            step("vector", plo, reads=[res("pp"), res("lr"), res("phl")], writes=[res("phl")])

        def ptb_at(i, kb, lo):
            at, _ = A_["pt"]
            base = at + 640 * (i % 2)
            off = (64 * kb) if not lo else (320 + 64 * kb)
            return arena[:, base + off: base + off + 64].bitcast(BF16)

        def emit_ptr(i):
            p = i % 2
            pb_pt = 4 + p
            def ptrA(E, pb_pt=pb_pt):
                last = None
                for kb in range(4):
                    last = E.transpose(psbf(pb_pt, 256 * kb, 256 * kb + 128),
                                       aAbf("phl", 128 * kb, 128 * (kb + 1)), identb)
                    last = E.transpose(psbf(pb_pt, 256 * kb + 128, 256 * kb + 256),
                                       aAbf("phl", 640 + 128 * kb, 640 + 128 * (kb + 1)), identb)
                return last
            step("tensor", ptrA, reads=[res("phl"), res("consts"), res(f"pb{pb_pt}")],
                 writes=[res(f"pb{pb_pt}")])
            def ptevA(E, pb_pt=pb_pt, i=i):
                at, _ = A_["pt"]
                base = at + 640 * (i % 2)
                hi = arena[:, base: base + 256].bitcast(BF16).rearrange("p (a c) -> p a c", c=128)
                lo = arena[:, base + 320: base + 576].bitcast(BF16).rearrange("p (a c) -> p a c", c=128)
                src3 = psbf(pb_pt, 0, 1024).rearrange("p (a c) -> p a c", c=128)
                E.tensor_copy(hi, src3[:, 0::2, :])
                return E.tensor_copy(lo, src3[:, 1::2, :])
            step("vector", ptevA, reads=[res(f"pb{pb_pt}"), res(f"pt{i%2}")],
                 writes=[res(f"pt{i%2}")])
            def ptrB(E, pb_pt=pb_pt):
                E.transpose(psbf(pb_pt, 0, 128), aAbf("phl", 512, 640), identb)
                return E.transpose(psbf(pb_pt, 128, 256), aAbf("phl", 1152, 1280), identb)
            step("tensor", ptrB, reads=[res("phl"), res("consts"), res(f"pb{pb_pt}"),
                                        res(f"pt{i%2}")],
                 writes=[res(f"pb{pb_pt}")])
            def ptevB(E, pb_pt=pb_pt, i=i):
                at, _ = A_["pt"]
                base = at + 640 * (i % 2)
                E.tensor_copy(arena[:, base + 256: base + 320].bitcast(BF16), psbf(pb_pt, 0, 128))
                return E.tensor_copy(arena[:, base + 576: base + 640].bitcast(BF16),
                                     psbf(pb_pt, 128, 256))
            step("vector", ptevB, reads=[res(f"pb{pb_pt}"), res(f"pt{i%2}")],
                 writes=[res(f"pt{i%2}")])

        def emit_av(i, h, qt, t0g, kslots):
            def avmm2(E, t0g=t0g, i=i):
                last = None
                for dc in range(2):
                    ob, oc0 = (7, 0) if dc == 0 else (6, 256)
                    plan = []
                    for kb in range(5):
                        tk = t0g - 512 + 128 * kb
                        rr = tk // RNG
                        sl = (rr + 1) % 3
                        col = tk - RNG * rr
                        sub = col // 128
                        plan.append((vhib(sl, sub, dc), ptb_at(i, kb, False)))
                        plan.append((vhib(sl, sub, dc), ptb_at(i, kb, True)))
                        plan.append((vlob(sl, sub, dc), ptb_at(i, kb, False)))
                    for i_, (lhs, rhs) in enumerate(plan):
                        last = E.matmul(bank(ob, c0=oc0, c1=oc0 + 128), lhs, rhs,
                                        start=(i_ == 0), stop=(i_ == len(plan) - 1))
                return last
            step("tensor", avmm2,
                 reads=[res(f"pt{i%2}"), res("pb6"), res("pb7")] +
                       [res(f"vs{s}") for s in sorted(kslots)],
                 writes=[res("pb6"), res("pb7")])
            def oevh(E, h=h, qt=qt):
                E.copy(aAbf("orng", RNG * (2 * h) + 128 * qt,
                            RNG * (2 * h) + 128 * (qt + 1)),
                       bank(7, c0=0, c1=128))
                return E.copy(aAbf("orng", RNG * (2 * h + 1) + 128 * qt,
                                   RNG * (2 * h + 1) + 128 * (qt + 1)),
                              bank(6, c0=256, c1=384))
            step("scalar", oevh, reads=[res("pb6"), res("pb7"), res("orng")],
                 writes=[res("orng")])
            def oevl(E, h=h, qt=qt):
                b0 = NCH * RNG
                E.tensor_tensor(aAbf("orng", b0 + RNG * (2 * h) + 128 * qt,
                                     b0 + RNG * (2 * h) + 128 * (qt + 1)),
                                bank(7, c0=0, c1=128),
                                aAbf("orng", RNG * (2 * h) + 128 * qt,
                                     RNG * (2 * h) + 128 * (qt + 1)), ALU.subtract)
                return E.tensor_tensor(aAbf("orng", b0 + RNG * (2 * h + 1) + 128 * qt,
                                            b0 + RNG * (2 * h + 1) + 128 * (qt + 1)),
                                       bank(6, c0=256, c1=384),
                                       aAbf("orng", RNG * (2 * h + 1) + 128 * qt,
                                            RNG * (2 * h + 1) + 128 * (qt + 1)), ALU.subtract)
            step("vector", oevl, reads=[res("pb6"), res("pb7"), res("orng")],
                 writes=[res("orng")])

        tiles = [(h, qt) for h in range(0 if dbg_noattn else NH) for qt in range(4)]
        pend_ptr = None
        pend_av = []
        for i, (h, qt) in enumerate(tiles):
            t0g = RNG * r + 128 * qt
            kslots = emit_smm(i, h, qt, t0g)
            emit_exp(i, h, qt)
            if pend_ptr is not None:
                emit_ptr(pend_ptr[0])
                pend_av.append(pend_ptr)
            emit_psplit(i)
            while len(pend_av) > 1:
                emit_av(*pend_av.pop(0))
            pend_ptr = (i, h, qt, t0g, kslots)
        if pend_ptr is not None:
            emit_ptr(pend_ptr[0])
            pend_av.append(pend_ptr)
        while pend_av:
            emit_av(*pend_av.pop(0))

        def ohib(ch):
            return aAbf("orng", RNG * ch, RNG * (ch + 1))
        def olob(ch):
            return aAbf("orng", NCH * RNG + RNG * ch, NCH * RNG + RNG * (ch + 1))
        for oc in range(0 if dbg_nowo else NCH):
            b = oc % 2
            def womm(E, oc=oc, b=b):
                plan = []
                for ch in range(NCH):
                    whi = wob(0, ch, 128 * oc, 128 * (oc + 1))
                    wlo = wob(1, ch, 128 * oc, 128 * (oc + 1))
                    plan += [(whi, ohib(ch)), (whi, olob(ch)), (wlo, ohib(ch))]
                last = None
                for i_, (lhs, rhs) in enumerate(plan):
                    last = E.matmul(bank(b, c0=0, c1=RNG), lhs, rhs,
                                    start=(i_ == 0), stop=(i_ == len(plan) - 1))
                return last
            step("tensor", womm, reads=[res("orng"), res("w"), res(f"pb{b}")],
                 writes=[res(f"pb{b}")])
            def woev(E, oc=oc, b=b):
                return E.tensor_tensor(aA("xfm", c0=RNG * oc, c1=RNG * (oc + 1)),
                                       bank(b, c0=0, c1=RNG),
                                       aA("xfm", c0=RNG * oc, c1=RNG * (oc + 1)), ALU.add)
            step("vector", woev, reads=[res(f"pb{b}"), res("xfm")], writes=[res("xfm")])
            step("gpsimd", lambda E, oc=oc, r=r:
                 E.dma_start(out=d_x1[128 * oc:128 * (oc + 1), RNG * r:RNG * (r + 1)],
                             in_=aA("xfm", c0=RNG * oc, c1=RNG * (oc + 1))),
                 reads=[res("xfm")], writes=[res("x1s")], dma="x1o")

    if dbg_nob:
        bld.finalize()
        return nc
    barrier()
    for ch in range(NCH):
        stg = "x1slab" if ch % 2 == 0 else "xn2"
        step("sync", lambda E, ch=ch, stg=stg: E.dma_start(out=aB(stg),
                                                           in_=d_w1[128 * ch:128 * (ch + 1), :]),
             reads=[res(stg)], writes=[res(stg)], dma=f"wld{ch % 2}")
        def w1split(E, ch=ch, stg=stg):
            E.tensor_copy(w1b(0, ch, 0, FF), aB(stg))
            E.drain()
            return E.tensor_tensor(w1b(1, ch, 0, FF), aB(stg), w1b(0, ch, 0, FF),
                                   ALU.subtract)
        step("vector", w1split, reads=[res(stg), res("w")], writes=[res("w")])

    for s in range(NRANGE):
        def ldx1(E, s=s):
            return [E.dma_start(out=aB("x1slab", c0=RNG * ch, c1=RNG * (ch + 1)),
                                in_=d_x1[128 * ch:128 * (ch + 1), RNG * s:RNG * (s + 1)])
                    for ch in range(NCH)]
        step("sync", ldx1, reads=[res("x1s"), res("x1slab")], writes=[res("x1slab")],
             dma="x1i", n_dma=NCH)
        def x1sq(E):
            last = None
            for ch in range(NCH):
                last = E.activation(aB("xn2", c0=RNG * ch, c1=RNG * (ch + 1)),
                                    aB("x1slab", c0=RNG * ch, c1=RNG * (ch + 1)), AF.Square)
            return last
        step("scalar", x1sq, reads=[res("x1slab"), res("xn2")], writes=[res("xn2")])
        def stat2(E):
            last = None
            for ch in range(NCH):
                E.matmul(bank(6, p1=1, c0=0, c1=RNG), ones_col,
                         aB("x1slab", c0=RNG * ch, c1=RNG * (ch + 1)),
                         start=(ch == 0), stop=(ch == NCH - 1))
            for ch in range(NCH):
                last = E.matmul(bank(7, p1=1, c0=0, c1=RNG), ones_col,
                                aB("xn2", c0=RNG * ch, c1=RNG * (ch + 1)),
                                start=(ch == 0), stop=(ch == NCH - 1))
            return last
        step("tensor", stat2, reads=[res("x1slab"), res("xn2"), res("consts"),
                                     res("pb6"), res("pb7")],
             writes=[res("pb6"), res("pb7")])
        def stat2r(E):
            mu = aB("brows", p0=0, p1=1, c0=0, c1=RNG)
            e2 = aB("brows", p0=0, p1=1, c0=RNG, c1=2 * RNG)
            tmp = aB("bbc", p0=0, p1=1, c0=0, c1=RNG)
            E.tensor_scalar_mul(mu, bank(6, p1=1, c0=0, c1=RNG), 1.0 / C)
            E.tensor_scalar_mul(e2, bank(7, p1=1, c0=0, c1=RNG), 1.0 / C)
            E.drain()
            E.tensor_tensor(tmp, mu, mu, ALU.mult)
            E.drain()
            return E.tensor_tensor(e2, e2, tmp, ALU.subtract)
        step("vector", stat2r, reads=[res("pb6"), res("pb7"), res("brows"), res("bbc")],
             writes=[res("brows"), res("bbc")])
        def stat2s(E):
            e2 = aB("brows", p0=0, p1=1, c0=RNG, c1=2 * RNG)
            E.activation(e2, e2, AF.Ln, bias=epsln[0:1, :])
            E.drain()
            return E.activation(e2, e2, AF.Exp, scale=-0.5)
        step("scalar", stat2s, reads=[res("brows"), res("consts")], writes=[res("brows")])
        def bc2(E):
            E.matmul(bank(6, c0=0, c1=RNG), ones_row, aB("brows", p0=0, p1=1, c0=0, c1=RNG),
                     start=True, stop=True)
            return E.matmul(bank(7, c0=0, c1=RNG), ones_row,
                            aB("brows", p0=0, p1=1, c0=RNG, c1=2 * RNG),
                            start=True, stop=True)
        step("tensor", bc2, reads=[res("brows"), res("consts"), res("pb6"), res("pb7")],
             writes=[res("pb6"), res("pb7")])
        def bc2ev(E):
            E.tensor_copy(aB("bbc", c0=0, c1=RNG), bank(6, c0=0, c1=RNG))
            return E.tensor_copy(aB("bbc", c0=RNG, c1=2 * RNG), bank(7, c0=0, c1=RNG))
        step("vector", bc2ev, reads=[res("pb6"), res("pb7"), res("bbc")], writes=[res("bbc")])
        def xn2mk(E):
            last = None
            for ch in range(NCH):
                E.tensor_tensor(aB("xn2", c0=RNG * ch, c1=RNG * (ch + 1)),
                                aB("x1slab", c0=RNG * ch, c1=RNG * (ch + 1)),
                                aB("bbc", c0=0, c1=RNG), ALU.subtract)
            E.drain()
            for ch in range(NCH):
                last = E.tensor_tensor(aB("xn2", c0=RNG * ch, c1=RNG * (ch + 1)),
                                       aB("xn2", c0=RNG * ch, c1=RNG * (ch + 1)),
                                       aB("bbc", c0=RNG, c1=2 * RNG), ALU.mult)
            return last
        step("vector", xn2mk, reads=[res("x1slab"), res("bbc"), res("xn2")], writes=[res("xn2")])
        def xn2split(E):
            E.tensor_copy(aBbf("xn2p", 0, NCH * RNG), aB("xn2"))
            E.drain()
            return E.tensor_tensor(aBbf("xn2p", NCH * RNG, 2 * NCH * RNG), aB("xn2"),
                                   aBbf("xn2p", 0, NCH * RNG), ALU.subtract)
        step("vector", xn2split, reads=[res("xn2"), res("xn2p")], writes=[res("xn2p")])
        for fc in range(NFCH):
            b = fc % 4
            def hmm(E, fc=fc, b=b):
                last = None
                first = True
                for ch in range(NCH):
                    xhi = aBbf("xn2p", RNG * ch, RNG * ch + RNG)
                    xlo = aBbf("xn2p", NCH * RNG + RNG * ch, NCH * RNG + RNG * (ch + 1))
                    whi = w1b(0, ch, 128 * fc, 128 * (fc + 1))
                    wlo = w1b(1, ch, 128 * fc, 128 * (fc + 1))
                    for lhs, rhs in ((whi, xhi), (whi, xlo), (wlo, xhi)):
                        last = E.matmul(bank(b, c0=0, c1=RNG), lhs, rhs,
                                        start=first,
                                        stop=(ch == NCH - 1 and rhs is xhi and lhs is wlo))
                        first = False
                return last
            step("tensor", hmm, reads=[res("xn2p"), res("w"), res(f"pb{b}")],
                 writes=[res(f"pb{b}")])
            hb = fc % 2
            def hgelu(E, fc=fc, b=b, hb=hb):
                return E.activation(aB("hstg", c0=512 * hb, c1=512 * hb + RNG),
                                    bank(b, c0=0, c1=RNG), AF.Gelu_apprx_tanh, bias=b1ap(fc))
            step("scalar", hgelu, reads=[res(f"pb{b}"), res("consts"), res(f"hst{hb}")],
                 writes=[res(f"hst{hb}")])
            def hhicast(E, hb=hb):
                return E.copy(aBbf("hhi", 512 * hb, 512 * hb + RNG),
                              aB("hstg", c0=512 * hb, c1=512 * hb + RNG))
            step("scalar", hhicast, reads=[res(f"hst{hb}"), res(f"hhi{hb}")],
                 writes=[res(f"hhi{hb}")])
            def hlocomp(E, hb=hb):
                return E.tensor_tensor(aBbf("hlo", 512 * hb, 512 * hb + RNG),
                                       aB("hstg", c0=512 * hb, c1=512 * hb + RNG),
                                       aBbf("hhi", 512 * hb, 512 * hb + RNG), ALU.subtract)
            step("vector", hlocomp, reads=[res(f"hst{hb}"), res(f"hhi{hb}"), res(f"hlo{hb}")],
                 writes=[res(f"hlo{hb}")])
            step("gpsimd", lambda E, fc=fc, s=s, hb=hb: [
                 E.dma_start(out=d_hhi[128 * fc:128 * (fc + 1), RNG * s:RNG * (s + 1)],
                             in_=aBbf("hhi", 512 * hb, 512 * hb + RNG)),
                 E.dma_start(out=d_hlo[128 * fc:128 * (fc + 1), RNG * s:RNG * (s + 1)],
                             in_=aBbf("hlo", 512 * hb, 512 * hb + RNG))],
                 reads=[res(f"hhi{hb}"), res(f"hlo{hb}")], writes=[res("hs")],
                 dma="hso", n_dma=2)

    if dbg_noc:
        bld.finalize()
        return nc
    barrier()
    for fc4 in range(NFCH // 4):
        stg = "cx1" if fc4 % 2 == 0 else "x2"
        step("sync", lambda E, fc4=fc4, stg=stg: [
            E.dma_start(out=aC(stg, c0=1024 * q, c1=1024 * (q + 1)),
                        in_=d_w2[128 * (4 * fc4 + q):128 * (4 * fc4 + q + 1), :])
            for q in range(4)],
             reads=[res("cx10" if stg == "cx1" else stg)],
             writes=[res("cx10" if stg == "cx1" else stg)], dma=f"wld{fc4 % 2}", n_dma=4)
        def w2split(E, fc4=fc4, stg=stg):
            for q in range(4):
                E.tensor_copy(w2b(0, 4 * fc4 + q, 0, C), aC(stg, c0=1024 * q, c1=1024 * (q + 1)))
            E.drain()
            last = None
            for q in range(4):
                last = E.tensor_tensor(w2b(1, 4 * fc4 + q, 0, C),
                                       aC(stg, c0=1024 * q, c1=1024 * (q + 1)),
                                       w2b(0, 4 * fc4 + q, 0, C), ALU.subtract)
            return last
        step("vector", w2split, reads=[res("cx10" if stg == "cx1" else stg), res("w")],
             writes=[res("w")])

    for s in range(NRANGE):
        sp_ = s % 2
        def ldx1c(E, s=s, sp_=sp_):
            return [E.dma_start(out=aC("cx1", c0=4096 * sp_ + RNG * ch,
                                       c1=4096 * sp_ + RNG * (ch + 1)),
                                in_=d_x1[128 * ch:128 * (ch + 1), RNG * s:RNG * (s + 1)])
                    for ch in range(NCH)]
        step("sync", ldx1c, reads=[res("x1s"), res(f"cx1{sp_}")], writes=[res(f"cx1{sp_}")],
             dma=f"x1i{sp_}", n_dma=NCH)
        for fch in range(NFCH):
            hbuf = fch % 4
            step("sync", lambda E, fch=fch, s=s, hbuf=hbuf: [
                 E.dma_start(out=aCbf("hbufs", 1024 * hbuf, 1024 * hbuf + RNG),
                             in_=d_hhi[128 * fch:128 * (fch + 1), RNG * s:RNG * (s + 1)]),
                 E.dma_start(out=aCbf("hbufs", 1024 * hbuf + RNG, 1024 * hbuf + 2 * RNG),
                             in_=d_hlo[128 * fch:128 * (fch + 1), RNG * s:RNG * (s + 1)])],
                 reads=[res("hs"), res(f"hb{hbuf}")], writes=[res(f"hb{hbuf}")],
                 dma=f"hsi{hbuf}", n_dma=2)
            def y2mm(E, fch=fch, hbuf=hbuf):
                hhi = aCbf("hbufs", 1024 * hbuf, 1024 * hbuf + RNG)
                hlo = aCbf("hbufs", 1024 * hbuf + RNG, 1024 * hbuf + 2 * RNG)
                last = None
                for dc in range(NCH):
                    whi = w2b(0, fch, 128 * dc, 128 * (dc + 1))
                    wlo = w2b(1, fch, 128 * dc, 128 * (dc + 1))
                    for lhs, rhs in ((whi, hhi), (whi, hlo), (wlo, hhi)):
                        last = E.matmul(bank(dc, c0=0, c1=RNG), lhs, rhs,
                                        start=(fch == 0 and lhs is whi and rhs is hhi),
                                        stop=(fch == NFCH - 1 and lhs is wlo))
                return last
            step("tensor", y2mm,
                 reads=[res(f"hb{hbuf}"), res("w")] + [res(f"pb{d}") for d in range(8)],
                 writes=[res(f"pb{d}") for d in range(8)])
        def y2ev(E, sp_=sp_):
            last = None
            for dc in range(NCH):
                last = E.scalar_tensor_tensor(aC("x2", c0=RNG * dc, c1=RNG * (dc + 1)),
                                              bank(dc, c0=0, c1=RNG), b2ap(dc),
                                              aC("cx1", c0=4096 * sp_ + RNG * dc,
                                                 c1=4096 * sp_ + RNG * (dc + 1)),
                                              ALU.add, ALU.add)
            return last
        step("vector", y2ev,
             reads=[res(f"pb{d}") for d in range(8)] + [res(f"cx1{sp_}"), res("consts"),
                                                        res("x2")],
             writes=[res("x2")])
        for j in range(4):
            b0 = 2 * (j % 2)
            def otr(E, j=j, b0=b0):
                last = None
                for dc in range(NCH):
                    bb = b0 + dc // 4
                    last = E.transpose(bank(bb, c0=128 * (dc % 4), c1=128 * (dc % 4) + 128),
                                       aC("x2", c0=RNG * dc + 128 * j,
                                          c1=RNG * dc + 128 * (j + 1)),
                                       ident)
                return last
            step("tensor", otr, reads=[res("x2"), res("consts"),
                                       res(f"pb{b0}"), res(f"pb{b0+1}")],
                 writes=[res(f"pb{b0}"), res(f"pb{b0+1}")])
            ob = j % 2
            def otev(E, b0=b0, ob=ob):
                E.tensor_copy(aC("ostg", c0=1024 * ob, c1=1024 * ob + 512),
                              bank(b0, c0=0, c1=512))
                return E.tensor_copy(aC("ostg", c0=1024 * ob + 512, c1=1024 * ob + 1024),
                                     bank(b0 + 1, c0=0, c1=512))
            step("vector", otev, reads=[res(f"pb{b0}"), res(f"pb{b0+1}"), res(f"os{ob}")],
                 writes=[res(f"os{ob}")])
            step("gpsimd", lambda E, j=j, s=s, ob=ob:
                 E.dma_start(out=d_y[RNG * s + 128 * j: RNG * s + 128 * (j + 1), :],
                             in_=aC("ostg", c0=1024 * ob, c1=1024 * ob + 1024)),
                 reads=[res(f"os{ob}")], writes=[res("y")], dma="yo")

    bld.finalize()
    return nc


def _host_stage(inputs):
    x = np.asarray(inputs["L0_attn_x"], dtype=np.float32)
    Wq = np.asarray(inputs["L0_attn_Wq"], dtype=np.float32)
    Wk = np.asarray(inputs["L0_attn_Wk"], dtype=np.float32)
    Wv = np.asarray(inputs["L0_attn_Wv"], dtype=np.float32)
    Wo = np.asarray(inputs["L0_attn_Wo"], dtype=np.float32)
    gq = np.asarray(inputs["L0_attn_q_gamma"], dtype=np.float32)
    gk = np.asarray(inputs["L0_attn_k_gamma"], dtype=np.float32)
    w1 = np.asarray(inputs["L0_ffn_w1"], dtype=np.float32)
    b1 = np.asarray(inputs["L0_ffn_b1"], dtype=np.float32)
    w2 = np.asarray(inputs["L0_ffn_w2"], dtype=np.float32)
    b2 = np.asarray(inputs["L0_ffn_b2"], dtype=np.float32)

    permq = np.concatenate([np.concatenate([np.arange(0, HD, 2), np.arange(1, HD, 2)]) + HD * h
                            for h in range(NH)])
    permk = np.concatenate([np.arange(0, HD, 2), np.arange(1, HD, 2)])
    wqT = np.ascontiguousarray(Wq[permq, :].T)
    wkT = np.ascontiguousarray(Wk[permk, :].T)
    wvT = np.ascontiguousarray(Wv.T)
    woT = np.ascontiguousarray(Wo.T)

    gams = np.ascontiguousarray(
        np.stack([gq[0::2], gq[1::2], gk[0::2], gk[1::2]], axis=1))

    import ml_dtypes
    ii = np.arange(128)
    consts = np.zeros((128, 386), np.float32)
    consts[:, 0:128] = np.eye(128, dtype=np.float32)
    consts[:, 128] = 1.0
    consts[:, 129] = EPS_LN
    def _b(a):
        return a.astype(ml_dtypes.bfloat16).view(np.uint16).view(np.float32)
    consts[:, 130:194] = _b(np.eye(128, dtype=np.float32))
    consts[:, 194:258] = _b((ii[:, None] < ii[None, :]).astype(np.float32))
    consts[:, 258:322] = _b((ii[:, None] > ii[None, :]).astype(np.float32))
    consts[:, 322:386] = _b(NEG * np.eye(128, dtype=np.float32))

    import jax
    import jax.numpy as jnp
    cpu = jax.devices("cpu")[0]
    with jax.default_device(cpu):
        inv_freq = 1.0 / (1000000.0 ** (jnp.arange(0, HD, 2, dtype=jnp.float32) / HD))
        t = jnp.arange(8192, dtype=jnp.float32)
        freqs = jnp.outer(t, inv_freq)
        cos_full = np.asarray(jnp.cos(freqs), dtype=np.float32)
        sin_full = np.asarray(jnp.sin(freqs), dtype=np.float32)

    in_maps = []
    for core in range(8):
        b = core // 2
        half = core % 2
        g0 = half * TCORE
        if half == 0:
            haloimg = np.zeros((HALO, C), np.float32)
        else:
            haloimg = x[b, g0 - HALO:g0, :]
        x_sh = np.ascontiguousarray(np.concatenate([haloimg, x[b, g0:g0 + TCORE, :]], axis=0))
        pos = np.clip(np.arange(g0 - HALO, g0 + TCORE), 0, 8191)
        cosT = np.ascontiguousarray(cos_full[pos, :].T)
        sinT = np.ascontiguousarray(sin_full[pos, :].T)
        crow = np.zeros((1, 384), np.float32)
        crow[0, 0:128] = 1.0
        crow[0, 128:256] = NEG if half == 0 else 0.0
        crow[0, 256:384] = 16.0
        in_maps.append(dict(
            x_sh=x_sh, wqT=wqT, wkT=wkT, wvT=wvT, woT=woT,
            w1=np.ascontiguousarray(w1[core]),
            b1c=np.ascontiguousarray(b1[core].reshape(NFCH, 128).T),
            w2=np.ascontiguousarray(w2[core]),
            b2c=np.ascontiguousarray(b2[core].reshape(NCH, 128).T),
            gams=gams, cosT=cosT, sinT=sinT, consts=consts, crow=crow,
        ))
    return in_maps


def kernel(**inputs):
    in_maps = _host_stage(inputs)
    nc = build_program()
    res = run_bass_kernel_spmd(nc, in_maps, core_ids=list(range(8)))
    out = np.empty((4, 8192, C), np.float32)
    for core in range(8):
        b = core // 2
        half = core % 2
        out[b, half * TCORE:(half + 1) * TCORE, :] = np.asarray(res.results[core]["y"])
    return out
